# version 07
# baseline (speedup 1.0000x reference)
import os
import numpy as np
import ml_dtypes
import concourse.bass as bass
import concourse.mybir as mybir
from concourse.bass_utils import run_bass_kernel_spmd
from contextlib import ExitStack

F32 = mybir.dt.float32
F32R = mybir.dt.float32r
BF16 = mybir.dt.bfloat16
I32 = mybir.dt.int32
I8 = mybir.dt.int8
A = mybir.AluOpType
AF = mybir.ActivationFunctionType
AX = mybir.AxisListType.X

NCORE = 8
NT = 16384
OWN = 2048
H = 2048
NKB = 16
DG = 16384
DHALF = 8192
NDKB = 64
DFF = 5461
DFFP = 5632
NFKB = 44
NE = 8
KCAP = 2048
M32 = 12582912.0
M64 = 6755399441055744.0
N_BISECT = 36
EPS_LN = 1e-5
EPS_G = 1e-10
INV_H = 1.0 / 2048.0
NWG = float(DG * H)
NWD = float(DHALF * H)

Q_R, Q_MR, Q_RS, Q_SQ, Q_SC1, Q_F2, Q_P2, Q_CO = range(8)


def _gumbel_tables():
    import jax
    import jax.numpy as jnp
    cpu = jax.devices("cpu")[0]
    fp64 = os.environ.get("BASS_SIM_FP64") == "1"
    with jax.default_device(cpu):
        key = jax.random.key(42, impl="threefry2x32")
        if fp64:
            from jax.experimental import enable_x64
            with enable_x64():
                u = jax.random.uniform(key, (NT, NE), dtype=jnp.float64)
                g = np.asarray(-jnp.log(-jnp.log(u + EPS_G) + EPS_G), dtype=np.float64)
            hi = g.astype(np.float32)
            lo = (g - hi.astype(np.float64)).astype(np.float32)
        else:
            u = jax.random.uniform(key, (NT, NE), dtype=jnp.float32)
            g = -jnp.log(-jnp.log(u + np.float32(EPS_G)) + np.float32(EPS_G))
            hi = np.asarray(g, dtype=np.float32)
            lo = np.zeros_like(hi)
    return hi, lo


class Plan:
    ENGS = ("sync", "vector", "scalar", "tensor", "gpsimd")
    SEMS = ("SW", "SG", "SA", "DV", "AC", "PE", "PL", "OUT")
    ESEM = {"vector": "DV", "scalar": "AC", "gpsimd": "PL"}

    def __init__(self, dep=None, rec=None):
        self.ops = {e: [] for e in self.ENGS}
        self.sem = {}
        self.cnt = {}
        self.hw = {e: {} for e in self.ENGS}
        self.bankf = {}
        self.dep = dep
        self.rec = rec
        self.opidx = {e: 0 for e in self.ENGS}

    def _dep_waits(self, eng, idx):
        if self.dep is not None:
            for s, v in self.dep.get((eng, idx), ()):
                self.wait(eng, s, v)

    def _mk_recorder(self, fn, eng, idx, sem, val):
        rec = self.rec
        def run(E, f=fn, key=(eng, idx), sv=(sem, val)):
            inst = f(E)
            if rec is not None:
                try:
                    mi = inst.ins
                    rd = tuple(getattr(x, "memref", None) for x in mi.ins)
                    wr = tuple(getattr(x, "memref", None) for x in mi.outs)
                    rec[key] = (tuple(n for n in rd if n), tuple(n for n in wr if n), sv)
                except Exception:
                    pass
            return inst
        return run

    def use_bank(self, b, eng="tensor"):
        s_v = self.bankf.get(b)
        if s_v:
            self.wait(eng, s_v[0], s_v[1])

    def free_bank(self, b, sem, val):
        self.bankf[b] = (sem, val)

    def emit(self, eng, fn):
        if os.environ.get("KBUILD_DEBUG"):
            import traceback
            site = traceback.extract_stack()[-3]
            def wrapped(E, f=fn, s=site):
                try:
                    return f(E)
                except Exception:
                    print(f"PLAN-SITE: {s.filename}:{s.lineno} {s.line}")
                    raise
            self.ops[eng].append(wrapped)
        else:
            self.ops[eng].append(fn)

    def bump(self, name, k=1):
        self.cnt[name] = self.cnt.get(name, 0) + k
        return self.cnt[name]

    def cur(self, name):
        return self.cnt.get(name, 0)

    def wait(self, eng, sem, val):
        if not val or val <= 0:
            return
        if self.hw[eng].get(sem, 0) >= val:
            return
        self.hw[eng][sem] = val
        self.emit(eng, lambda E, s=sem, v=val: E.wait_ge(self.sem[s], v))

    def dma(self, eng, out, in_, sem, slow=False, ordered=False):
        idx = self.opidx[eng]
        self.opidx[eng] += 1
        self._dep_waits(eng, idx)
        if ordered:
            self.wait(eng, sem, self.cur(sem))
        v = self.bump(sem, 16)
        fn = lambda E, o=out, i=in_, s=sem, sl=slow: E.dma_start(
            out=o, in_=i, allow_slow_non_contiguous=sl).then_inc(self.sem[s], 16)
        self.emit(eng, self._mk_recorder(fn, eng, idx, sem, v))
        return v

    def ins(self, eng, fn, sem=None):
        idx = self.opidx[eng]
        self.opidx[eng] += 1
        self._dep_waits(eng, idx)
        auto = self.ESEM.get(eng)
        if auto is not None:
            sem = auto
        if sem is None:
            self.emit(eng, self._mk_recorder(fn, eng, idx, None, None))
            return None
        v = self.bump(sem)
        fn2 = lambda E, f=fn, s=sem: f(E).then_inc(self.sem[s], 1)
        self.emit(eng, self._mk_recorder(fn2, eng, idx, sem, v))
        return v

    def sync_phase(self, sems=None, engs=None):
        sems = sems or list(self.cnt.keys())
        snap = [(s, self.cur(s)) for s in sems]
        for eng in (engs or self.ENGS):
            for s, v in snap:
                self.wait(eng, s, v)


def v3(ap):
    return ap.rearrange("p (c o) -> p c o", o=1)


def compute_edges(rec):
    dep = {}
    streams = {}
    for (eng, idx) in rec:
        streams.setdefault(eng, []).append(idx)
    for eng, idxs in streams.items():
        if eng == "tensor":
            continue
        track_w = {}
        track_r = {}
        for idx in sorted(idxs):
            rd, wr, (sem, val) = rec[(eng, idx)]
            need = {}
            for n in rd:
                for s, v in track_w.get(n, {}).items():
                    if need.get(s, 0) < v:
                        need[s] = v
            for n in wr:
                for d in (track_w.get(n, {}), track_r.get(n, {})):
                    for s, v in d.items():
                        if need.get(s, 0) < v:
                            need[s] = v
            if sem is not None:
                need.pop(None, None)
                need = {s: v for s, v in need.items() if not (s == sem and v >= val)}
            if need:
                dep[(eng, idx)] = tuple(need.items())
            if sem is not None:
                for n in rd:
                    track_r.setdefault(n, {})[sem] = val
                for n in wr:
                    track_w[n] = {sem: val}
                    track_r[n] = {}
    return dep


def build_program(dep=None, rec=None):
    nc = bass.Bass("TRN2", target_bir_lowering=False, detect_race_conditions=False)
    dt = nc.dram_tensor

    xt = dt("xt", [NKB * 32, 128, 512], F32, kind="ExternalInput").ap()
    gmh = dt("gmh", [NE, 128, 128], F32, kind="ExternalInput").ap()
    gml = dt("gml", [NE, 128, 128], F32, kind="ExternalInput").ap()
    wgt = dt("wgt", [NKB * 32, 128, 512], F32, kind="ExternalInput").ap()
    wdt = dt("wdt", [NDKB * 4, 128, 512], F32, kind="ExternalInput").ap()
    fwg = dt("fwg", [NKB * NFKB, 128, 128], F32, kind="ExternalInput").ap()
    fwu = dt("fwu", [NKB * NFKB, 128, 128], F32, kind="ExternalInput").ap()
    fwd = dt("fwd", [NFKB * 4, 128, 512], F32, kind="ExternalInput").ap()
    ggate2 = dt("ggate2", [128, NKB], F32, kind="ExternalInput").ap()
    gdown2 = dt("gdown2", [128, NDKB], F32, kind="ExternalInput").ap()
    rwa1 = dt("rwa1", [128, NKB * 16], F32, kind="ExternalInput").ap()
    rwa2 = dt("rwa2", [128, NKB * 16], F32, kind="ExternalInput").ap()
    bg2 = dt("bg2", [128, NFKB], F32, kind="ExternalInput").ap()
    bu2 = dt("bu2", [128, NFKB], F32, kind="ExternalInput").ap()
    bd2 = dt("bd2", [128, NKB], F32, kind="ExternalInput").ap()
    rbrow = dt("rbrow", [1, NE], F32, kind="ExternalInput").ap()
    identd = dt("identd", [128, 128], F32, kind="ExternalInput").ap()
    yt = dt("yt", [H, OWN], F32, kind="ExternalOutput").ap()

    dbg = "ExternalOutput" if os.environ.get("KDEBUG_HW") else "Internal"
    d_h2 = dt("d_h2", [NDKB * 4, 128, 512], F32, kind=dbg).ap()
    d_mid = dt("d_mid", [NFKB * 4, 128, 512], BF16, kind=dbg).ap()
    d_xnt = dt("d_xnt", [NKB * 4, 128, 512], BF16, kind=dbg).ap()
    d_h2q = dt("d_h2q", [NDKB * 4, 128, 512], BF16, kind=dbg).ap()
    d_smt = dt("d_smt", [NE, NT], F32, kind=dbg).ap()
    d_rows = dt("d_rows", [8, OWN], F32, kind=dbg).ap()
    d_small = dt("d_small", [4, 128], F32, kind=dbg).ap()
    d_trep = dt("d_trep", [NE, 16], F32).ap()
    d_dbg1 = dt("d_dbg1", [128, NKB * OWN], BF16, kind=dbg).ap()
    d_dbg2 = dt("d_dbg2", [128, NKB * 512], BF16, kind=dbg).ap()

    p = Plan(dep=dep, rec=rec)
    es = ExitStack()
    sb = lambda name, shape, dtyp: es.enter_context(nc.sbuf_tensor(name, shape, dtyp))

    big1 = sb("big1", [128, NKB * OWN], BF16)
    c_id = sb("c_id", [128, 128], F32)
    ones_col = sb("ones_col", [128, 1], F32)
    ones_colr = sb("ones_colr", [128, 16], F32R)
    ones_row = sb("ones_row", [1, 128], F32)
    mqcol = sb("mqcol", [128, 1], F32)
    hlfcol = sb("hlfcol", [128, 1], F32)
    epscol = sb("epscol", [128, 1], F32)
    ggcol = sb("ggcol", [128, NKB], F32)
    gdcol = sb("gdcol", [128, NDKB], F32)
    gd2col = sb("gd2col", [128, NDKB], F32)
    bgcol = sb("bgcol", [128, NFKB], F32)
    bucol = sb("bucol", [128, NFKB], F32)
    bdcol = sb("bdcol", [128, NKB], F32)
    rbr = sb("rbr", [1, NE], F32)
    rbcols = sb("rbcols", [128, NE], F32)
    negcc = sb("negcc", [128, 16], F32)
    rown = sb("rown", [1, 128], F32)
    rowstage = [sb(f"rowstage{i}", [1, 512], F32) for i in range(2)]
    tk_r = sb("tk_r", [128, 128], F32)
    tk_mr = sb("tk_mr", [128, 128], F32)
    tk_rs = sb("tk_rs", [128, 128], F32)
    tmpA = sb("tmpA", [128, 128], F32)
    tmpB = sb("tmpB", [128, 128], F32)
    tmpC = sb("tmpC", [128, 128], F32)
    tmpD = sb("tmpD", [128, 128], F32)
    wacc_g = sb("wacc_g", [128, 1], F32)
    wacc_d = sb("wacc_d", [128, 1], F32)
    wrow = sb("wrow", [1, 16], F32)
    swg = sb("swg", [128, 2], F32)
    swd = sb("swd", [128, 2], F32)
    bis_lo = sb("bis_lo", [128, 1], F32)
    bis_hi = sb("bis_hi", [128, 1], F32)
    bis_mid = sb("bis_mid", [128, 1], F32)
    bis_m8 = sb("bis_m8", [128, 1], F32)
    bis_cnt = sb("bis_cnt", [128, 1], F32)
    bis_c16 = sb("bis_c16", [8, 16], F32)
    bis_sel = sb("bis_sel", [128, 1], F32)
    bis_t1 = sb("bis_t1", [128, 1], F32)
    bis_t2 = sb("bis_t2", [128, 1], F32)
    th_row = sb("th_row", [1, 16], F32)
    th_cols = sb("th_cols", [128, NE], F32)
    mxcols = sb("mxcols", [128, 16], F32)
    stat_cols = sb("stat_cols", [128, 48], F32)
    colasm = sb("colasm", [128, 128], F32)
    es2 = ExitStack()
    sb2 = lambda name, shape, dtyp: es2.enter_context(nc.sbuf_tensor(name, shape, dtyp))
    sm8 = sb2("sm8", [128, NE * 128], F32)
    smb = sb2("smb", [128, 1024], F32)
    bis_junk = sb2("bis_junk", [128, 1024], F32)
    rxs = ExitStack()
    xring = [rxs.enter_context(nc.sbuf_tensor(f"xring{i}", [128, 512], F32)) for i in range(4)]

    pse = ExitStack()
    ps = [pse.enter_context(nc.psum_tensor(f"ps{i}", [128, 512], F32)) for i in range(8)]

    SY, VE, SC, TE, GP = "sync", "vector", "scalar", "tensor", "gpsimd"

    def row_put(cols_dv, src16, q_idx, col0=0, ncols=16):
        p.wait(TE, "DV", cols_dv)
        p.use_bank(2)
        vt = p.ins(TE, lambda E, s2=src16: E.transpose(ps[2][0:ncols, 0:128], s2, c_id[:, :]), "PE")
        p.wait(VE, "PE", vt)
        vev = p.ins(VE, lambda E, n=ncols: E.tensor_copy(out=colasm[0:n, :], in_=ps[2][0:n, 0:128]), "DV")
        p.free_bank(2, "DV", vev)
        p.wait(SY, "DV", vev)
        vd = p.dma(SY, d_rows[q_idx:q_idx + 1, col0 * 128:(col0 + ncols) * 128].rearrange(
            "o (c q) -> (o c) q", q=128), colasm[0:ncols, :], "RW", ordered=True)
        p.wait(VE, "RW", vd)
        return vd

    rowstage_free = [("PE", 0), ("PE", 0)]
    rowstage_idx = [0]

    def bc_row(q_idx, col0, width, dst, dst_off, row_ready_sw, bank):
        j = rowstage_idx[0] % 2
        rowstage_idx[0] += 1
        s, v = rowstage_free[j]
        p.wait(SY, s, v)
        p.wait(SY, "RW", row_ready_sw)
        vr = p.dma(SY, rowstage[j][:1, 0:width], d_rows[q_idx:q_idx + 1, col0:col0 + width], f"RS{j}")
        p.wait(TE, f"RS{j}", vr)
        p.use_bank(bank)
        vb = p.ins(TE, lambda E, j2=j, w=width, b=bank: E.matmul(
            ps[b][:, 0:w], ones_row[:1, :], rowstage[j2][:1, 0:w], start=True, stop=True), "PE")
        rowstage_free[j] = ("PE", vb)
        p.wait(VE, "PE", vb)
        vev = p.ins(VE, lambda E, d=dst, o=dst_off, w=width, b=bank: E.tensor_copy(
            out=d[:, o:o + w], in_=ps[b][:, 0:w]), "DV")
        p.free_bank(bank, "DV", vev)
        return vev

    p.dma(SY, c_id[:, :], identd, "PR")
    p.dma(SY, ggcol[:, :], ggate2, "PR")
    p.dma(SY, gdcol[:, :], gdown2, "PR")
    p.dma(SY, bgcol[:, :], bg2, "PR")
    p.dma(SY, bucol[:, :], bu2, "PR")
    p.dma(SY, bdcol[:, :], bd2, "PR")
    p.dma(SY, rbr[:1, :], rbrow, "PR")
    v_pro = p.cur("PR")

    p.ins(VE, lambda E: E.memset(ones_col[:, :], 1.0))
    p.ins(VE, lambda E: E.memset(stat_cols[:, 0:16], 1.0))
    p.ins(VE, lambda E: E.tensor_copy(out=ones_colr[:, :], in_=stat_cols[:, 0:16]))
    p.ins(VE, lambda E: E.memset(ones_row[:1, :], 1.0))
    p.ins(VE, lambda E: E.memset(epscol[:, :], EPS_LN))
    p.ins(VE, lambda E: E.memset(hlfcol[:, :], 0.5))
    p.ins(VE, lambda E: E.tensor_scalar(hlfcol[:, :], hlfcol[:, :], M32, M32, A.add, A.subtract))
    p.ins(VE, lambda E: E.tensor_scalar(hlfcol[:, :], hlfcol[:, :], 2.0, None, A.mult))
    p.ins(VE, lambda E: E.tensor_scalar(mqcol[:, :], hlfcol[:, :], M64 - M32, M32, A.mult, A.add))
    p.ins(VE, lambda E: E.memset(wacc_g[:, :], 0.0))
    p.ins(VE, lambda E: E.memset(colasm[:, :], 0.0))
    p.ins(VE, lambda E: E.memset(wrow[:1, :], 0.0))
    p.ins(VE, lambda E: E.memset(wacc_d[:, :], 0.0))
    p.wait(VE, "PR", v_pro)
    v_gd2pre = p.ins(VE, lambda E: E.engine_nop(), "DV")
    p.wait(SC, "DV", v_gd2pre)
    p.ins(SC, lambda E: E.activation(gd2col[:, :], gdcol[:, :], AF.Square), "AC")

    rsc = ExitStack()
    rb_ = lambda name, shape, dtyp: rsc.enter_context(nc.sbuf_tensor(name, shape, dtyp))
    statsT = rb_("statsT", [128, 2048], F32)
    gmh_sb = rb_("gmh_sb", [128, NE * 128], F32)
    gml_sb = rb_("gml_sb", [128, NE * 128], F32)
    xfr = [rb_(f"xfr{i}", [128, 512], F32R) for i in range(2)]
    xsq = [rb_(f"xsq{i}", [128, 512], F32R) for i in range(2)]
    stats_sl = [rb_(f"statsl{i}", [128, 512], F32) for i in range(2)]
    rwa1s = rb_("rwa1s", [128, NKB * 16], F32R)
    rwa2s = rb_("rwa2s", [128, NKB * 16], F32R)
    wgpre = [rb_(f"wgpre{i}", [128, 512], F32) for i in range(4)]
    wdpre = [rb_(f"wdpre{i}", [128, 2048], F32) for i in range(2)]
    wtmp = rb_("wtmp", [128, 512], F32)
    wtmp2 = rb_("wtmp2", [128, 128], F32)
    wjunk = rb_("wjunk", [128, 512], F32)
    wjunk2 = rb_("wjunk2", [128, 1024], F32)

    for e in range(NE):
        p.dma(SY, gmh_sb[:, e * 128:(e + 1) * 128], gmh[e, :, :], "PR")
        p.dma(SY, gml_sb[:, e * 128:(e + 1) * 128], gml[e, :, :], "PR")
    p.dma(SY, wgpre[0][:, 0:256], rwa1, "PR")
    v_rwl = p.dma(SY, wgpre[1][:, 0:256], rwa2, "PR")
    p.wait(VE, "PR", v_rwl)
    p.ins(VE, lambda E: E.tensor_copy(out=rwa1s[:, :], in_=wgpre[0][:, 0:256]))
    v_rwa = p.ins(VE, lambda E: E.tensor_copy(out=rwa2s[:, :], in_=wgpre[1][:, 0:256]), "DV")

    NSL = NT // 512
    xr_free = [("DV", 0)] * 4
    xfr_free = [("PE", 0)] * 2
    xsq_free = [("PE", 0)] * 2
    stats_free = [("DV", 0)] * 2
    p.ins(VE, lambda E: E.memset(stats_sl[0][:, :], 0.0))
    p.ins(VE, lambda E: E.memset(stats_sl[1][:, :], 0.0))
    p.wait(TE, "DV", v_rwa)

    wg_free = [("AC", 0)] * 4
    wg_free[0] = ("DV", v_rwa)
    wg_free[1] = ("DV", v_rwa)
    wd_free = [("AC", 0)] * 2
    WGQ = (GP, GP, SY, GP, GP, GP, SC, GP, GP, GP, SY, GP, GP, GP, SC, GP)

    def emit_wg_unit(t):
        kb2 = t % NKB
        cs = t // NKB
        slot = t % 4
        eng = WGQ[t % 16]
        s, v = wg_free[slot]
        p.wait(eng, s, v)
        vld = p.dma(eng, wgpre[slot][:, :], wgt[kb2 * 32 + cs, :, :], f"GWG{slot}")
        p.wait(SC, f"GWG{slot}", vld)
        vr = p.ins(SC, lambda E, sl2=slot, t2=t: E.activation(
            wjunk[:, :], wgpre[sl2][:, :], AF.Abs, accum_out=wtmp[:, t2:t2 + 1]), "AC")
        wg_free[slot] = ("AC", vr)

    def emit_wd_unit(td):
        dslot = td % 2
        s, v = wd_free[dslot]
        p.wait(SC, s, v)
        vldd = 0
        for j in range(4):
            vldd = p.dma(SC, wdpre[dslot][:, j * 512:(j + 1) * 512], wdt[td * 4 + j, :, :], f"GWD{dslot}")
        p.wait(SC, f"GWD{dslot}", vldd)
        p.ins(SC, lambda E, sl2=dslot, t2=td: E.activation(
            wjunk2[:, 0:1024], wdpre[sl2][:, 0:1024], AF.Abs, accum_out=wtmp2[:, t2:t2 + 1]), "AC")
        vrr = p.ins(SC, lambda E, sl2=dslot, t2=td: E.activation(
            wjunk2[:, 0:1024], wdpre[sl2][:, 1024:2048], AF.Abs, accum_out=wtmp2[:, 64 + t2:64 + t2 + 1]), "AC")
        wd_free[dslot] = ("AC", vrr)

    for sl in range(NSL):
        bank = sl % 2
        vmm = None
        for kb in range(NKB):
            u = sl * NKB + kb
            slot = u % 4
            s, v = xr_free[slot]
            p.wait(SY, s, v)
            vld = p.dma(SY, xring[slot][:, :], xt[kb * 32 + sl, :, :], f"XR{slot}")
            j = u % 2
            s, v = xfr_free[j]
            p.wait(VE, s, v)
            p.wait(VE, f"XR{slot}", vld)
            vcv = p.ins(VE, lambda E, a=xfr[j], b=xring[slot]: E.tensor_copy(out=a[:, :], in_=b[:, :]), "DV")
            xr_free[slot] = ("DV", vcv)
            s, v = xsq_free[j]
            p.wait(VE, s, v)
            vsq = p.ins(VE, lambda E, a=xsq[j], b=xfr[j]: E.tensor_mul(a[:, :], b[:, :], b[:, :]), "DV")
            if kb == 0:
                s, v = stats_free[bank]
                p.wait(TE, s, v)
            p.wait(TE, "DV", vcv)
            p.ins(TE, lambda E, b=bank, k=kb, j2=j: E.matmul(
                ps[b][0:16, :], rwa1s[:, k * 16:(k + 1) * 16], xfr[j2][:, :], start=(k == 0), stop=False))
            p.wait(TE, "DV", vsq)
            vmm = p.ins(TE, lambda E, b=bank, k=kb, j2=j: E.matmul(
                ps[b][0:16, :], rwa2s[:, k * 16:(k + 1) * 16], xsq[j2][:, :], start=False, stop=(k == NKB - 1)),
                "PE")
            xfr_free[j] = ("PE", vmm)
            xsq_free[j] = ("PE", vmm)
            emit_wg_unit(u)
            if u % 8 == 0:
                emit_wd_unit(u // 8)
        p.wait(VE, "PE", vmm)
        vev = p.ins(VE, lambda E, d=stats_sl[sl % 2], b=bank: E.tensor_copy(out=d[0:16, :], in_=ps[b][0:16, :]), "DV")
        stats_free[bank] = ("DV", vev)
        for c in range(4):
            p.wait(TE, "DV", vev)
            p.use_bank(2)
            vtr = p.ins(TE, lambda E, d=stats_sl[sl % 2], c2=c: E.transpose(
                ps[2][:, 0:128], d[:, c2 * 128:(c2 + 1) * 128], c_id[:, :]), "PE")
            p.wait(VE, "PE", vtr)
            vst2 = p.ins(VE, lambda E, sl2=sl, c2=c: E.tensor_copy(
                out=statsT[:, (sl2 * 4 + c2) * 16:(sl2 * 4 + c2) * 16 + 16], in_=ps[2][:, 0:16]), "DV")
            p.free_bank(2, "DV", vst2)
    v_statsT = p.ins(VE, lambda E: E.engine_nop(), "DV")

    v_wabs = p.cur("AC")
    p.wait(VE, "AC", v_wabs)
    v_waccg = p.ins(VE, lambda E: E.tensor_reduce(wacc_g[:, :], wtmp[:, :], AX, A.add), "DV")
    v_waccd = p.ins(VE, lambda E: E.tensor_reduce(wacc_d[:, :], wtmp2[:, :], AX, A.add), "DV")

    p.wait(TE, "DV", v_waccg)
    p.use_bank(3)
    vmw = p.ins(TE, lambda E: E.transpose(ps[3][0:1, 0:128], wacc_g[:, 0:1], c_id[:, :]), "PE")
    p.wait(TE, "DV", v_waccd)
    p.use_bank(2)
    vmw2 = p.ins(TE, lambda E: E.transpose(ps[2][0:1, 0:128], wacc_d[:, 0:1], c_id[:, :]), "PE")
    p.wait(VE, "PE", vmw2)
    p.ins(VE, lambda E: E.reduce_sum(wrow[:1, 0:1], ps[3][0:1, 0:128], AX))
    vwd0 = p.ins(VE, lambda E: E.reduce_sum(wrow[:1, 1:2], ps[2][0:1, 0:128], AX), "DV")
    p.free_bank(2, "DV", vwd0)
    p.ins(VE, lambda E: E.tensor_scalar(wrow[:1, 0:1], wrow[:1, 0:1], 1.0 / NWG, 1e-5, A.mult, A.max))
    p.ins(VE, lambda E: E.tensor_scalar(wrow[:1, 1:2], wrow[:1, 1:2], 1.0 / NWD, 1e-5, A.mult, A.max))
    vwr = p.ins(VE, lambda E: E.reciprocal(wrow[:1, 2:4], wrow[:1, 0:2]), "DV")
    p.wait(TE, "DV", vwr)
    p.use_bank(3)
    vbw = p.ins(TE, lambda E: E.matmul(ps[3][:, 0:16], ones_row[:1, :], wrow[:1, 0:16], start=True, stop=True), "PE")
    p.wait(VE, "PE", vbw)
    p.ins(VE, lambda E: E.tensor_copy(out=swg[:, 1:2], in_=ps[3][:, 0:1]))
    p.ins(VE, lambda E: E.tensor_copy(out=swd[:, 1:2], in_=ps[3][:, 1:2]))
    p.ins(VE, lambda E: E.tensor_copy(out=swg[:, 0:1], in_=ps[3][:, 2:3]))
    v_sw = p.ins(VE, lambda E: E.tensor_copy(out=swd[:, 0:1], in_=ps[3][:, 3:4]), "DV")
    p.free_bank(3, "DV", v_sw)

    def qv(q):
        return statsT[:, :].rearrange("p (c q) -> p c q", q=16)[:, :, q:q + 1]

    p.wait(VE, "DV", v_statsT)
    p.use_bank(3)
    for kb in range(NKB):
        vcs = p.ins(TE, lambda E, k=kb: E.matmul(
            ps[3][0:16, 32:48], rwa1s[:, k * 16:(k + 1) * 16], ones_colr[:, :],
            start=(k == 0), stop=(k == NKB - 1)), "PE" if kb == NKB - 1 else None)
    p.wait(VE, "PE", vcs)
    vcse = p.ins(VE, lambda E: E.tensor_copy(out=colasm[0:16, 0:1], in_=ps[3][0:16, 32:33]), "DV")
    p.free_bank(3, "DV", vcse)
    p.wait(TE, "DV", vcse)
    p.use_bank(2)
    vcst = p.ins(TE, lambda E: E.transpose(ps[2][0:1, 0:128], colasm[:, 0:1], c_id[:, :]), "PE")
    p.wait(VE, "PE", vcst)
    vrn = p.ins(VE, lambda E: E.tensor_copy(out=rown[:1, :], in_=ps[2][0:1, 0:128]), "DV")
    p.free_bank(2, "DV", vrn)
    vneg = p.ins(VE, lambda E: E.tensor_scalar(rown[:1, 0:16], rown[:1, 0:16], -INV_H, None, A.mult), "DV")
    p.wait(TE, "DV", vneg)
    p.use_bank(3)
    p.ins(TE, lambda E: E.matmul(ps[3][:, 8:24], ones_row[:1, :], rown[:1, 0:16], start=True, stop=True))
    vb2 = p.ins(TE, lambda E: E.matmul(ps[3][:, 24:32], ones_row[:1, :], rbr[:1, :], start=True, stop=True), "PE")
    p.wait(VE, "PE", vb2)
    p.ins(VE, lambda E: E.tensor_copy(out=negcc[:, :], in_=ps[3][:, 8:24]))
    vrbc = p.ins(VE, lambda E: E.tensor_copy(out=rbcols[:, :], in_=ps[3][:, 24:32]), "DV")
    p.free_bank(3, "DV", vrbc)
    p.ins(VE, lambda E: E.tensor_scalar(v3(tmpA[:, :]), qv(8), INV_H, None, A.mult))
    p.ins(VE, lambda E: E.tensor_scalar(v3(tmpB[:, :]), qv(9), INV_H, None, A.mult))
    p.ins(VE, lambda E: E.tensor_mul(tmpC[:, :], tmpA[:, :], tmpA[:, :]))
    p.ins(VE, lambda E: E.tensor_tensor(tmpB[:, :], tmpB[:, :], tmpC[:, :], A.subtract))
    vv = p.ins(VE, lambda E: E.engine_nop(), "DV")
    p.wait(SC, "DV", vv)
    va = p.ins(SC, lambda E: E.activation(tmpC[:, :], tmpB[:, :], AF.Sqrt, bias=epscol[:, 0:1]), "AC")
    p.wait(VE, "AC", va)
    p.ins(VE, lambda E: E.reciprocal(tk_r[:, :], tmpC[:, :]))
    p.ins(VE, lambda E: E.tensor_mul(tk_mr[:, :], tmpA[:, :], tk_r[:, :]))
    p.ins(VE, lambda E: E.tensor_mul(tmpD[:, :], tk_r[:, :], tk_r[:, :]))
    p.ins(VE, lambda E: E.tensor_mul(tmpD[:, :], tmpB[:, :], tmpD[:, :]))
    vq = p.ins(VE, lambda E: E.engine_nop(), "DV")
    p.wait(SC, "DV", vq)
    va2 = p.ins(SC, lambda E: E.activation(tmpD[:, :], tmpD[:, :], AF.Sqrt, bias=epscol[:, 0:1]), "AC")
    p.wait(VE, "AC", va2)
    p.ins(VE, lambda E: E.reciprocal(tk_rs[:, :], tmpD[:, :]))
    for e in range(NE):
        sl8 = lambda e2: sm8[:, e2 * 128:(e2 + 1) * 128]
        p.ins(VE, lambda E, e2=e: E.scalar_tensor_tensor(
            v3(sm8[:, e2 * 128:(e2 + 1) * 128]), qv(8), negcc[:, e2:e2 + 1], qv(e2), A.mult, A.add))
        p.ins(VE, lambda E, e2=e: E.tensor_mul(sl8(e2), sl8(e2), tk_r[:, :]))
        p.ins(VE, lambda E, e2=e: E.tensor_scalar(sl8(e2), sl8(e2), rbcols[:, e2:e2 + 1], None, A.add))
        p.ins(VE, lambda E, e2=e: E.tensor_add(sl8(e2), sl8(e2), gmh_sb[:, e2 * 128:(e2 + 1) * 128]))
        p.ins(VE, lambda E, e2=e: E.tensor_add(sl8(e2), sl8(e2), gml_sb[:, e2 * 128:(e2 + 1) * 128]))
    p.ins(VE, lambda E: E.tensor_copy(out=tmpA[:, :], in_=sm8[:, 0:128]))
    for e in range(1, NE):
        p.ins(VE, lambda E, e2=e: E.tensor_max(tmpA[:, :], tmpA[:, :], sm8[:, e2 * 128:(e2 + 1) * 128]))
    for e in range(NE):
        p.ins(VE, lambda E, e2=e: E.tensor_tensor(
            sm8[:, e2 * 128:(e2 + 1) * 128], sm8[:, e2 * 128:(e2 + 1) * 128], tmpA[:, :], A.subtract))
    vsm = p.ins(VE, lambda E: E.engine_nop(), "DV")
    p.wait(SC, "DV", vsm)
    vex = None
    for e in range(NE):
        vex = p.ins(SC, lambda E, e2=e: E.activation(
            sm8[:, e2 * 128:(e2 + 1) * 128], sm8[:, e2 * 128:(e2 + 1) * 128], AF.Exp), "AC")
    p.wait(VE, "AC", vex)
    p.ins(VE, lambda E: E.tensor_copy(out=tmpB[:, :], in_=sm8[:, 0:128]))
    for e in range(1, NE):
        p.ins(VE, lambda E, e2=e: E.tensor_add(tmpB[:, :], tmpB[:, :], sm8[:, e2 * 128:(e2 + 1) * 128]))
    p.ins(VE, lambda E: E.reciprocal(tmpC[:, :], tmpB[:, :]))
    for e in range(NE):
        p.ins(VE, lambda E, e2=e: E.tensor_mul(
            sm8[:, e2 * 128:(e2 + 1) * 128], sm8[:, e2 * 128:(e2 + 1) * 128], tmpC[:, :]))
    v_sm8 = p.ins(VE, lambda E: E.engine_nop(), "DV")

    vsmt_dma = 0
    for e in range(NE):
        p.wait(TE, "DV", v_sm8)
        p.use_bank(2)
        vt = p.ins(TE, lambda E, e2=e: E.transpose(ps[2][:, 0:128], sm8[:, e2 * 128:(e2 + 1) * 128], c_id[:, :]), "PE")
        p.wait(VE, "PE", vt)
        p.wait(VE, "SM", vsmt_dma)
        vte = p.ins(VE, lambda E: E.tensor_copy(out=tmpD[:, :], in_=ps[2][:, 0:128]), "DV")
        p.free_bank(2, "DV", vte)
        p.wait(SY, "DV", vte)
        vsmt_dma = p.dma(SY, d_smt[e:e + 1, :].rearrange("o (c q) -> (o c) q", q=128), tmpD[:, :], "SM", ordered=True)
    p.wait(GP, "SM", vsmt_dma)
    v_smb = p.dma(GP, smb[:, :], d_smt.rearrange("e t -> (e t)").rearrange("(q f) -> q f", q=128), "SG")

    vmk = p.ins(VE, lambda E: E.tensor_copy(out=mxcols[:, 0:16], in_=tk_r[:, 0:16]), "DV")
    v_row_r = row_put(vmk, mxcols[:, 0:16], Q_R)
    vmk = p.ins(VE, lambda E: E.tensor_copy(out=mxcols[:, 0:16], in_=tk_mr[:, 0:16]), "DV")
    v_row_mr = row_put(vmk, mxcols[:, 0:16], Q_MR)
    vmk = p.ins(VE, lambda E: E.tensor_copy(out=mxcols[:, 0:16], in_=tk_rs[:, 0:16]), "DV")
    v_row_rs = row_put(vmk, mxcols[:, 0:16], Q_RS)

    p.ins(GP, lambda E: E.memset(bis_lo[:, :], 0.0))
    p.ins(GP, lambda E: E.memset(bis_hi[:, :], 1.5))
    p.ins(GP, lambda E: E.memset(bis_mid[:, :], 0.75))
    p.ins(GP, lambda E: E.memset(bis_m8[:, :], 0.75))
    p.wait(GP, "SG", v_smb)
    for it in range(N_BISECT):
        p.ins(GP, lambda E: E.tensor_single_scalar(bis_junk[:, :], smb[:, :], bis_mid[:, 0:1], A.is_ge))
        for w in (512, 256, 128, 64, 32, 16, 8, 4, 2, 1):
            p.ins(GP, lambda E, w2=w: E.tensor_add(bis_junk[:, 0:w2], bis_junk[:, 0:w2], bis_junk[:, w2:2 * w2]))
        p.ins(GP, lambda E: E.tensor_copy(out=bis_cnt[:, 0:1], in_=bis_junk[:, 0:1]))
        v1 = p.dma(GP, d_small[0:1, 0:128], bis_cnt[:, 0:1], "SG")
        p.wait(GP, "SG", v1)
        v2 = p.dma(GP, bis_c16[0:8, :], d_small[0:1, 0:128].rearrange("o (e j) -> (o e) j", j=16), "SG")
        p.wait(GP, "SG", v2)
        for w in (8, 4, 2, 1):
            p.ins(GP, lambda E, w2=w: E.tensor_add(bis_c16[0:8, 0:w2], bis_c16[0:8, 0:w2], bis_c16[0:8, w2:2 * w2]))
        p.ins(GP, lambda E: E.tensor_copy(out=bis_t1[0:8, 0:1], in_=bis_c16[0:8, 0:1]))
        p.ins(GP, lambda E: E.tensor_scalar(bis_sel[0:8, 0:1], bis_t1[0:8, 0:1], float(KCAP), None, A.is_ge))
        p.ins(GP, lambda E: E.tensor_tensor(bis_t1[0:8, 0:1], bis_m8[0:8, 0:1], bis_lo[0:8, 0:1], A.subtract))
        p.ins(GP, lambda E: E.tensor_mul(bis_t1[0:8, 0:1], bis_t1[0:8, 0:1], bis_sel[0:8, 0:1]))
        p.ins(GP, lambda E: E.tensor_add(bis_lo[0:8, 0:1], bis_lo[0:8, 0:1], bis_t1[0:8, 0:1]))
        p.ins(GP, lambda E: E.tensor_scalar(bis_sel[0:8, 0:1], bis_sel[0:8, 0:1], -1.0, 1.0, A.mult, A.add))
        p.ins(GP, lambda E: E.tensor_tensor(bis_t2[0:8, 0:1], bis_m8[0:8, 0:1], bis_hi[0:8, 0:1], A.subtract))
        p.ins(GP, lambda E: E.tensor_mul(bis_t2[0:8, 0:1], bis_t2[0:8, 0:1], bis_sel[0:8, 0:1]))
        p.ins(GP, lambda E: E.tensor_add(bis_hi[0:8, 0:1], bis_hi[0:8, 0:1], bis_t2[0:8, 0:1]))
        p.ins(GP, lambda E: E.tensor_tensor(bis_t1[0:8, 0:1], bis_lo[0:8, 0:1], bis_hi[0:8, 0:1], A.add))
        p.ins(GP, lambda E: E.tensor_scalar(bis_t1[0:8, 0:1], bis_t1[0:8, 0:1], 0.5, None, A.mult))
        p.ins(GP, lambda E: E.tensor_copy(out=bis_m8[0:8, 0:1], in_=bis_t1[0:8, 0:1]))
        v3_ = p.dma(GP, d_small[1:2, 0:8], bis_t1[0:8, 0:1], "SG")
        p.wait(GP, "SG", v3_)
        vrp = None
        for jj in range(16):
            vrp = p.dma(GP, d_trep[:, jj:jj + 1],
                        d_small[1:2, 0:8].rearrange("o e -> (o e)").rearrange("(e o) -> e o", o=1), "SG", slow=True)
        p.wait(GP, "SG", vrp)
        vm = p.dma(GP, bis_mid[:, 0:1], d_trep.rearrange("e j -> (e j)").rearrange("(q o) -> q o", o=1), "SG")
        p.wait(GP, "SG", vm)
    v_th = p.dma(GP, d_small[3:4, 0:8], bis_lo[0:8, 0:1], "SG")
    v_bis = p.ins(GP, lambda E: E.engine_nop(), "PL")

    p.sync_phase(sems=("SW", "DV", "AC", "PE"))
    rsc.close()

    xsc = ExitStack()
    xb_ = lambda name, shape, dtyp: xsc.enter_context(nc.sbuf_tensor(name, shape, dtyp))
    xr_sl = xb_("xr_sl", [128, NKB * 512], F32)
    xn_f = xb_("xn_f", [128, 512], F32)
    xnb = [xb_(f"xnb{i}", [128, 512], BF16) for i in range(2)]
    sqh = xb_("sqh", [128, 512], F32)
    silx = [xb_(f"silx{i}", [128, 512], F32) for i in range(2)]
    bcx = [xb_(f"bcx{i}", [128, 512], F32) for i in range(3)]
    i32t = [xb_(f"i32t{i}", [128, 512], F32) for i in range(2)]
    prex = [xb_(f"prex{i}", [128, 512], F32) for i in range(2)]

    xnb_free = [("XB0", 0), ("XB1", 0)]
    i32_free = [("DV", 0)] * 2
    prex_free = [("AC", 0)] * 2
    silx_free = [("DV", 0)] * 2
    v_row_sq = 0
    for sl in range(4):
        vr1 = bc_row(Q_R, sl * 512, 512, bcx[0], 0, v_row_r, 0)
        vr2 = bc_row(Q_MR, sl * 512, 512, bcx[1], 0, v_row_mr, 1)
        vr3 = bc_row(Q_RS, sl * 512, 512, bcx[2], 0, v_row_rs, 3)
        p.wait(VE, "DV", max(vr1, vr2, vr3))
        p.ins(VE, lambda E: E.memset(sqh[:, :], 0.0))
        for kb in range(NKB):
            u = sl * NKB + kb
            slot = u % 4
            s, v = xr_free[slot]
            p.wait(SY, s, v)
            vld = p.dma(SY, xring[slot][:, :], xt[kb * 32 + sl, :, :], f"XR{slot}")
            p.wait(VE, f"XR{slot}", vld)
            p.ins(VE, lambda E, sl2=slot: E.tensor_mul(xn_f[:, :], xring[sl2][:, :], bcx[0][:, :]))
            vxn = p.ins(VE, lambda E: E.tensor_tensor(xn_f[:, :], xn_f[:, :], bcx[1][:, :], A.subtract), "DV")
            xr_free[slot] = ("DV", vxn)
            jb = u % 2
            s, v = xnb_free[jb]
            p.wait(VE, s, v)
            p.ins(VE, lambda E, j2=jb: E.tensor_copy(out=xnb[j2][:, :], in_=xn_f[:, :]))
            vxr = p.ins(VE, lambda E, k=kb: E.scalar_tensor_tensor(
                xr_sl[:, k * 512:(k + 1) * 512], xn_f[:, :], ggcol[:, k:k + 1], bcx[2][:, :], A.mult, A.mult), "DV")
            p.wait(SC, "DV", vxr)
            vsp = p.dma(SC, d_xnt[kb * 4 + sl, :, :], xnb[jb][:, :], f"XB{jb}")
            xnb_free[jb] = (f"XB{jb}", vsp)
            jx = u % 2
            s, v = silx_free[jx]
            p.wait(SC, s, v)
            vsq2 = p.ins(SC, lambda E, k=kb, j2=jx: E.activation(
                silx[j2][:, :], xr_sl[:, k * 512:(k + 1) * 512], AF.Square), "AC")
            p.wait(VE, "AC", vsq2)
            vfo = p.ins(VE, lambda E, j2=jx: E.tensor_max(sqh[:, :], sqh[:, :], silx[j2][:, :]), "DV")
            silx_free[jx] = ("DV", vfo)
        vfold = p.cur("DV")
        for c in range(4):
            p.wait(TE, "DV", vfold)
            p.use_bank(2)
            vt = p.ins(TE, lambda E, c2=c: E.transpose(ps[2][:, 0:128], sqh[:, c2 * 128:(c2 + 1) * 128], c_id[:, :]), "PE")
            p.wait(VE, "PE", vt)
            vrm = p.ins(VE, lambda E, s2=sl, c2=c: E.reduce_max(mxcols[:, s2 * 4 + c2:s2 * 4 + c2 + 1], ps[2][:, 0:128], AX), "DV")
            p.free_bank(2, "DV", vrm)
        p.wait(SC, "DV", p.cur("DV"))
        vsc = p.ins(SC, lambda E, s2=sl: E.activation(
            mxcols[:, s2 * 4:s2 * 4 + 4], mxcols[:, s2 * 4:s2 * 4 + 4], AF.Sqrt), "AC")
        p.wait(VE, "AC", vsc)
        p.ins(VE, lambda E, s2=sl: E.tensor_scalar(
            stat_cols[:, s2 * 4:s2 * 4 + 4], mxcols[:, s2 * 4:s2 * 4 + 4], 1e-5, None, A.max))
        p.ins(VE, lambda E, s2=sl: E.reciprocal(mxcols[:, s2 * 4:s2 * 4 + 4], stat_cols[:, s2 * 4:s2 * 4 + 4]))
        vsq_c = p.ins(VE, lambda E, s2=sl: E.tensor_scalar(
            mxcols[:, s2 * 4:s2 * 4 + 4], mxcols[:, s2 * 4:s2 * 4 + 4], 127.0, None, A.mult), "DV")
        v_row_sq = row_put(vsq_c, mxcols[:, sl * 4:sl * 4 + 4], Q_SQ, col0=sl * 4, ncols=4)
        vbq = bc_row(Q_SQ, sl * 512, 512, bcx[0], 0, v_row_sq, 0)
        p.wait(VE, "DV", vbq)
        for kb in range(NKB):
            ji = kb % 2
            s, v = prex_free[ji]
            p.wait(VE, s, v)
            vpre = p.ins(VE, lambda E, k=kb, j2=ji: E.tensor_mul(
                prex[j2][:, :], xr_sl[:, k * 512:(k + 1) * 512], bcx[0][:, :]), "DV")
            s, v = i32_free[ji]
            p.wait(SC, s, v)
            p.wait(SC, "DV", vpre)
            vcv = p.ins(SC, lambda E, j2=ji: E.activation(i32t[j2][:, :], prex[j2][:, :], AF.Identity, bias=mqcol[:, 0:1]), "AC")
            prex_free[ji] = ("AC", vcv)
            p.wait(VE, "AC", vcv)
            vun = p.ins(VE, lambda E, s2=sl, k=kb, j2=ji: E.tensor_scalar(
                big1[:, k * OWN + s2 * 512:k * OWN + s2 * 512 + 512], i32t[j2][:, :], mqcol[:, 0:1], None, A.subtract), "DV")
            i32_free[ji] = ("DV", vun)
    p.wait(VE, "DV", v_sw)
    vsc1c = p.ins(VE, lambda E: E.tensor_scalar(
        mxcols[:, 0:16], stat_cols[:, 0:16], swg[:, 1:2], 1.0 / 127.0, A.mult, A.mult), "DV")
    v_row_sc1 = row_put(vsc1c, mxcols[:, 0:16], Q_SC1)
    v_xq = p.cur("DV")
    if os.environ.get("KDEBUG_A1"):
        p.wait(SY, "DV", v_xq)
        p.dma(SY, d_dbg1, big1[:, :], "SW")

    p.sync_phase(sems=("SW", "SA", "DV", "AC", "PE"))
    xsc.close()
    rxs.close()

    a1 = ExitStack()
    ab_ = lambda name, shape, dtyp: a1.enter_context(nc.sbuf_tensor(name, shape, dtyp))
    wstage = [ab_(f"wstage{i}", [128, 512], F32) for i in range(3)]
    i8a = [ab_(f"i8a{i}", [128, 512], F32) for i in range(2)]
    clipa = ab_("clipa", [128, 512], F32)
    wq_g = [ab_(f"wq_g{i}", [128, NKB * 512], BF16) for i in range(2)]
    wq_v = [ab_(f"wq_v{i}", [128, NKB * 512], BF16) for i in range(2)]
    h2ring = [ab_(f"h2ring{i}", [128, 512], F32) for i in range(4)]
    accR2 = ab_("accR2", [128, OWN], F32)
    accM2 = ab_("accM2", [128, OWN], F32)
    sc1bc = ab_("sc1bc", [128, OWN], F32)
    gsc_t = [ab_(f"gsc_t{i}", [128, 512], F32) for i in range(2)]
    vsc_t = [ab_(f"vsc_t{i}", [128, 512], F32) for i in range(2)]
    sq_t = [ab_(f"sq_t{i}", [128, 512], F32) for i in range(2)]

    p.ins(VE, lambda E: E.memset(accR2[:, :], 0.0))
    p.ins(VE, lambda E: E.memset(accM2[:, :], 0.0))
    for ts in range(4):
        bc_row(Q_SC1, ts * 512, 512, sc1bc, ts * 512, v_row_sc1, ts % 2)
    v_a1pre = p.ins(VE, lambda E: E.engine_nop(), "DV")
    p.wait(TE, "DV", v_a1pre)

    wstage_free = [("AC", 0)] * 3
    i8a_free = [("DV", 0)] * 2
    panel_free = [("PE", 0)] * 2
    set_free = [("DV", 0)] * 2
    h2_free = [[(f"HS{i}", 0)] for i in range(4)]
    gsc_free = [("DV", 0)] * 2
    vsc_free = [("DV", 0)] * 2
    sq_free = [("DV", 0)] * 2

    def quant_w_subtile(dst, dst_off, src_ap, sw_tile, u):
        slot = u % 3
        s, v = wstage_free[slot]
        p.wait(SY, s, v)
        vld = p.dma(SY, wstage[slot][:, :], src_ap, f"WS{slot}")
        ji = u % 2
        s, v = i8a_free[ji]
        p.wait(SC, s, v)
        p.wait(SC, f"WS{slot}", vld)
        vq = p.ins(SC, lambda E, sl2=slot, j2=ji, sw2=sw_tile: E.activation(
            i8a[j2][:, :], wstage[sl2][:, :], AF.Identity, scale=sw2[:, 0:1], bias=mqcol[:, 0:1]), "AC")
        wstage_free[slot] = ("AC", vq)
        p.wait(VE, "AC", vq)
        p.ins(VE, lambda E, j2=ji: E.tensor_scalar(clipa[:, :], i8a[j2][:, :], mqcol[:, 0:1], None, A.subtract))
        vw = p.ins(VE, lambda E, d=dst, o=dst_off: E.tensor_scalar(
            d[:, o:o + 512], clipa[:, :], -1.0, 1.0, A.max, A.min), "DV")
        i8a_free[ji] = ("DV", vw)
        return vw

    NPAIR = 16
    panel_pe = {}
    vwq_done = {}

    def emit_quant_unit(pb2, u2):
        buf2 = pb2 % 2
        if u2 == 0:
            p.wait(VE, "PE", panel_pe.get(pb2 - 2, 0))
        if u2 < NKB:
            quant_w_subtile(wq_g[buf2], u2 * 512, wgt[u2 * 32 + pb2, :, :],
                            swg, pb2 * 32 + u2)
        else:
            kb2 = u2 - NKB
            quant_w_subtile(wq_v[buf2], kb2 * 512, wgt[kb2 * 32 + 16 + pb2, :, :],
                            swg, pb2 * 32 + u2)

    for u in range(32):
        emit_quant_unit(0, u)
    vwq_done[0] = p.cur("DV")
    for pb in range(NPAIR):
        buf = pb % 2
        p.wait(TE, "DV", vwq_done[pb])
        vlast_mm = 0
        for ts in range(4):
            for half in range(2):
                hs = half * 2
                sf, vf = set_free[half]
                p.wait(TE, sf, vf)
                for gv in range(2):
                    wq = wq_g[buf] if gv == 0 else wq_v[buf]
                    for obh in range(2):
                        ob = hs + obh
                        bank = half * 4 + gv * 2 + obh
                        for kb in range(NKB):
                            vlast_mm = p.ins(TE, lambda E, b=bank, w=wq, k=kb, o=ob, t2=ts: E.matmul(
                                ps[b][:, :], w[:, k * 512 + o * 128:k * 512 + o * 128 + 128],
                                big1[:, k * OWN + t2 * 512:k * OWN + t2 * 512 + 512],
                                start=(k == 0), stop=(k == NKB - 1)),
                                "PE" if (kb == NKB - 1) else None)
                vlastdv = 0
                for obh in range(2):
                    ob = hs + obh
                    o_abs = pb * 4 + ob
                    bg_ = half * 4 + 0 * 2 + obh
                    bv_ = half * 4 + 1 * 2 + obh
                    jp = (ts * 2 + obh) % 2
                    p.wait(VE, "PE", vlast_mm)
                    s, v = gsc_free[jp]
                    p.wait(VE, s, v)
                    p.ins(VE, lambda E, j2=jp, b=bg_, t2=ts: E.tensor_mul(
                        gsc_t[j2][:, :], ps[b][:, :], sc1bc[:, t2 * 512:(t2 + 1) * 512]))
                    s, v = vsc_free[jp]
                    p.wait(VE, s, v)
                    vvs = p.ins(VE, lambda E, j2=jp, b=bv_, t2=ts: E.tensor_mul(
                        vsc_t[j2][:, :], ps[b][:, :], sc1bc[:, t2 * 512:(t2 + 1) * 512]), "DV")
                    p.wait(SC, "DV", vvs)
                    vsl = p.ins(SC, lambda E, j2=jp: E.activation(
                        gsc_t[j2][:, :], gsc_t[j2][:, :], AF.Silu), "AC")
                    hslot = (pb * 8 + ts * 2 + obh) % 4
                    for s, v in h2_free[hslot]:
                        p.wait(VE, s, v)
                    p.wait(VE, "AC", vsl)
                    vh2 = p.ins(VE, lambda E, j2=jp, h=hslot: E.tensor_mul(
                        h2ring[h][:, :], gsc_t[j2][:, :], vsc_t[j2][:, :]), "DV")
                    gsc_free[jp] = ("DV", vh2)
                    vsc_free[jp] = ("DV", vh2)
                    p.wait(SC, "DV", vh2)
                    vspl = p.dma(SC, d_h2[o_abs * 4 + ts, :, :],
                                 h2ring[hslot][:, :], f"HS{hslot}")
                    jq = jp
                    s, v = sq_free[jq]
                    p.wait(SC, s, v)
                    vsq3 = p.ins(SC, lambda E, j2=jq, h=hslot: E.activation(
                        sq_t[j2][:, :], h2ring[h][:, :], AF.Square), "AC")
                    p.wait(VE, "AC", vsq3)
                    p.ins(VE, lambda E, j2=jq, t2=ts: E.tensor_add(
                        accR2[:, t2 * 512:(t2 + 1) * 512], accR2[:, t2 * 512:(t2 + 1) * 512], sq_t[j2][:, :]))
                    vlastdv = p.ins(VE, lambda E, j2=jq, t2=ts, oa=o_abs: E.scalar_tensor_tensor(
                        accM2[:, t2 * 512:(t2 + 1) * 512], sq_t[j2][:, :], gd2col[:, oa:oa + 1],
                        accM2[:, t2 * 512:(t2 + 1) * 512], A.mult, A.max), "DV")
                    sq_free[jq] = ("DV", vlastdv)
                    h2_free[hslot] = [(f"HS{hslot}", vspl), ("AC", vsq3)]
                set_free[half] = ("DV", vlastdv)
                if pb + 1 < NPAIR:
                    g = ts * 2 + half
                    for q in range(4):
                        emit_quant_unit(pb + 1, g * 4 + q)
        panel_pe[pb] = vlast_mm
        vwq_done[pb + 1] = p.cur("DV")
    v_a1 = p.cur("DV")

    for c in range(16):
        p.wait(TE, "DV", v_a1)
        p.use_bank(2)
        vt1 = p.ins(TE, lambda E, c2=c: E.transpose(ps[2][:, 0:128], accR2[:, c2 * 128:(c2 + 1) * 128], c_id[:, :]), "PE")
        p.use_bank(3)
        vt2 = p.ins(TE, lambda E, c2=c: E.transpose(ps[3][:, 0:128], accM2[:, c2 * 128:(c2 + 1) * 128], c_id[:, :]), "PE")
        p.wait(VE, "PE", vt1)
        vrd1 = p.ins(VE, lambda E, c2=c: E.reduce_sum(stat_cols[:, c2:c2 + 1], ps[2][:, 0:128], AX), "DV")
        p.free_bank(2, "DV", vrd1)
        p.wait(VE, "PE", vt2)
        vrd2 = p.ins(VE, lambda E, c2=c: E.reduce_max(stat_cols[:, 16 + c2:17 + c2], ps[3][:, 0:128], AX), "DV")
        p.free_bank(3, "DV", vrd2)
    vst = p.ins(VE, lambda E: E.tensor_scalar(stat_cols[:, 0:16], stat_cols[:, 0:16], 1.0 / DHALF, None, A.mult), "DV")
    p.wait(SC, "DV", vst)
    vsr2 = p.ins(SC, lambda E: E.activation(stat_cols[:, 0:16], stat_cols[:, 0:16], AF.Sqrt, bias=epscol[:, 0:1]), "AC")
    p.wait(VE, "AC", vsr2)
    p.ins(VE, lambda E: E.reciprocal(stat_cols[:, 0:16], stat_cols[:, 0:16]))
    vms = p.ins(VE, lambda E: E.engine_nop(), "DV")
    p.wait(SC, "DV", vms)
    vsm2 = p.ins(SC, lambda E: E.activation(stat_cols[:, 16:32], stat_cols[:, 16:32], AF.Sqrt), "AC")
    p.wait(VE, "AC", vsm2)
    p.ins(VE, lambda E: E.tensor_mul(stat_cols[:, 16:32], stat_cols[:, 16:32], stat_cols[:, 0:16]))
    p.ins(VE, lambda E: E.tensor_scalar(stat_cols[:, 16:32], stat_cols[:, 16:32], 1e-5, None, A.max))
    p.ins(VE, lambda E: E.reciprocal(stat_cols[:, 32:48], stat_cols[:, 16:32]))
    vf2c = p.ins(VE, lambda E: E.tensor_scalar(
        mxcols[:, 0:16], stat_cols[:, 32:48], 127.0, None, A.mult), "DV")
    p.ins(VE, lambda E: E.tensor_mul(mxcols[:, 0:16], mxcols[:, 0:16], stat_cols[:, 0:16]))
    vf2c = p.ins(VE, lambda E: E.engine_nop(), "DV")
    v_row_f2 = row_put(vf2c, mxcols[:, 0:16], Q_F2)
    vp2c = p.ins(VE, lambda E: E.tensor_scalar(
        mxcols[:, 0:16], stat_cols[:, 16:32], swd[:, 1:2], 1.0 / 127.0, A.mult, A.mult), "DV")
    v_row_p2 = row_put(vp2c, mxcols[:, 0:16], Q_P2)

    p.sync_phase(sems=("SW", "SA", "DV", "AC", "PE"))
    a1.close()

    a2 = ExitStack()
    ab2 = lambda name, shape, dtyp: a2.enter_context(nc.sbuf_tensor(name, shape, dtyp))
    wd_st = [ab2(f"wd_st{i}", [128, 512], F32) for i in range(2)]
    wd_q = [ab2(f"wd_q{i}", [128, 512], BF16) for i in range(2)]
    i8b = [ab2(f"i8b{i}", [128, 512], F32) for i in range(2)]
    clipb = ab2("clipb", [128, 512], F32)
    h2f = [ab2(f"h2f{i}", [128, 1024], F32) for i in range(2)]
    h2qs = [ab2(f"h2qs{i}", [128, 1024], BF16) for i in range(2)]
    h2qr = [ab2(f"h2qr{i}", [128, 512], BF16) for i in range(4)]
    i32b = [ab2(f"i32b{i}", [128, 1024], F32) for i in range(2)]
    preb = [ab2(f"preb{i}", [128, 1024], F32) for i in range(2)]
    bc3g = ab2("bc3g", [128, 2048], F32)
    bc4g = ab2("bc4g", [128, 2048], F32)

    for t4 in range(4):
        bc_row(Q_F2, t4 * 512, 512, bc3g, t4 * 512, v_row_f2, 0)
        bc_row(Q_P2, t4 * 512, 512, bc4g, t4 * 512, v_row_p2, 1)
    v_a2bc = p.cur("DV")
    p.wait(TE, "DV", v_a2bc)

    wdst_free = [("AC", 0)] * 2
    wdq_free = [("PE", 0)] * 2
    i8b_free = [("DV", 0)] * 2
    h2f_free = [("DV", 0)] * 2
    h2qs_free = [("HQ0", 0), ("HQ1", 0)]
    h2qr_free = [("PE", 0)] * 4
    i32b_free = [("DV", 0)] * 2
    preb_free = [("AC", 0)] * 2
    s_evac_done = 0
    for st_ in range(4):
        if st_ == 1:
            for eng in (GP, SC):
                p.wait(eng, "HQ0", p.cur("HQ0"))
                p.wait(eng, "HQ1", p.cur("HQ1"))
        for th in range(2):
            vlast_mm = 0
            for kb in range(NDKB):
                wslot = kb % 2
                s, v = wdst_free[wslot]
                p.wait(SY, s, v)
                vld = p.dma(SY, wd_st[wslot][:, :], wdt[kb * 4 + st_, :, :], f"WD{wslot}")
                s, v = i8b_free[wslot]
                p.wait(SC, s, v)
                p.wait(SC, f"WD{wslot}", vld)
                vq = p.ins(SC, lambda E, w=wslot: E.activation(
                    i8b[w][:, :], wd_st[w][:, :], AF.Identity, scale=swd[:, 0:1], bias=mqcol[:, 0:1]), "AC")
                wdst_free[wslot] = ("AC", vq)
                s, v = wdq_free[wslot]
                p.wait(VE, s, v)
                p.wait(VE, "AC", vq)
                p.ins(VE, lambda E, w=wslot: E.tensor_scalar(clipb[:, :], i8b[w][:, :], mqcol[:, 0:1], None, A.subtract))
                vqd = p.ins(VE, lambda E, w=wslot: E.tensor_scalar(wd_q[w][:, :], clipb[:, :], -1.0, 1.0, A.max, A.min), "DV")
                i8b_free[wslot] = ("DV", vqd)
                p.wait(TE, "DV", vqd)
                if st_ == 0:
                    hslot = kb % 2
                    heng = GP if hslot == 0 else SC
                    s, v = h2f_free[hslot]
                    p.wait(heng, s, v)
                    for j in range(2):
                        vh = p.dma(heng, h2f[hslot][:, j * 512:(j + 1) * 512],
                                   d_h2[kb * 4 + th * 2 + j, :, :], f"GH2{hslot}")
                    s, v = preb_free[hslot]
                    p.wait(VE, s, v)
                    p.wait(VE, f"GH2{hslot}", vh)
                    vpre = p.ins(VE, lambda E, h=hslot, k=kb, t2=th: E.scalar_tensor_tensor(
                        preb[h][:, :], h2f[h][:, :], gdcol[:, k:k + 1], bc3g[:, t2 * 1024:(t2 + 1) * 1024],
                        A.mult, A.mult), "DV")
                    h2f_free[hslot] = ("DV", vpre)
                    s, v = i32b_free[hslot]
                    p.wait(SC, s, v)
                    p.wait(SC, "DV", vpre)
                    vcv = p.ins(SC, lambda E, h=hslot: E.activation(
                        i32b[h][:, :], preb[h][:, :], AF.Identity, bias=mqcol[:, 0:1]), "AC")
                    preb_free[hslot] = ("AC", vcv)
                    s, v = h2qs_free[hslot]
                    p.wait(GP, s, v)
                    p.wait(GP, "AC", vcv)
                    vqq = p.ins(GP, lambda E, h=hslot: E.tensor_scalar(
                        h2qs[h][:, :], i32b[h][:, :], mqcol[:, 0:1], None, A.subtract), "PL")
                    i32b_free[hslot] = ("PL", vqq)
                    p.wait(SC, "PL", vqq)
                    for j in range(2):
                        vsp = p.dma(SC, d_h2q[kb * 4 + th * 2 + j, :, :],
                                    h2qs[hslot][:, j * 512:(j + 1) * 512], f"HQ{hslot}")
                    h2qs_free[hslot] = (f"HQ{hslot}", vsp)
                    p.wait(TE, "PL", vqq)
                    for w2 in range(2):
                        bank = (kb % 1) * 0
                        for hb in range(4):
                            b = hb * 2 + w2
                            vlast_mm = p.ins(TE, lambda E, b2=b, w=wslot, h=hslot, h3=hb, w3=w2, k=kb: E.matmul(
                                ps[b2][:, :], wd_q[w][:, h3 * 128:(h3 + 1) * 128],
                                h2qs[h][:, w3 * 512:(w3 + 1) * 512],
                                start=(k == 0), stop=(k == NDKB - 1)), "PE" if hb == 3 else None)
                else:
                    for w2 in range(2):
                        rslot = (kb * 2 + w2) % 4
                        reng = (GP, GP, SC, GP)[rslot]
                        s, v = h2qr_free[rslot]
                        p.wait(reng, s, v)
                        vh = p.dma(reng, h2qr[rslot][:, :],
                                   d_h2q[kb * 4 + th * 2 + w2, :, :],
                                   f"GHQ{rslot}")
                        p.wait(TE, f"GHQ{rslot}", vh)
                        for hb in range(4):
                            b = hb * 2 + w2
                            vlast_mm = p.ins(TE, lambda E, b2=b, w=wslot, r=rslot, h3=hb, k=kb: E.matmul(
                                ps[b2][:, :], wd_q[w][:, h3 * 128:(h3 + 1) * 128], h2qr[r][:, :],
                                start=(k == 0), stop=(k == NDKB - 1)), "PE" if hb == 3 else None)
                        h2qr_free[rslot] = ("PE", vlast_mm)
                wdq_free[wslot] = ("PE", vlast_mm)
            p.wait(VE, "PE", vlast_mm)
            for w2 in range(2):
                for hb in range(4):
                    b = hb * 2 + w2
                    ho = st_ * 4 + hb
                    s_evac_done = p.ins(VE, lambda E, b2=b, h3=ho, t2=th, w3=w2: E.tensor_mul(
                        big1[:, h3 * OWN + t2 * 1024 + w3 * 512:h3 * OWN + t2 * 1024 + w3 * 512 + 512],
                        ps[b2][:, :], bc4g[:, t2 * 1024 + w3 * 512:t2 * 1024 + w3 * 512 + 512]), "DV")
            p.wait(TE, "DV", s_evac_done)
    v_a2 = p.cur("DV")

    p.sync_phase(sems=("SW", "SG", "SA", "DV", "AC", "PE"))
    a2.close()

    p.wait(SY, "SG", v_th)
    p.wait(SY, "PL", v_bis)
    p.ins(VE, lambda E: E.memset(th_row[:1, :], 0.0))
    p.wait(SY, "DV", p.cur("DV"))
    vthl = p.dma(SY, th_row[:1, 0:8], d_small[3:4, 0:8], "SW")
    p.wait(TE, "SW", vthl)
    p.use_bank(0)
    vthb = p.ins(TE, lambda E: E.matmul(ps[0][:, 0:16], ones_row[:1, :], th_row[:1, 0:16], start=True, stop=True), "PE")
    p.wait(VE, "PE", vthb)
    vthe = p.ins(VE, lambda E: E.tensor_copy(out=th_cols[:, :], in_=ps[0][:, 0:8]), "DV")
    p.free_bank(0, "DV", vthe)
    for e in range(NE):
        p.ins(VE, lambda E, e2=e: E.scalar_tensor_tensor(
            tmpA[:, 0:16], sm8[:, e2 * 128:e2 * 128 + 16], th_cols[:, e2:e2 + 1],
            sm8[:, e2 * 128:e2 * 128 + 16], A.is_ge, A.mult))
        if e == 0:
            p.ins(VE, lambda E: E.tensor_copy(out=mxcols[:, 0:16], in_=tmpA[:, 0:16]))
        else:
            p.ins(VE, lambda E: E.tensor_add(mxcols[:, 0:16], mxcols[:, 0:16], tmpA[:, 0:16]))
    vco = p.ins(VE, lambda E: E.engine_nop(), "DV")
    v_row_co = row_put(vco, mxcols[:, 0:16], Q_CO)

    p.sync_phase()
    es2.close()

    f1 = ExitStack()
    fb_ = lambda name, shape, dtyp: f1.enter_context(nc.sbuf_tensor(name, shape, dtyp))
    xnt_res = fb_("xnt_res", [128, NKB * 2048], BF16)
    fw_st = [fb_(f"fw_st{i}", [128, 128], F32) for i in range(4)]
    fq_g = [fb_(f"fq_g{i}", [128, NKB * 128], BF16) for i in range(2)]
    fq_u = [fb_(f"fq_u{i}", [128, NKB * 128], BF16) for i in range(2)]
    silf = [fb_(f"silf{i}", [128, 512], F32) for i in range(2)]
    ubuf = [fb_(f"ubuf{i}", [128, 512], F32) for i in range(2)]
    midb = [fb_(f"midb{i}", [128, 512], BF16) for i in range(4)]

    p.wait(GP, "XB0", p.cur("XB0"))
    p.wait(GP, "XB1", p.cur("XB1"))
    vxa_kb = []
    for kb in range(NKB):
        for sl in range(4):
            vxa = p.dma(GP, xnt_res[:, kb * 2048 + sl * 512:kb * 2048 + (sl + 1) * 512],
                        d_xnt[kb * 4 + sl, :, :], "GXA")
        vxa_kb.append(vxa)

    fwst_free = [("DV", 0)] * 4
    fbank_free = [("DV", 0)] * 8
    silf_free = [("DV", 0)] * 2
    ubuf_free = [("DV", 0)] * 2
    midb_free = [(f"MB{i}", 0) for i in range(4)]
    f1_spill = {}
    fcvt_done = {}
    fpanel_pe = {}

    def emit_fcvt_unit(pc2, u2):
        buf2 = pc2 % 2
        if u2 == 0:
            p.wait(VE, "PE", fpanel_pe.get(pc2 - 2, 0))
        gu = u2 // NKB
        kb2 = u2 % NKB
        src = fwg if gu == 0 else fwu
        dstp = fq_g[buf2] if gu == 0 else fq_u[buf2]
        slot = (pc2 * 32 + u2) % 4
        s, v = fwst_free[slot]
        p.wait(SY, s, v)
        vld = p.dma(SY, fw_st[slot][:, :], src[kb2 * NFKB + pc2, :, :], f"FW{slot}")
        p.wait(VE, f"FW{slot}", vld)
        vcvt = p.ins(VE, lambda E, sl2=slot, d=dstp, k=kb2: E.tensor_copy(
            out=d[:, k * 128:(k + 1) * 128], in_=fw_st[sl2][:, :]), "DV")
        fwst_free[slot] = ("DV", vcvt)

    for u in range(32):
        emit_fcvt_unit(0, u)
    fcvt_done[0] = p.cur("DV")
    for pc in range(NFKB):
        buf = pc % 2
        p.wait(TE, "DV", fcvt_done[pc])
        for ts in range(4):
            vlast_mm = 0
            for gu in range(2):
                bank = ts * 2 + gu
                s, v = fbank_free[bank]
                p.wait(TE, s, v)
                wqp = fq_g[buf] if gu == 0 else fq_u[buf]
                for kb in range(NKB):
                    if pc == 0 and gu == 0:
                        p.wait(TE, "GXA", vxa_kb[kb])
                    vlast_mm = p.ins(TE, lambda E, b=bank, w=wqp, k=kb, t2=ts: E.matmul(
                        ps[b][:, :], w[:, k * 128:(k + 1) * 128],
                        xnt_res[:, k * 2048 + t2 * 512:k * 2048 + t2 * 512 + 512],
                        start=(k == 0), stop=(k == NKB - 1)),
                        "PE" if kb == NKB - 1 else None)
            jp = ts % 2
            p.wait(SC, "PE", vlast_mm)
            s, v = silf_free[jp]
            p.wait(SC, s, v)
            vslf = p.ins(SC, lambda E, j2=jp, b=ts * 2, f2=pc: E.activation(
                silf[j2][:, :], ps[b][:, :], AF.Silu, bias=bgcol[:, f2:f2 + 1]), "AC")
            s, v = ubuf_free[jp]
            p.wait(SC, s, v)
            vub = p.ins(SC, lambda E, j2=jp, b=ts * 2 + 1, f2=pc: E.activation(
                ubuf[j2][:, :], ps[b][:, :], AF.Identity, bias=bucol[:, f2:f2 + 1]), "AC")
            mslot = ts
            s, v = midb_free[mslot]
            p.wait(VE, s, v)
            p.wait(VE, "AC", vub)
            vmd = p.ins(VE, lambda E, j2=jp, m=mslot: E.tensor_mul(
                midb[m][:, :], silf[j2][:, :], ubuf[j2][:, :]), "DV")
            silf_free[jp] = ("DV", vmd)
            ubuf_free[jp] = ("DV", vmd)
            fbank_free[ts * 2] = ("DV", vmd)
            fbank_free[ts * 2 + 1] = ("DV", vmd)
            p.wait(SC, "DV", vmd)
            vmsp = p.dma(SC, d_mid[pc * 4 + ts, :, :], midb[mslot][:, :], f"MB{mslot}")
            midb_free[mslot] = (f"MB{mslot}", vmsp)
            f1_spill[(pc, ts)] = vmsp
            if pc + 1 < NFKB:
                for q in range(8):
                    emit_fcvt_unit(pc + 1, ts * 8 + q)
        fpanel_pe[pc] = vlast_mm
        fcvt_done[pc + 1] = p.cur("DV")

    p.sync_phase(sems=("SW", "SG", "SA", "DV", "AC", "PE"))
    f1.close()

    f2 = ExitStack()
    fb2_ = lambda name, shape, dtyp: f2.enter_context(nc.sbuf_tensor(name, shape, dtyp))
    mid_res = fb2_("mid_res", [128, NFKB * 1024], BF16)
    fd_st = [fb2_(f"fd_st{i}", [128, 512], F32) for i in range(2)]
    fd_q = [fb2_(f"fd_q{i}", [128, 512], BF16) for i in range(2)]
    xres = [fb2_(f"xres{i}", [128, 512], F32) for i in range(2)]
    outt = [fb2_(f"outt{i}", [128, 512], F32) for i in range(2)]
    gbuf = [fb2_(f"gbuf{i}", [128, 512], F32) for i in range(2)]
    cobc = fb2_("cobc", [128, 2048], F32)

    for t4 in range(4):
        bc_row(Q_CO, t4 * 512, 512, cobc, t4 * 512, v_row_co, 0)
    v_f2bc = p.cur("DV")
    p.wait(TE, "DV", v_f2bc)

    fdst_free = [("DV", 0)] * 2
    fdq_free = [("PE", 0)] * 2
    xres_free = [("DV", 0)] * 2
    outt_free = [("OUT", 0)] * 2
    gbuf_free = [("DV", 0)] * 2
    for th in range(2):
        if th == 1:
            p.wait(SC, "PE", p.cur("PE"))
        vml_kb = []
        for kb in range(NFKB):
            for j in range(2):
                p.wait(SC, f"MB{th * 2 + j}", f1_spill[(kb, th * 2 + j)])
                vml = p.dma(SC, mid_res[:, kb * 1024 + j * 512:kb * 1024 + (j + 1) * 512],
                            d_mid[kb * 4 + th * 2 + j, :, :], "GMI")
            vml_kb.append(vml)
        for st_ in range(4):
            vlast_mm = 0
            for kb in range(NFKB):
                wslot = kb % 2
                weng = SY if wslot == 0 else GP
                s, v = fdst_free[wslot]
                p.wait(weng, s, v)
                vld = p.dma(weng, fd_st[wslot][:, :], fwd[kb * 4 + st_, :, :], f"FD{wslot}")
                s, v = fdq_free[wslot]
                p.wait(VE, s, v)
                p.wait(VE, f"FD{wslot}", vld)
                vcv = p.ins(VE, lambda E, w=wslot: E.tensor_copy(out=fd_q[w][:, :], in_=fd_st[w][:, :]), "DV")
                fdst_free[wslot] = ("DV", vcv)
                p.wait(TE, "DV", vcv)
                if st_ == 0:
                    p.wait(TE, "GMI", vml_kb[kb])
                for w2 in range(2):
                    for hb in range(4):
                        b = hb * 2 + w2
                        vlast_mm = p.ins(TE, lambda E, b2=b, w=wslot, h3=hb, k=kb, w3=w2: E.matmul(
                            ps[b2][:, :], fd_q[w][:, h3 * 128:(h3 + 1) * 128],
                            mid_res[:, k * 1024 + w3 * 512:k * 1024 + w3 * 512 + 512],
                            start=(k == 0), stop=(k == NFKB - 1)), "PE" if (w2 == 1 and hb == 3) else None)
                fdq_free[wslot] = ("PE", vlast_mm)
            p.wait(SC, "PE", vlast_mm)
            for w2 in range(2):
                for hb in range(4):
                    b = hb * 2 + w2
                    ho = st_ * 4 + hb
                    toff = th * 1024 + w2 * 512
                    jg = (w2 * 4 + hb) % 2
                    xslot = jg
                    s, v = xres_free[xslot]
                    p.wait(GP, s, v)
                    vxl = p.dma(GP, xres[xslot][:, :], xt[ho * 32 + th * 2 + w2, :, :], f"XE{xslot}")
                    s, v = gbuf_free[jg]
                    p.wait(SC, s, v)
                    vg1 = p.ins(SC, lambda E, j2=jg, b2=b, h3=ho: E.activation(
                        gbuf[j2][:, :], ps[b2][:, :], AF.Identity, bias=bdcol[:, h3:h3 + 1]), "AC")
                    p.wait(VE, "AC", vg1)
                    vg2 = p.ins(VE, lambda E, j2=jg, t2=toff: E.tensor_mul(
                        gbuf[j2][:, :], gbuf[j2][:, :], cobc[:, t2:t2 + 512]), "DV")
                    p.wait(SC, "DV", vg2)
                    vg3 = p.ins(SC, lambda E, j2=jg: E.activation(gbuf[j2][:, :], gbuf[j2][:, :], AF.Silu), "AC")
                    oslot = jg
                    s, v = outt_free[oslot]
                    p.wait(VE, s, v)
                    p.wait(VE, "AC", vg3)
                    p.ins(VE, lambda E, j2=jg, o=oslot, h3=ho, t2=toff: E.tensor_mul(
                        outt[o][:, :], gbuf[j2][:, :], big1[:, h3 * OWN + t2:h3 * OWN + t2 + 512]))
                    p.wait(VE, f"XE{xslot}", vxl)
                    vo = p.ins(VE, lambda E, o=oslot, x=xslot: E.tensor_add(
                        outt[o][:, :], outt[o][:, :], xres[x][:, :]), "DV")
                    xres_free[xslot] = ("DV", vo)
                    gbuf_free[jg] = ("DV", vo)
                    p.wait(SC, "DV", vo)
                    vout = p.dma(SC, yt[ho * 128:(ho + 1) * 128, toff:toff + 512], outt[oslot][:, :], f"OT{oslot}")
                    outt_free[oslot] = (f"OT{oslot}", vout)
            p.wait(TE, "OT0", p.cur("OT0"))
            p.wait(TE, "OT1", p.cur("OT1"))

    p.wait(SY, "OT0", p.cur("OT0"))
    p.wait(SY, "OT1", p.cur("OT1"))
    f2.close()

    semstack = ExitStack()
    for name in sorted(set(list(Plan.SEMS) + list(p.cnt.keys()))):
        p.sem[name] = semstack.enter_context(nc.semaphore(name))
    with nc.Block() as block:
        @block.sync
        def _(E):
            for f in p.ops["sync"]:
                f(E)

        @block.vector
        def _(E):
            for f in p.ops["vector"]:
                f(E)

        @block.scalar
        def _(E):
            for f in p.ops["scalar"]:
                f(E)

        @block.tensor
        def _(E):
            for f in p.ops["tensor"]:
                f(E)

        @block.gpsimd
        def _(E):
            for f in p.ops["gpsimd"]:
                f(E)

    semstack.close()
    pse.close()
    es.close()
    return nc


def kernel(L0_attn_x, L0_attn_w_gate, L0_attn_g_gate, L0_attn_w_down, L0_attn_g_down,
           L0_ffn_router_w, L0_ffn_router_b, L0_ffn_w_gate, L0_ffn_b_gate,
           L0_ffn_w_up, L0_ffn_b_up, L0_ffn_w_down, L0_ffn_b_down):
    f32 = np.float32

    def tiles(a, pr, pc_):
        nr = a.shape[0] // 128
        ncol = a.shape[1] // pc_
        return np.ascontiguousarray(
            a.reshape(nr, 128, ncol, pc_).transpose(0, 2, 1, 3).reshape(nr * ncol, 128, pc_))

    x = np.ascontiguousarray(np.asarray(L0_attn_x, f32).reshape(NT, H))
    xT = np.ascontiguousarray(x.T)
    wgT = np.asarray(L0_attn_w_gate, f32).T
    wdT = np.asarray(L0_attn_w_down, f32).T
    wgt_t = tiles(np.ascontiguousarray(wgT), 128, 512)
    wdt_t = tiles(np.ascontiguousarray(wdT), 128, 512)
    fwg = np.zeros((H, DFFP), f32)
    fwg[:, :DFF] = np.asarray(L0_ffn_w_gate, f32)
    fwu = np.zeros((H, DFFP), f32)
    fwu[:, :DFF] = np.asarray(L0_ffn_w_up, f32)
    fwd = np.zeros((DFFP, H), f32)
    fwd[:DFF, :] = np.asarray(L0_ffn_w_down, f32)
    fwg_t = tiles(fwg, 128, 128)
    fwu_t = tiles(fwu, 128, 128)
    fwd_t = tiles(fwd, 128, 512)
    bgp = np.zeros((DFFP,), f32)
    bgp[:DFF] = np.asarray(L0_ffn_b_gate, f32)
    bup = np.zeros((DFFP,), f32)
    bup[:DFF] = np.asarray(L0_ffn_b_up, f32)

    ggate2 = np.ascontiguousarray(np.asarray(L0_attn_g_gate, f32).reshape(NKB, 128).T)
    gdown2 = np.ascontiguousarray(np.asarray(L0_attn_g_down, f32).reshape(NDKB, 128).T)
    bg2 = np.ascontiguousarray(bgp.reshape(NFKB, 128).T)
    bu2 = np.ascontiguousarray(bup.reshape(NFKB, 128).T)
    bd2 = np.ascontiguousarray(np.asarray(L0_ffn_b_down, f32).reshape(NKB, 128).T)
    rb = np.asarray(L0_ffn_router_b, f32).reshape(1, NE)
    rw = np.asarray(L0_ffn_router_w, f32)
    rwa1 = np.zeros((128, NKB * 16), f32)
    rwa2 = np.zeros((128, NKB * 16), f32)
    for kb in range(NKB):
        rwa1[:, kb * 16:kb * 16 + 8] = rw[kb * 128:(kb + 1) * 128, :]
        rwa1[:, kb * 16 + 8] = 1.0
        rwa2[:, kb * 16 + 9] = 1.0
    ident = np.eye(128, dtype=f32)

    gh, gl = _gumbel_tables()

    nc = build_program_synced()
    in_maps = []
    for c in range(NCORE):
        sh = c * OWN
        xTc = np.concatenate([xT[:, sh:], xT[:, :sh]], axis=1)
        xtc_t = tiles(np.ascontiguousarray(xTc), 128, 512)
        ghc = np.concatenate([gh[sh:], gh[:sh]], axis=0)
        glc = np.concatenate([gl[sh:], gl[:sh]], axis=0)
        gmhc = np.ascontiguousarray(ghc.reshape(128, 128, NE).transpose(2, 1, 0))
        gmlc = np.ascontiguousarray(glc.reshape(128, 128, NE).transpose(2, 1, 0))
        in_maps.append({
            "xt": xtc_t, "gmh": gmhc, "gml": gmlc, "wgt": wgt_t, "wdt": wdt_t,
            "fwg": fwg_t, "fwu": fwu_t, "fwd": fwd_t, "ggate2": ggate2, "gdown2": gdown2,
            "rwa1": rwa1, "rwa2": rwa2, "bg2": bg2, "bu2": bu2, "bd2": bd2,
            "rbrow": rb, "identd": ident,
        })
    res = run_bass_kernel_spmd(nc, in_maps, core_ids=list(range(NCORE)))
    out = np.empty((NT, H), f32)
    for c in range(NCORE):
        out[c * OWN:(c + 1) * OWN, :] = res.results[c]["yt"].T
    return out.reshape(4, 4096, 2048)


def build_program_synced():
    rec = {}
    build_program(rec=rec)
    dep = compute_edges(rec)
    return build_program(dep=dep)


if __name__ == "__main__":
    prog = build_program_synced()
    n = sum(1 for _ in prog.all_instructions())
    print("built ok, instructions:", n)


# version 08
# speedup vs baseline: 1.1841x; 1.1841x over previous
import os
import numpy as np
import ml_dtypes
import concourse.bass as bass
import concourse.mybir as mybir
from concourse.bass_utils import run_bass_kernel_spmd
from contextlib import ExitStack

F32 = mybir.dt.float32
F32R = mybir.dt.float32r
BF16 = mybir.dt.bfloat16
I32 = mybir.dt.int32
I8 = mybir.dt.int8
A = mybir.AluOpType
AF = mybir.ActivationFunctionType
AX = mybir.AxisListType.X

NCORE = 8
NT = 16384
OWN = 2048
H = 2048
NKB = 16
DG = 16384
DHALF = 8192
NDKB = 64
DFF = 5461
DFFP = 5632
NFKB = 44
NE = 8
KCAP = 2048
M32 = 12582912.0
M64 = 6755399441055744.0
N_BISECT = 36
EPS_LN = 1e-5
EPS_G = 1e-10
INV_H = 1.0 / 2048.0
NWG = float(DG * H)
NWD = float(DHALF * H)

Q_R, Q_MR, Q_RS, Q_SQ, Q_SC1, Q_F2, Q_P2, Q_CO = range(8)


def _gumbel_tables():
    import jax
    import jax.numpy as jnp
    cpu = jax.devices("cpu")[0]
    fp64 = os.environ.get("BASS_SIM_FP64") == "1"
    with jax.default_device(cpu):
        key = jax.random.key(42, impl="threefry2x32")
        if fp64:
            from jax.experimental import enable_x64
            with enable_x64():
                u = jax.random.uniform(key, (NT, NE), dtype=jnp.float64)
                g = np.asarray(-jnp.log(-jnp.log(u + EPS_G) + EPS_G), dtype=np.float64)
            hi = g.astype(np.float32)
            lo = (g - hi.astype(np.float64)).astype(np.float32)
        else:
            u = jax.random.uniform(key, (NT, NE), dtype=jnp.float32)
            g = -jnp.log(-jnp.log(u + np.float32(EPS_G)) + np.float32(EPS_G))
            hi = np.asarray(g, dtype=np.float32)
            lo = np.zeros_like(hi)
    return hi, lo


class Plan:
    ENGS = ("sync", "vector", "scalar", "tensor", "gpsimd")
    SEMS = ("SW", "SG", "SA", "DV", "AC", "PE", "PL", "OUT")
    ESEM = {"vector": "DV", "scalar": "AC", "gpsimd": "PL"}

    def __init__(self, dep=None, rec=None):
        self.ops = {e: [] for e in self.ENGS}
        self.sem = {}
        self.cnt = {}
        self.hw = {e: {} for e in self.ENGS}
        self.bankf = {}
        self.dep = dep
        self.rec = rec
        self.opidx = {e: 0 for e in self.ENGS}

    def _dep_waits(self, eng, idx):
        if self.dep is not None:
            for s, v in self.dep.get((eng, idx), ()):
                self.wait(eng, s, v)

    def _mk_recorder(self, fn, eng, idx, sem, val):
        rec = self.rec
        def run(E, f=fn, key=(eng, idx), sv=(sem, val)):
            inst = f(E)
            if rec is not None:
                try:
                    mi = inst.ins
                    rd = tuple(getattr(x, "memref", None) for x in mi.ins)
                    wr = tuple(getattr(x, "memref", None) for x in mi.outs)
                    rec[key] = (tuple(n for n in rd if n), tuple(n for n in wr if n), sv)
                except Exception:
                    pass
            return inst
        return run

    def use_bank(self, b, eng="tensor"):
        s_v = self.bankf.get(b)
        if s_v:
            self.wait(eng, s_v[0], s_v[1])

    def free_bank(self, b, sem, val):
        self.bankf[b] = (sem, val)

    def emit(self, eng, fn):
        if os.environ.get("KBUILD_DEBUG"):
            import traceback
            site = traceback.extract_stack()[-3]
            def wrapped(E, f=fn, s=site):
                try:
                    return f(E)
                except Exception:
                    print(f"PLAN-SITE: {s.filename}:{s.lineno} {s.line}")
                    raise
            self.ops[eng].append(wrapped)
        else:
            self.ops[eng].append(fn)

    def bump(self, name, k=1):
        self.cnt[name] = self.cnt.get(name, 0) + k
        return self.cnt[name]

    def cur(self, name):
        return self.cnt.get(name, 0)

    def wait(self, eng, sem, val):
        if not val or val <= 0:
            return
        if self.hw[eng].get(sem, 0) >= val:
            return
        self.hw[eng][sem] = val
        self.emit(eng, lambda E, s=sem, v=val: E.wait_ge(self.sem[s], v))

    def dma(self, eng, out, in_, sem, slow=False, ordered=False):
        idx = self.opidx[eng]
        self.opidx[eng] += 1
        self._dep_waits(eng, idx)
        if ordered:
            self.wait(eng, sem, self.cur(sem))
        v = self.bump(sem, 16)
        fn = lambda E, o=out, i=in_, s=sem, sl=slow: E.dma_start(
            out=o, in_=i, allow_slow_non_contiguous=sl).then_inc(self.sem[s], 16)
        self.emit(eng, self._mk_recorder(fn, eng, idx, sem, v))
        return v

    def ins(self, eng, fn, sem=None):
        idx = self.opidx[eng]
        self.opidx[eng] += 1
        self._dep_waits(eng, idx)
        auto = self.ESEM.get(eng)
        if auto is not None:
            sem = auto
        if sem is None:
            self.emit(eng, self._mk_recorder(fn, eng, idx, None, None))
            return None
        v = self.bump(sem)
        fn2 = lambda E, f=fn, s=sem: f(E).then_inc(self.sem[s], 1)
        self.emit(eng, self._mk_recorder(fn2, eng, idx, sem, v))
        return v

    def sync_phase(self, sems=None, engs=None):
        sems = sems or list(self.cnt.keys())
        snap = [(s, self.cur(s)) for s in sems]
        for eng in (engs or self.ENGS):
            for s, v in snap:
                self.wait(eng, s, v)


def v3(ap):
    return ap.rearrange("p (c o) -> p c o", o=1)


def compute_edges(rec):
    dep = {}
    streams = {}
    for (eng, idx) in rec:
        streams.setdefault(eng, []).append(idx)
    for eng, idxs in streams.items():
        if eng == "tensor":
            continue
        track_w = {}
        track_r = {}
        for idx in sorted(idxs):
            rd, wr, (sem, val) = rec[(eng, idx)]
            need = {}
            for n in rd:
                for s, v in track_w.get(n, {}).items():
                    if need.get(s, 0) < v:
                        need[s] = v
            for n in wr:
                for d in (track_w.get(n, {}), track_r.get(n, {})):
                    for s, v in d.items():
                        if need.get(s, 0) < v:
                            need[s] = v
            if sem is not None:
                need.pop(None, None)
                need = {s: v for s, v in need.items() if not (s == sem and v >= val)}
            if need:
                dep[(eng, idx)] = tuple(need.items())
            if sem is not None:
                for n in rd:
                    track_r.setdefault(n, {})[sem] = val
                for n in wr:
                    track_w[n] = {sem: val}
                    track_r[n] = {}
    return dep


def build_program(dep=None, rec=None):
    nc = bass.Bass("TRN2", target_bir_lowering=False, detect_race_conditions=False)
    dt = nc.dram_tensor

    xt = dt("xt", [NKB * 32, 128, 512], F32, kind="ExternalInput").ap()
    gmh = dt("gmh", [NE, 128, 128], F32, kind="ExternalInput").ap()
    gml = dt("gml", [NE, 128, 128], F32, kind="ExternalInput").ap()
    wgt = dt("wgt", [NKB * 32, 128, 512], F32, kind="ExternalInput").ap()
    wdt = dt("wdt", [NDKB * 4, 128, 512], F32, kind="ExternalInput").ap()
    fwg = dt("fwg", [NKB * NFKB, 128, 128], F32, kind="ExternalInput").ap()
    fwu = dt("fwu", [NKB * NFKB, 128, 128], F32, kind="ExternalInput").ap()
    fwd = dt("fwd", [NFKB * 4, 128, 512], F32, kind="ExternalInput").ap()
    ggate2 = dt("ggate2", [128, NKB], F32, kind="ExternalInput").ap()
    gdown2 = dt("gdown2", [128, NDKB], F32, kind="ExternalInput").ap()
    rwa1 = dt("rwa1", [128, NKB * 16], F32, kind="ExternalInput").ap()
    rwa2 = dt("rwa2", [128, NKB * 16], F32, kind="ExternalInput").ap()
    bg2 = dt("bg2", [128, NFKB], F32, kind="ExternalInput").ap()
    bu2 = dt("bu2", [128, NFKB], F32, kind="ExternalInput").ap()
    bd2 = dt("bd2", [128, NKB], F32, kind="ExternalInput").ap()
    rbrow = dt("rbrow", [1, NE], F32, kind="ExternalInput").ap()
    identd = dt("identd", [128, 128], F32, kind="ExternalInput").ap()
    yt = dt("yt", [H, OWN], F32, kind="ExternalOutput").ap()

    dbg = "ExternalOutput" if os.environ.get("KDEBUG_HW") else "Internal"
    d_h2 = dt("d_h2", [NDKB * 4, 128, 512], F32, kind=dbg).ap()
    d_mid = dt("d_mid", [NFKB * 4, 128, 512], BF16, kind=dbg).ap()
    d_xnt = dt("d_xnt", [NKB * 4, 128, 512], BF16, kind=dbg).ap()
    d_h2q = dt("d_h2q", [NDKB * 4, 128, 512], BF16, kind=dbg).ap()
    d_smt = dt("d_smt", [NE, NT], F32, kind=dbg).ap()
    d_rows = dt("d_rows", [8, OWN], F32, kind=dbg).ap()
    d_small = dt("d_small", [4, 128], F32, kind=dbg).ap()
    d_trep = dt("d_trep", [NE, 16], F32).ap()
    d_dbg1 = dt("d_dbg1", [128, NKB * OWN], BF16, kind=dbg).ap()
    d_dbg2 = dt("d_dbg2", [128, NKB * 512], BF16, kind=dbg).ap()

    p = Plan(dep=dep, rec=rec)
    es = ExitStack()
    sb = lambda name, shape, dtyp: es.enter_context(nc.sbuf_tensor(name, shape, dtyp))

    big1 = sb("big1", [128, NKB * OWN], BF16)
    c_id = sb("c_id", [128, 128], F32)
    ones_col = sb("ones_col", [128, 1], F32)
    ones_colr = sb("ones_colr", [128, 16], F32R)
    ones_row = sb("ones_row", [1, 128], F32)
    mqcol = sb("mqcol", [128, 1], F32)
    hlfcol = sb("hlfcol", [128, 1], F32)
    epscol = sb("epscol", [128, 1], F32)
    ggcol = sb("ggcol", [128, NKB], F32)
    gdcol = sb("gdcol", [128, NDKB], F32)
    gd2col = sb("gd2col", [128, NDKB], F32)
    bgcol = sb("bgcol", [128, NFKB], F32)
    bucol = sb("bucol", [128, NFKB], F32)
    bdcol = sb("bdcol", [128, NKB], F32)
    rbr = sb("rbr", [1, NE], F32)
    rbcols = sb("rbcols", [128, NE], F32)
    negcc = sb("negcc", [128, 16], F32)
    rown = sb("rown", [1, 128], F32)
    rowstage = [sb(f"rowstage{i}", [1, 512], F32) for i in range(2)]
    tk_r = sb("tk_r", [128, 128], F32)
    tk_mr = sb("tk_mr", [128, 128], F32)
    tk_rs = sb("tk_rs", [128, 128], F32)
    tmpA = sb("tmpA", [128, 128], F32)
    tmpB = sb("tmpB", [128, 128], F32)
    tmpC = sb("tmpC", [128, 128], F32)
    tmpD = sb("tmpD", [128, 128], F32)
    wacc_g = sb("wacc_g", [128, 1], F32)
    wacc_d = sb("wacc_d", [128, 1], F32)
    wrow = sb("wrow", [1, 16], F32)
    swg = sb("swg", [128, 2], F32)
    swd = sb("swd", [128, 2], F32)
    bis_lo = sb("bis_lo", [128, 1], F32)
    bis_hi = sb("bis_hi", [128, 1], F32)
    bis_mid = sb("bis_mid", [128, 1], F32)
    bis_m8 = sb("bis_m8", [128, 1], F32)
    bis_cnt = sb("bis_cnt", [128, 1], F32)
    bis_c16 = sb("bis_c16", [8, 16], F32)
    bis_sel = sb("bis_sel", [128, 1], F32)
    bis_t1 = sb("bis_t1", [128, 1], F32)
    bis_t2 = sb("bis_t2", [128, 1], F32)
    th_row = sb("th_row", [1, 16], F32)
    th_cols = sb("th_cols", [128, NE], F32)
    mxcols = sb("mxcols", [128, 16], F32)
    stat_cols = sb("stat_cols", [128, 48], F32)
    colasm = sb("colasm", [128, 128], F32)
    es2 = ExitStack()
    sb2 = lambda name, shape, dtyp: es2.enter_context(nc.sbuf_tensor(name, shape, dtyp))
    sm8 = sb2("sm8", [128, NE * 128], F32)
    smb = sb2("smb", [128, 1024], F32)
    bis_junk = sb2("bis_junk", [128, 1024], F32)
    rxs = ExitStack()
    xring = [rxs.enter_context(nc.sbuf_tensor(f"xring{i}", [128, 512], F32)) for i in range(4)]

    pse = ExitStack()
    ps = [pse.enter_context(nc.psum_tensor(f"ps{i}", [128, 512], F32)) for i in range(8)]

    SY, VE, SC, TE, GP = "sync", "vector", "scalar", "tensor", "gpsimd"

    def row_put(cols_dv, src16, q_idx, col0=0, ncols=16):
        p.wait(TE, "DV", cols_dv)
        p.use_bank(2)
        vt = p.ins(TE, lambda E, s2=src16: E.transpose(ps[2][0:ncols, 0:128], s2, c_id[:, :]), "PE")
        p.wait(VE, "PE", vt)
        vev = p.ins(VE, lambda E, n=ncols: E.tensor_copy(out=colasm[0:n, :], in_=ps[2][0:n, 0:128]), "DV")
        p.free_bank(2, "DV", vev)
        p.wait(SY, "DV", vev)
        vd = p.dma(SY, d_rows[q_idx:q_idx + 1, col0 * 128:(col0 + ncols) * 128].rearrange(
            "o (c q) -> (o c) q", q=128), colasm[0:ncols, :], "RW", ordered=True)
        p.wait(VE, "RW", vd)
        return vd

    rowstage_free = [("PE", 0), ("PE", 0)]
    rowstage_idx = [0]

    def bc_row(q_idx, col0, width, dst, dst_off, row_ready_sw, bank):
        j = rowstage_idx[0] % 2
        rowstage_idx[0] += 1
        s, v = rowstage_free[j]
        p.wait(SY, s, v)
        p.wait(SY, "RW", row_ready_sw)
        vr = p.dma(SY, rowstage[j][:1, 0:width], d_rows[q_idx:q_idx + 1, col0:col0 + width], f"RS{j}")
        p.wait(TE, f"RS{j}", vr)
        p.use_bank(bank)
        vb = p.ins(TE, lambda E, j2=j, w=width, b=bank: E.matmul(
            ps[b][:, 0:w], ones_row[:1, :], rowstage[j2][:1, 0:w], start=True, stop=True), "PE")
        rowstage_free[j] = ("PE", vb)
        p.wait(VE, "PE", vb)
        vev = p.ins(VE, lambda E, d=dst, o=dst_off, w=width, b=bank: E.tensor_copy(
            out=d[:, o:o + w], in_=ps[b][:, 0:w]), "DV")
        p.free_bank(bank, "DV", vev)
        return vev

    p.dma(SY, c_id[:, :], identd, "PR")
    p.dma(SY, ggcol[:, :], ggate2, "PR")
    p.dma(SY, gdcol[:, :], gdown2, "PR")
    p.dma(SY, bgcol[:, :], bg2, "PR")
    p.dma(SY, bucol[:, :], bu2, "PR")
    p.dma(SY, bdcol[:, :], bd2, "PR")
    p.dma(SY, rbr[:1, :], rbrow, "PR")
    v_pro = p.cur("PR")

    p.ins(VE, lambda E: E.memset(ones_col[:, :], 1.0))
    p.ins(VE, lambda E: E.memset(stat_cols[:, 0:16], 1.0))
    p.ins(VE, lambda E: E.tensor_copy(out=ones_colr[:, :], in_=stat_cols[:, 0:16]))
    p.ins(VE, lambda E: E.memset(ones_row[:1, :], 1.0))
    p.ins(VE, lambda E: E.memset(epscol[:, :], EPS_LN))
    p.ins(VE, lambda E: E.memset(hlfcol[:, :], 0.5))
    p.ins(VE, lambda E: E.tensor_scalar(hlfcol[:, :], hlfcol[:, :], M32, M32, A.add, A.subtract))
    p.ins(VE, lambda E: E.tensor_scalar(hlfcol[:, :], hlfcol[:, :], 2.0, None, A.mult))
    p.ins(VE, lambda E: E.tensor_scalar(mqcol[:, :], hlfcol[:, :], M64 - M32, M32, A.mult, A.add))
    p.ins(VE, lambda E: E.memset(wacc_g[:, :], 0.0))
    p.ins(VE, lambda E: E.memset(colasm[:, :], 0.0))
    p.ins(VE, lambda E: E.memset(wrow[:1, :], 0.0))
    p.ins(VE, lambda E: E.memset(wacc_d[:, :], 0.0))
    p.wait(VE, "PR", v_pro)
    v_gd2pre = p.ins(VE, lambda E: E.engine_nop(), "DV")
    p.wait(SC, "DV", v_gd2pre)
    p.ins(SC, lambda E: E.activation(gd2col[:, :], gdcol[:, :], AF.Square), "AC")

    rsc = ExitStack()
    rb_ = lambda name, shape, dtyp: rsc.enter_context(nc.sbuf_tensor(name, shape, dtyp))
    statsT = rb_("statsT", [128, 2048], F32)
    gmh_sb = rb_("gmh_sb", [128, NE * 128], F32)
    gml_sb = rb_("gml_sb", [128, NE * 128], F32)
    xfr = [rb_(f"xfr{i}", [128, 512], F32R) for i in range(2)]
    xsq = [rb_(f"xsq{i}", [128, 512], F32R) for i in range(2)]
    stats_sl = [rb_(f"statsl{i}", [128, 512], F32) for i in range(2)]
    rwa1s = rb_("rwa1s", [128, NKB * 16], F32R)
    rwa2s = rb_("rwa2s", [128, NKB * 16], F32R)
    wgpre = [rb_(f"wgpre{i}", [128, 512], F32) for i in range(4)]
    wdpre = [rb_(f"wdpre{i}", [128, 2048], F32) for i in range(2)]
    wtmp = rb_("wtmp", [128, 512], F32)
    wtmp2 = rb_("wtmp2", [128, 128], F32)
    wjunk = rb_("wjunk", [128, 512], F32)
    wjunk2 = rb_("wjunk2", [128, 1024], F32)

    for e in range(NE):
        p.dma(SY, gmh_sb[:, e * 128:(e + 1) * 128], gmh[e, :, :], "PR")
        p.dma(SY, gml_sb[:, e * 128:(e + 1) * 128], gml[e, :, :], "PR")
    p.dma(SY, wgpre[0][:, 0:256], rwa1, "PR")
    v_rwl = p.dma(SY, wgpre[1][:, 0:256], rwa2, "PR")
    p.wait(VE, "PR", v_rwl)
    p.ins(VE, lambda E: E.tensor_copy(out=rwa1s[:, :], in_=wgpre[0][:, 0:256]))
    v_rwa = p.ins(VE, lambda E: E.tensor_copy(out=rwa2s[:, :], in_=wgpre[1][:, 0:256]), "DV")

    NSL = NT // 512
    xr_free = [("DV", 0)] * 4
    xfr_free = [("PE", 0)] * 2
    xsq_free = [("PE", 0)] * 2
    stats_free = [("DV", 0)] * 2
    p.ins(VE, lambda E: E.memset(stats_sl[0][:, :], 0.0))
    p.ins(VE, lambda E: E.memset(stats_sl[1][:, :], 0.0))
    p.wait(TE, "DV", v_rwa)

    wg_free = [("AC", 0)] * 4
    wg_free[0] = ("DV", v_rwa)
    wg_free[1] = ("DV", v_rwa)
    wd_free = [("AC", 0)] * 2
    WGQ = (GP, GP, SY, GP, GP, GP, SC, GP, GP, GP, SY, GP, GP, GP, SC, GP)

    def emit_wg_unit(t):
        kb2 = t % NKB
        cs = t // NKB
        slot = t % 4
        eng = WGQ[t % 16]
        s, v = wg_free[slot]
        p.wait(eng, s, v)
        vld = p.dma(eng, wgpre[slot][:, :], wgt[kb2 * 32 + cs, :, :], f"GWG{slot}")
        p.wait(SC, f"GWG{slot}", vld)
        vr = p.ins(SC, lambda E, sl2=slot, t2=t: E.activation(
            wjunk[:, :], wgpre[sl2][:, :], AF.Abs, accum_out=wtmp[:, t2:t2 + 1]), "AC")
        wg_free[slot] = ("AC", vr)

    def emit_wd_unit(td):
        dslot = td % 2
        s, v = wd_free[dslot]
        p.wait(SC, s, v)
        vldd = 0
        for j in range(4):
            vldd = p.dma(SC, wdpre[dslot][:, j * 512:(j + 1) * 512], wdt[td * 4 + j, :, :], f"GWD{dslot}")
        p.wait(SC, f"GWD{dslot}", vldd)
        p.ins(SC, lambda E, sl2=dslot, t2=td: E.activation(
            wjunk2[:, 0:1024], wdpre[sl2][:, 0:1024], AF.Abs, accum_out=wtmp2[:, t2:t2 + 1]), "AC")
        vrr = p.ins(SC, lambda E, sl2=dslot, t2=td: E.activation(
            wjunk2[:, 0:1024], wdpre[sl2][:, 1024:2048], AF.Abs, accum_out=wtmp2[:, 64 + t2:64 + t2 + 1]), "AC")
        wd_free[dslot] = ("AC", vrr)

    for sl in range(NSL):
        bank = sl % 2
        vmm = None
        for kb in range(NKB):
            u = sl * NKB + kb
            slot = u % 4
            s, v = xr_free[slot]
            p.wait(SY, s, v)
            vld = p.dma(SY, xring[slot][:, :], xt[kb * 32 + sl, :, :], f"XR{slot}")
            j = u % 2
            s, v = xfr_free[j]
            p.wait(VE, s, v)
            p.wait(VE, f"XR{slot}", vld)
            vcv = p.ins(VE, lambda E, a=xfr[j], b=xring[slot]: E.tensor_copy(out=a[:, :], in_=b[:, :]), "DV")
            xr_free[slot] = ("DV", vcv)
            s, v = xsq_free[j]
            p.wait(VE, s, v)
            vsq = p.ins(VE, lambda E, a=xsq[j], b=xfr[j]: E.tensor_mul(a[:, :], b[:, :], b[:, :]), "DV")
            if kb == 0:
                s, v = stats_free[bank]
                p.wait(TE, s, v)
            p.wait(TE, "DV", vcv)
            p.ins(TE, lambda E, b=bank, k=kb, j2=j: E.matmul(
                ps[b][0:16, :], rwa1s[:, k * 16:(k + 1) * 16], xfr[j2][:, :], start=(k == 0), stop=False))
            p.wait(TE, "DV", vsq)
            vmm = p.ins(TE, lambda E, b=bank, k=kb, j2=j: E.matmul(
                ps[b][0:16, :], rwa2s[:, k * 16:(k + 1) * 16], xsq[j2][:, :], start=False, stop=(k == NKB - 1)),
                "PE")
            xfr_free[j] = ("PE", vmm)
            xsq_free[j] = ("PE", vmm)
            emit_wg_unit(u)
            if u % 8 == 0:
                emit_wd_unit(u // 8)
        p.wait(VE, "PE", vmm)
        vev = p.ins(VE, lambda E, d=stats_sl[sl % 2], b=bank: E.tensor_copy(out=d[0:16, :], in_=ps[b][0:16, :]), "DV")
        stats_free[bank] = ("DV", vev)
        for c in range(4):
            p.wait(TE, "DV", vev)
            p.use_bank(2)
            vtr = p.ins(TE, lambda E, d=stats_sl[sl % 2], c2=c: E.transpose(
                ps[2][:, 0:128], d[:, c2 * 128:(c2 + 1) * 128], c_id[:, :]), "PE")
            p.wait(VE, "PE", vtr)
            vst2 = p.ins(VE, lambda E, sl2=sl, c2=c: E.tensor_copy(
                out=statsT[:, (sl2 * 4 + c2) * 16:(sl2 * 4 + c2) * 16 + 16], in_=ps[2][:, 0:16]), "DV")
            p.free_bank(2, "DV", vst2)
    v_statsT = p.ins(VE, lambda E: E.engine_nop(), "DV")

    v_wabs = p.cur("AC")
    p.wait(VE, "AC", v_wabs)
    v_waccg = p.ins(VE, lambda E: E.tensor_reduce(wacc_g[:, :], wtmp[:, :], AX, A.add), "DV")
    v_waccd = p.ins(VE, lambda E: E.tensor_reduce(wacc_d[:, :], wtmp2[:, :], AX, A.add), "DV")

    p.wait(TE, "DV", v_waccg)
    p.use_bank(3)
    vmw = p.ins(TE, lambda E: E.transpose(ps[3][0:1, 0:128], wacc_g[:, 0:1], c_id[:, :]), "PE")
    p.wait(TE, "DV", v_waccd)
    p.use_bank(2)
    vmw2 = p.ins(TE, lambda E: E.transpose(ps[2][0:1, 0:128], wacc_d[:, 0:1], c_id[:, :]), "PE")
    p.wait(VE, "PE", vmw2)
    p.ins(VE, lambda E: E.reduce_sum(wrow[:1, 0:1], ps[3][0:1, 0:128], AX))
    vwd0 = p.ins(VE, lambda E: E.reduce_sum(wrow[:1, 1:2], ps[2][0:1, 0:128], AX), "DV")
    p.free_bank(2, "DV", vwd0)
    p.ins(VE, lambda E: E.tensor_scalar(wrow[:1, 0:1], wrow[:1, 0:1], 1.0 / NWG, 1e-5, A.mult, A.max))
    p.ins(VE, lambda E: E.tensor_scalar(wrow[:1, 1:2], wrow[:1, 1:2], 1.0 / NWD, 1e-5, A.mult, A.max))
    vwr = p.ins(VE, lambda E: E.reciprocal(wrow[:1, 2:4], wrow[:1, 0:2]), "DV")
    p.wait(TE, "DV", vwr)
    p.use_bank(3)
    vbw = p.ins(TE, lambda E: E.matmul(ps[3][:, 0:16], ones_row[:1, :], wrow[:1, 0:16], start=True, stop=True), "PE")
    p.wait(VE, "PE", vbw)
    p.ins(VE, lambda E: E.tensor_copy(out=swg[:, 1:2], in_=ps[3][:, 0:1]))
    p.ins(VE, lambda E: E.tensor_copy(out=swd[:, 1:2], in_=ps[3][:, 1:2]))
    p.ins(VE, lambda E: E.tensor_copy(out=swg[:, 0:1], in_=ps[3][:, 2:3]))
    v_sw = p.ins(VE, lambda E: E.tensor_copy(out=swd[:, 0:1], in_=ps[3][:, 3:4]), "DV")
    p.free_bank(3, "DV", v_sw)

    def qv(q):
        return statsT[:, :].rearrange("p (c q) -> p c q", q=16)[:, :, q:q + 1]

    p.wait(VE, "DV", v_statsT)
    p.use_bank(3)
    for kb in range(NKB):
        vcs = p.ins(TE, lambda E, k=kb: E.matmul(
            ps[3][0:16, 32:48], rwa1s[:, k * 16:(k + 1) * 16], ones_colr[:, :],
            start=(k == 0), stop=(k == NKB - 1)), "PE" if kb == NKB - 1 else None)
    p.wait(VE, "PE", vcs)
    vcse = p.ins(VE, lambda E: E.tensor_copy(out=colasm[0:16, 0:1], in_=ps[3][0:16, 32:33]), "DV")
    p.free_bank(3, "DV", vcse)
    p.wait(TE, "DV", vcse)
    p.use_bank(2)
    vcst = p.ins(TE, lambda E: E.transpose(ps[2][0:1, 0:128], colasm[:, 0:1], c_id[:, :]), "PE")
    p.wait(VE, "PE", vcst)
    vrn = p.ins(VE, lambda E: E.tensor_copy(out=rown[:1, :], in_=ps[2][0:1, 0:128]), "DV")
    p.free_bank(2, "DV", vrn)
    vneg = p.ins(VE, lambda E: E.tensor_scalar(rown[:1, 0:16], rown[:1, 0:16], -INV_H, None, A.mult), "DV")
    p.wait(TE, "DV", vneg)
    p.use_bank(3)
    p.ins(TE, lambda E: E.matmul(ps[3][:, 8:24], ones_row[:1, :], rown[:1, 0:16], start=True, stop=True))
    vb2 = p.ins(TE, lambda E: E.matmul(ps[3][:, 24:32], ones_row[:1, :], rbr[:1, :], start=True, stop=True), "PE")
    p.wait(VE, "PE", vb2)
    p.ins(VE, lambda E: E.tensor_copy(out=negcc[:, :], in_=ps[3][:, 8:24]))
    vrbc = p.ins(VE, lambda E: E.tensor_copy(out=rbcols[:, :], in_=ps[3][:, 24:32]), "DV")
    p.free_bank(3, "DV", vrbc)
    p.ins(VE, lambda E: E.tensor_scalar(v3(tmpA[:, :]), qv(8), INV_H, None, A.mult))
    p.ins(VE, lambda E: E.tensor_scalar(v3(tmpB[:, :]), qv(9), INV_H, None, A.mult))
    p.ins(VE, lambda E: E.tensor_mul(tmpC[:, :], tmpA[:, :], tmpA[:, :]))
    p.ins(VE, lambda E: E.tensor_tensor(tmpB[:, :], tmpB[:, :], tmpC[:, :], A.subtract))
    vv = p.ins(VE, lambda E: E.engine_nop(), "DV")
    p.wait(SC, "DV", vv)
    va = p.ins(SC, lambda E: E.activation(tmpC[:, :], tmpB[:, :], AF.Sqrt, bias=epscol[:, 0:1]), "AC")
    p.wait(VE, "AC", va)
    p.ins(VE, lambda E: E.reciprocal(tk_r[:, :], tmpC[:, :]))
    p.ins(VE, lambda E: E.tensor_mul(tk_mr[:, :], tmpA[:, :], tk_r[:, :]))
    p.ins(VE, lambda E: E.tensor_mul(tmpD[:, :], tk_r[:, :], tk_r[:, :]))
    p.ins(VE, lambda E: E.tensor_mul(tmpD[:, :], tmpB[:, :], tmpD[:, :]))
    vq = p.ins(VE, lambda E: E.engine_nop(), "DV")
    p.wait(SC, "DV", vq)
    va2 = p.ins(SC, lambda E: E.activation(tmpD[:, :], tmpD[:, :], AF.Sqrt, bias=epscol[:, 0:1]), "AC")
    p.wait(VE, "AC", va2)
    p.ins(VE, lambda E: E.reciprocal(tk_rs[:, :], tmpD[:, :]))
    for e in range(NE):
        sl8 = lambda e2: sm8[:, e2 * 128:(e2 + 1) * 128]
        p.ins(VE, lambda E, e2=e: E.scalar_tensor_tensor(
            v3(sm8[:, e2 * 128:(e2 + 1) * 128]), qv(8), negcc[:, e2:e2 + 1], qv(e2), A.mult, A.add))
        p.ins(VE, lambda E, e2=e: E.tensor_mul(sl8(e2), sl8(e2), tk_r[:, :]))
        p.ins(VE, lambda E, e2=e: E.tensor_scalar(sl8(e2), sl8(e2), rbcols[:, e2:e2 + 1], None, A.add))
        p.ins(VE, lambda E, e2=e: E.tensor_add(sl8(e2), sl8(e2), gmh_sb[:, e2 * 128:(e2 + 1) * 128]))
        p.ins(VE, lambda E, e2=e: E.tensor_add(sl8(e2), sl8(e2), gml_sb[:, e2 * 128:(e2 + 1) * 128]))
    p.ins(VE, lambda E: E.tensor_copy(out=tmpA[:, :], in_=sm8[:, 0:128]))
    for e in range(1, NE):
        p.ins(VE, lambda E, e2=e: E.tensor_max(tmpA[:, :], tmpA[:, :], sm8[:, e2 * 128:(e2 + 1) * 128]))
    for e in range(NE):
        p.ins(VE, lambda E, e2=e: E.tensor_tensor(
            sm8[:, e2 * 128:(e2 + 1) * 128], sm8[:, e2 * 128:(e2 + 1) * 128], tmpA[:, :], A.subtract))
    vsm = p.ins(VE, lambda E: E.engine_nop(), "DV")
    p.wait(SC, "DV", vsm)
    vex = None
    for e in range(NE):
        vex = p.ins(SC, lambda E, e2=e: E.activation(
            sm8[:, e2 * 128:(e2 + 1) * 128], sm8[:, e2 * 128:(e2 + 1) * 128], AF.Exp), "AC")
    p.wait(VE, "AC", vex)
    p.ins(VE, lambda E: E.tensor_copy(out=tmpB[:, :], in_=sm8[:, 0:128]))
    for e in range(1, NE):
        p.ins(VE, lambda E, e2=e: E.tensor_add(tmpB[:, :], tmpB[:, :], sm8[:, e2 * 128:(e2 + 1) * 128]))
    p.ins(VE, lambda E: E.reciprocal(tmpC[:, :], tmpB[:, :]))
    for e in range(NE):
        p.ins(VE, lambda E, e2=e: E.tensor_mul(
            sm8[:, e2 * 128:(e2 + 1) * 128], sm8[:, e2 * 128:(e2 + 1) * 128], tmpC[:, :]))
    v_sm8 = p.ins(VE, lambda E: E.engine_nop(), "DV")

    vsmt_dma = 0
    for e in range(NE):
        p.wait(TE, "DV", v_sm8)
        p.use_bank(2)
        vt = p.ins(TE, lambda E, e2=e: E.transpose(ps[2][:, 0:128], sm8[:, e2 * 128:(e2 + 1) * 128], c_id[:, :]), "PE")
        p.wait(VE, "PE", vt)
        p.wait(VE, "SM", vsmt_dma)
        vte = p.ins(VE, lambda E: E.tensor_copy(out=tmpD[:, :], in_=ps[2][:, 0:128]), "DV")
        p.free_bank(2, "DV", vte)
        p.wait(SY, "DV", vte)
        vsmt_dma = p.dma(SY, d_smt[e:e + 1, :].rearrange("o (c q) -> (o c) q", q=128), tmpD[:, :], "SM", ordered=True)
    p.wait(GP, "SM", vsmt_dma)
    v_smb = p.dma(GP, smb[:, :], d_smt.rearrange("e t -> (e t)").rearrange("(q f) -> q f", q=128), "SG")

    vmk = p.ins(VE, lambda E: E.tensor_copy(out=mxcols[:, 0:16], in_=tk_r[:, 0:16]), "DV")
    v_row_r = row_put(vmk, mxcols[:, 0:16], Q_R)
    vmk = p.ins(VE, lambda E: E.tensor_copy(out=mxcols[:, 0:16], in_=tk_mr[:, 0:16]), "DV")
    v_row_mr = row_put(vmk, mxcols[:, 0:16], Q_MR)
    vmk = p.ins(VE, lambda E: E.tensor_copy(out=mxcols[:, 0:16], in_=tk_rs[:, 0:16]), "DV")
    v_row_rs = row_put(vmk, mxcols[:, 0:16], Q_RS)

    p.ins(GP, lambda E: E.memset(bis_lo[:, :], 0.0))
    p.ins(GP, lambda E: E.memset(bis_hi[:, :], 1.5))
    p.ins(GP, lambda E: E.memset(bis_mid[:, :], 0.75))
    p.ins(GP, lambda E: E.memset(bis_m8[:, :], 0.75))
    p.wait(GP, "SG", v_smb)
    for it in range(N_BISECT):
        p.ins(GP, lambda E: E.tensor_single_scalar(bis_junk[:, :], smb[:, :], bis_mid[:, 0:1], A.is_ge))
        for w in (512, 256, 128, 64, 32, 16, 8, 4, 2, 1):
            p.ins(GP, lambda E, w2=w: E.tensor_add(bis_junk[:, 0:w2], bis_junk[:, 0:w2], bis_junk[:, w2:2 * w2]))
        p.ins(GP, lambda E: E.tensor_copy(out=bis_cnt[:, 0:1], in_=bis_junk[:, 0:1]))
        v1 = p.dma(GP, d_small[0:1, 0:128], bis_cnt[:, 0:1], "SG")
        p.wait(GP, "SG", v1)
        v2 = p.dma(GP, bis_c16[0:8, :], d_small[0:1, 0:128].rearrange("o (e j) -> (o e) j", j=16), "SG")
        p.wait(GP, "SG", v2)
        for w in (8, 4, 2, 1):
            p.ins(GP, lambda E, w2=w: E.tensor_add(bis_c16[0:8, 0:w2], bis_c16[0:8, 0:w2], bis_c16[0:8, w2:2 * w2]))
        p.ins(GP, lambda E: E.tensor_copy(out=bis_t1[0:8, 0:1], in_=bis_c16[0:8, 0:1]))
        p.ins(GP, lambda E: E.tensor_scalar(bis_sel[0:8, 0:1], bis_t1[0:8, 0:1], float(KCAP), None, A.is_ge))
        p.ins(GP, lambda E: E.tensor_tensor(bis_t1[0:8, 0:1], bis_m8[0:8, 0:1], bis_lo[0:8, 0:1], A.subtract))
        p.ins(GP, lambda E: E.tensor_mul(bis_t1[0:8, 0:1], bis_t1[0:8, 0:1], bis_sel[0:8, 0:1]))
        p.ins(GP, lambda E: E.tensor_add(bis_lo[0:8, 0:1], bis_lo[0:8, 0:1], bis_t1[0:8, 0:1]))
        p.ins(GP, lambda E: E.tensor_scalar(bis_sel[0:8, 0:1], bis_sel[0:8, 0:1], -1.0, 1.0, A.mult, A.add))
        p.ins(GP, lambda E: E.tensor_tensor(bis_t2[0:8, 0:1], bis_m8[0:8, 0:1], bis_hi[0:8, 0:1], A.subtract))
        p.ins(GP, lambda E: E.tensor_mul(bis_t2[0:8, 0:1], bis_t2[0:8, 0:1], bis_sel[0:8, 0:1]))
        p.ins(GP, lambda E: E.tensor_add(bis_hi[0:8, 0:1], bis_hi[0:8, 0:1], bis_t2[0:8, 0:1]))
        p.ins(GP, lambda E: E.tensor_tensor(bis_t1[0:8, 0:1], bis_lo[0:8, 0:1], bis_hi[0:8, 0:1], A.add))
        p.ins(GP, lambda E: E.tensor_scalar(bis_t1[0:8, 0:1], bis_t1[0:8, 0:1], 0.5, None, A.mult))
        p.ins(GP, lambda E: E.tensor_copy(out=bis_m8[0:8, 0:1], in_=bis_t1[0:8, 0:1]))
        v3_ = p.dma(GP, d_small[1:2, 0:8], bis_t1[0:8, 0:1], "SG")
        p.wait(GP, "SG", v3_)
        vrp = None
        for jj in range(16):
            vrp = p.dma(GP, d_trep[:, jj:jj + 1],
                        d_small[1:2, 0:8].rearrange("o e -> (o e)").rearrange("(e o) -> e o", o=1), "SG", slow=True)
        p.wait(GP, "SG", vrp)
        vm = p.dma(GP, bis_mid[:, 0:1], d_trep.rearrange("e j -> (e j)").rearrange("(q o) -> q o", o=1), "SG")
        p.wait(GP, "SG", vm)
    v_th = p.dma(GP, d_small[3:4, 0:8], bis_lo[0:8, 0:1], "SG")
    v_bis = p.ins(GP, lambda E: E.engine_nop(), "PL")

    p.sync_phase(sems=("SW", "DV", "AC", "PE"))
    rsc.close()

    xsc = ExitStack()
    xb_ = lambda name, shape, dtyp: xsc.enter_context(nc.sbuf_tensor(name, shape, dtyp))
    xr_sl = xb_("xr_sl", [128, NKB * 512], F32)
    xn_f = xb_("xn_f", [128, 512], F32)
    xnb = [xb_(f"xnb{i}", [128, 512], BF16) for i in range(2)]
    sqh = xb_("sqh", [128, 512], F32)
    silx = [xb_(f"silx{i}", [128, 512], F32) for i in range(2)]
    bcx = [xb_(f"bcx{i}", [128, 512], F32) for i in range(3)]
    i32t = [xb_(f"i32t{i}", [128, 512], F32) for i in range(2)]
    prex = [xb_(f"prex{i}", [128, 512], F32) for i in range(2)]

    xnb_free = [("XB0", 0), ("XB1", 0)]
    i32_free = [("DV", 0)] * 2
    prex_free = [("AC", 0)] * 2
    silx_free = [("DV", 0)] * 2
    v_row_sq = 0
    for sl in range(4):
        vr1 = bc_row(Q_R, sl * 512, 512, bcx[0], 0, v_row_r, 0)
        vr2 = bc_row(Q_MR, sl * 512, 512, bcx[1], 0, v_row_mr, 1)
        vr3 = bc_row(Q_RS, sl * 512, 512, bcx[2], 0, v_row_rs, 3)
        p.wait(VE, "DV", max(vr1, vr2, vr3))
        p.ins(VE, lambda E: E.memset(sqh[:, :], 0.0))
        for kb in range(NKB):
            u = sl * NKB + kb
            slot = u % 4
            s, v = xr_free[slot]
            p.wait(SY, s, v)
            vld = p.dma(SY, xring[slot][:, :], xt[kb * 32 + sl, :, :], f"XR{slot}")
            p.wait(VE, f"XR{slot}", vld)
            p.ins(VE, lambda E, sl2=slot: E.tensor_mul(xn_f[:, :], xring[sl2][:, :], bcx[0][:, :]))
            vxn = p.ins(VE, lambda E: E.tensor_tensor(xn_f[:, :], xn_f[:, :], bcx[1][:, :], A.subtract), "DV")
            xr_free[slot] = ("DV", vxn)
            jb = u % 2
            s, v = xnb_free[jb]
            p.wait(VE, s, v)
            p.ins(VE, lambda E, j2=jb: E.tensor_copy(out=xnb[j2][:, :], in_=xn_f[:, :]))
            vxr = p.ins(VE, lambda E, k=kb: E.scalar_tensor_tensor(
                xr_sl[:, k * 512:(k + 1) * 512], xn_f[:, :], ggcol[:, k:k + 1], bcx[2][:, :], A.mult, A.mult), "DV")
            p.wait(SC, "DV", vxr)
            vsp = p.dma(SC, d_xnt[kb * 4 + sl, :, :], xnb[jb][:, :], f"XB{jb}")
            xnb_free[jb] = (f"XB{jb}", vsp)
            jx = u % 2
            s, v = silx_free[jx]
            p.wait(SC, s, v)
            vsq2 = p.ins(SC, lambda E, k=kb, j2=jx: E.activation(
                silx[j2][:, :], xr_sl[:, k * 512:(k + 1) * 512], AF.Square), "AC")
            p.wait(VE, "AC", vsq2)
            vfo = p.ins(VE, lambda E, j2=jx: E.tensor_max(sqh[:, :], sqh[:, :], silx[j2][:, :]), "DV")
            silx_free[jx] = ("DV", vfo)
        vfold = p.cur("DV")
        for c in range(4):
            p.wait(TE, "DV", vfold)
            p.use_bank(2)
            vt = p.ins(TE, lambda E, c2=c: E.transpose(ps[2][:, 0:128], sqh[:, c2 * 128:(c2 + 1) * 128], c_id[:, :]), "PE")
            p.wait(VE, "PE", vt)
            vrm = p.ins(VE, lambda E, s2=sl, c2=c: E.reduce_max(mxcols[:, s2 * 4 + c2:s2 * 4 + c2 + 1], ps[2][:, 0:128], AX), "DV")
            p.free_bank(2, "DV", vrm)
        p.wait(SC, "DV", p.cur("DV"))
        vsc = p.ins(SC, lambda E, s2=sl: E.activation(
            mxcols[:, s2 * 4:s2 * 4 + 4], mxcols[:, s2 * 4:s2 * 4 + 4], AF.Sqrt), "AC")
        p.wait(VE, "AC", vsc)
        p.ins(VE, lambda E, s2=sl: E.tensor_scalar(
            stat_cols[:, s2 * 4:s2 * 4 + 4], mxcols[:, s2 * 4:s2 * 4 + 4], 1e-5, None, A.max))
        p.ins(VE, lambda E, s2=sl: E.reciprocal(mxcols[:, s2 * 4:s2 * 4 + 4], stat_cols[:, s2 * 4:s2 * 4 + 4]))
        vsq_c = p.ins(VE, lambda E, s2=sl: E.tensor_scalar(
            mxcols[:, s2 * 4:s2 * 4 + 4], mxcols[:, s2 * 4:s2 * 4 + 4], 127.0, None, A.mult), "DV")
        v_row_sq = row_put(vsq_c, mxcols[:, sl * 4:sl * 4 + 4], Q_SQ, col0=sl * 4, ncols=4)
        vbq = bc_row(Q_SQ, sl * 512, 512, bcx[0], 0, v_row_sq, 0)
        p.wait(VE, "DV", vbq)
        for kb in range(NKB):
            ji = kb % 2
            s, v = prex_free[ji]
            p.wait(VE, s, v)
            vpre = p.ins(VE, lambda E, k=kb, j2=ji: E.tensor_mul(
                prex[j2][:, :], xr_sl[:, k * 512:(k + 1) * 512], bcx[0][:, :]), "DV")
            s, v = i32_free[ji]
            p.wait(SC, s, v)
            p.wait(SC, "DV", vpre)
            vcv = p.ins(SC, lambda E, j2=ji: E.activation(i32t[j2][:, :], prex[j2][:, :], AF.Identity, bias=mqcol[:, 0:1]), "AC")
            prex_free[ji] = ("AC", vcv)
            p.wait(VE, "AC", vcv)
            vun = p.ins(VE, lambda E, s2=sl, k=kb, j2=ji: E.tensor_scalar(
                big1[:, k * OWN + s2 * 512:k * OWN + s2 * 512 + 512], i32t[j2][:, :], mqcol[:, 0:1], None, A.subtract), "DV")
            i32_free[ji] = ("DV", vun)
    p.wait(VE, "DV", v_sw)
    vsc1c = p.ins(VE, lambda E: E.tensor_scalar(
        mxcols[:, 0:16], stat_cols[:, 0:16], swg[:, 1:2], 1.0 / 127.0, A.mult, A.mult), "DV")
    v_row_sc1 = row_put(vsc1c, mxcols[:, 0:16], Q_SC1)
    v_xq = p.cur("DV")
    if os.environ.get("KDEBUG_A1"):
        p.wait(SY, "DV", v_xq)
        p.dma(SY, d_dbg1, big1[:, :], "SW")

    p.sync_phase(sems=("SW", "SA", "DV", "AC", "PE"))
    xsc.close()
    rxs.close()

    a1 = ExitStack()
    ab_ = lambda name, shape, dtyp: a1.enter_context(nc.sbuf_tensor(name, shape, dtyp))
    wstage = [ab_(f"wstage{i}", [128, 512], F32) for i in range(3)]
    i8a = [ab_(f"i8a{i}", [128, 512], F32) for i in range(2)]
    clipa = ab_("clipa", [128, 512], F32)
    wq_g = [ab_(f"wq_g{i}", [128, NKB * 512], BF16) for i in range(2)]
    wq_v = [ab_(f"wq_v{i}", [128, NKB * 512], BF16) for i in range(2)]
    h2ring = [ab_(f"h2ring{i}", [128, 512], F32) for i in range(4)]
    accR2 = ab_("accR2", [128, OWN], F32)
    accM2 = ab_("accM2", [128, OWN], F32)
    sc1bc = ab_("sc1bc", [128, OWN], F32)
    gsc_t = [ab_(f"gsc_t{i}", [128, 512], F32) for i in range(2)]
    vsc_t = [ab_(f"vsc_t{i}", [128, 512], F32) for i in range(2)]
    sq_t = [ab_(f"sq_t{i}", [128, 512], F32) for i in range(2)]

    p.ins(VE, lambda E: E.memset(accR2[:, :], 0.0))
    p.ins(VE, lambda E: E.memset(accM2[:, :], 0.0))
    for ts in range(4):
        bc_row(Q_SC1, ts * 512, 512, sc1bc, ts * 512, v_row_sc1, ts % 2)
    v_a1pre = p.ins(VE, lambda E: E.engine_nop(), "DV")
    p.wait(TE, "DV", v_a1pre)

    wstage_free = [("AC", 0)] * 3
    i8a_free = [("DV", 0)] * 2
    panel_free = [("PE", 0)] * 2
    set_free = [("DV", 0)] * 2
    h2_free = [[(f"HS{i}", 0)] for i in range(4)]
    gsc_free = [("DV", 0)] * 2
    vsc_free = [("DV", 0)] * 2
    sq_free = [("DV", 0)] * 2

    def quant_w_subtile(dst, dst_off, src_ap, sw_tile, u):
        slot = u % 3
        s, v = wstage_free[slot]
        p.wait(SY, s, v)
        vld = p.dma(SY, wstage[slot][:, :], src_ap, f"WS{slot}")
        ji = u % 2
        s, v = i8a_free[ji]
        p.wait(SC, s, v)
        p.wait(SC, f"WS{slot}", vld)
        vq = p.ins(SC, lambda E, sl2=slot, j2=ji, sw2=sw_tile: E.activation(
            i8a[j2][:, :], wstage[sl2][:, :], AF.Identity, scale=sw2[:, 0:1], bias=mqcol[:, 0:1]), "AC")
        wstage_free[slot] = ("AC", vq)
        p.wait(VE, "AC", vq)
        p.ins(VE, lambda E, j2=ji: E.tensor_scalar(clipa[:, :], i8a[j2][:, :], mqcol[:, 0:1], None, A.subtract))
        vw = p.ins(VE, lambda E, d=dst, o=dst_off: E.tensor_scalar(
            d[:, o:o + 512], clipa[:, :], -1.0, 1.0, A.max, A.min), "DV")
        i8a_free[ji] = ("DV", vw)
        return vw

    NPAIR = 16
    panel_pe = {}
    vwq_done = {}

    def emit_quant_unit(pb2, u2):
        buf2 = pb2 % 2
        if u2 == 0:
            p.wait(VE, "PE", panel_pe.get(pb2 - 2, 0))
        if u2 < NKB:
            quant_w_subtile(wq_g[buf2], u2 * 512, wgt[u2 * 32 + pb2, :, :],
                            swg, pb2 * 32 + u2)
        else:
            kb2 = u2 - NKB
            quant_w_subtile(wq_v[buf2], kb2 * 512, wgt[kb2 * 32 + 16 + pb2, :, :],
                            swg, pb2 * 32 + u2)

    for u in range(32):
        emit_quant_unit(0, u)
    vwq_done[0] = p.cur("DV")
    for pb in range(NPAIR):
        buf = pb % 2
        p.wait(TE, "DV", vwq_done[pb])
        if pb + 1 < NPAIR:
            for q in range(8):
                emit_quant_unit(pb + 1, q)
        vlast_mm = 0
        for ts in range(4):
            for half in range(2):
                hs = half * 2
                sf, vf = set_free[half]
                p.wait(TE, sf, vf)
                for gv in range(2):
                    wq = wq_g[buf] if gv == 0 else wq_v[buf]
                    for obh in range(2):
                        ob = hs + obh
                        bank = half * 4 + gv * 2 + obh
                        for kb in range(NKB):
                            vlast_mm = p.ins(TE, lambda E, b=bank, w=wq, k=kb, o=ob, t2=ts: E.matmul(
                                ps[b][:, :], w[:, k * 512 + o * 128:k * 512 + o * 128 + 128],
                                big1[:, k * OWN + t2 * 512:k * OWN + t2 * 512 + 512],
                                start=(k == 0), stop=(k == NKB - 1)),
                                "PE" if (kb == NKB - 1) else None)
                vlastdv = 0
                for obh in range(2):
                    ob = hs + obh
                    o_abs = pb * 4 + ob
                    bg_ = half * 4 + 0 * 2 + obh
                    bv_ = half * 4 + 1 * 2 + obh
                    jp = (ts * 2 + obh) % 2
                    p.wait(VE, "PE", vlast_mm)
                    s, v = gsc_free[jp]
                    p.wait(VE, s, v)
                    p.ins(VE, lambda E, j2=jp, b=bg_, t2=ts: E.tensor_mul(
                        gsc_t[j2][:, :], ps[b][:, :], sc1bc[:, t2 * 512:(t2 + 1) * 512]))
                    s, v = vsc_free[jp]
                    p.wait(VE, s, v)
                    vvs = p.ins(VE, lambda E, j2=jp, b=bv_, t2=ts: E.tensor_mul(
                        vsc_t[j2][:, :], ps[b][:, :], sc1bc[:, t2 * 512:(t2 + 1) * 512]), "DV")
                    p.wait(SC, "DV", vvs)
                    vsl = p.ins(SC, lambda E, j2=jp: E.activation(
                        gsc_t[j2][:, :], gsc_t[j2][:, :], AF.Silu), "AC")
                    hslot = (pb * 8 + ts * 2 + obh) % 4
                    for s, v in h2_free[hslot]:
                        p.wait(VE, s, v)
                    p.wait(VE, "AC", vsl)
                    vh2 = p.ins(VE, lambda E, j2=jp, h=hslot: E.tensor_mul(
                        h2ring[h][:, :], gsc_t[j2][:, :], vsc_t[j2][:, :]), "DV")
                    gsc_free[jp] = ("DV", vh2)
                    vsc_free[jp] = ("DV", vh2)
                    p.wait(SC, "DV", vh2)
                    vspl = p.dma(SC, d_h2[o_abs * 4 + ts, :, :],
                                 h2ring[hslot][:, :], f"HS{hslot}")
                    jq = jp
                    s, v = sq_free[jq]
                    p.wait(SC, s, v)
                    vsq3 = p.ins(SC, lambda E, j2=jq, h=hslot: E.activation(
                        sq_t[j2][:, :], h2ring[h][:, :], AF.Square), "AC")
                    p.wait(VE, "AC", vsq3)
                    p.ins(VE, lambda E, j2=jq, t2=ts: E.tensor_add(
                        accR2[:, t2 * 512:(t2 + 1) * 512], accR2[:, t2 * 512:(t2 + 1) * 512], sq_t[j2][:, :]))
                    vlastdv = p.ins(VE, lambda E, j2=jq, t2=ts, oa=o_abs: E.scalar_tensor_tensor(
                        accM2[:, t2 * 512:(t2 + 1) * 512], sq_t[j2][:, :], gd2col[:, oa:oa + 1],
                        accM2[:, t2 * 512:(t2 + 1) * 512], A.mult, A.max), "DV")
                    sq_free[jq] = ("DV", vlastdv)
                    h2_free[hslot] = [(f"HS{hslot}", vspl), ("AC", vsq3)]
                set_free[half] = ("DV", vlastdv)
                if pb + 1 < NPAIR:
                    g = ts * 2 + half
                    if g < 6:
                        for q in range(4):
                            emit_quant_unit(pb + 1, 8 + g * 4 + q)
        panel_pe[pb] = vlast_mm
        vwq_done[pb + 1] = p.cur("DV")
    v_a1 = p.cur("DV")

    for c in range(16):
        p.wait(TE, "DV", v_a1)
        p.use_bank(2)
        vt1 = p.ins(TE, lambda E, c2=c: E.transpose(ps[2][:, 0:128], accR2[:, c2 * 128:(c2 + 1) * 128], c_id[:, :]), "PE")
        p.use_bank(3)
        vt2 = p.ins(TE, lambda E, c2=c: E.transpose(ps[3][:, 0:128], accM2[:, c2 * 128:(c2 + 1) * 128], c_id[:, :]), "PE")
        p.wait(VE, "PE", vt1)
        vrd1 = p.ins(VE, lambda E, c2=c: E.reduce_sum(stat_cols[:, c2:c2 + 1], ps[2][:, 0:128], AX), "DV")
        p.free_bank(2, "DV", vrd1)
        p.wait(VE, "PE", vt2)
        vrd2 = p.ins(VE, lambda E, c2=c: E.reduce_max(stat_cols[:, 16 + c2:17 + c2], ps[3][:, 0:128], AX), "DV")
        p.free_bank(3, "DV", vrd2)
    vst = p.ins(VE, lambda E: E.tensor_scalar(stat_cols[:, 0:16], stat_cols[:, 0:16], 1.0 / DHALF, None, A.mult), "DV")
    p.wait(SC, "DV", vst)
    vsr2 = p.ins(SC, lambda E: E.activation(stat_cols[:, 0:16], stat_cols[:, 0:16], AF.Sqrt, bias=epscol[:, 0:1]), "AC")
    p.wait(VE, "AC", vsr2)
    p.ins(VE, lambda E: E.reciprocal(stat_cols[:, 0:16], stat_cols[:, 0:16]))
    vms = p.ins(VE, lambda E: E.engine_nop(), "DV")
    p.wait(SC, "DV", vms)
    vsm2 = p.ins(SC, lambda E: E.activation(stat_cols[:, 16:32], stat_cols[:, 16:32], AF.Sqrt), "AC")
    p.wait(VE, "AC", vsm2)
    p.ins(VE, lambda E: E.tensor_mul(stat_cols[:, 16:32], stat_cols[:, 16:32], stat_cols[:, 0:16]))
    p.ins(VE, lambda E: E.tensor_scalar(stat_cols[:, 16:32], stat_cols[:, 16:32], 1e-5, None, A.max))
    p.ins(VE, lambda E: E.reciprocal(stat_cols[:, 32:48], stat_cols[:, 16:32]))
    vf2c = p.ins(VE, lambda E: E.tensor_scalar(
        mxcols[:, 0:16], stat_cols[:, 32:48], 127.0, None, A.mult), "DV")
    p.ins(VE, lambda E: E.tensor_mul(mxcols[:, 0:16], mxcols[:, 0:16], stat_cols[:, 0:16]))
    vf2c = p.ins(VE, lambda E: E.engine_nop(), "DV")
    v_row_f2 = row_put(vf2c, mxcols[:, 0:16], Q_F2)
    vp2c = p.ins(VE, lambda E: E.tensor_scalar(
        mxcols[:, 0:16], stat_cols[:, 16:32], swd[:, 1:2], 1.0 / 127.0, A.mult, A.mult), "DV")
    v_row_p2 = row_put(vp2c, mxcols[:, 0:16], Q_P2)

    p.sync_phase(sems=("SW", "SA", "DV", "AC", "PE"))
    a1.close()

    a2 = ExitStack()
    ab2 = lambda name, shape, dtyp: a2.enter_context(nc.sbuf_tensor(name, shape, dtyp))
    wd_st = [ab2(f"wd_st{i}", [128, 512], F32) for i in range(2)]
    wd_q = [ab2(f"wd_q{i}", [128, 512], BF16) for i in range(2)]
    i8b = [ab2(f"i8b{i}", [128, 512], F32) for i in range(2)]
    clipb = ab2("clipb", [128, 512], F32)
    h2f = [ab2(f"h2f{i}", [128, 1024], F32) for i in range(4)]
    h2qs = [ab2(f"h2qs{i}", [128, 1024], BF16) for i in range(4)]
    h2qr = [ab2(f"h2qr{i}", [128, 512], BF16) for i in range(4)]
    i32b = [ab2(f"i32b{i}", [128, 1024], F32) for i in range(4)]
    preb = [ab2(f"preb{i}", [128, 1024], F32) for i in range(4)]
    bc3g = ab2("bc3g", [128, 2048], F32)
    bc4g = ab2("bc4g", [128, 2048], F32)

    for t4 in range(4):
        bc_row(Q_F2, t4 * 512, 512, bc3g, t4 * 512, v_row_f2, 0)
        bc_row(Q_P2, t4 * 512, 512, bc4g, t4 * 512, v_row_p2, 1)
    v_a2bc = p.cur("DV")
    p.wait(TE, "DV", v_a2bc)

    wdst_free = [("AC", 0)] * 2
    wdq_free = [("PE", 0)] * 2
    i8b_free = [("DV", 0)] * 2
    h2f_free = [("DV", 0)] * 4
    h2qs_free = [(f"HQ{i}", 0) for i in range(4)]
    h2qr_free = [("PE", 0)] * 4
    i32b_free = [("DV", 0)] * 4
    preb_free = [("AC", 0)] * 4
    s_evac_done = 0
    for st_ in range(4):
        if st_ == 1:
            for eng in (GP, SC):
                for hq in range(4):
                    p.wait(eng, f"HQ{hq}", p.cur(f"HQ{hq}"))
        for th in range(2):
            vlast_mm = 0
            for kb in range(NDKB):
                wslot = kb % 2
                s, v = wdst_free[wslot]
                p.wait(SY, s, v)
                vld = p.dma(SY, wd_st[wslot][:, :], wdt[kb * 4 + st_, :, :], f"WD{wslot}")
                s, v = i8b_free[wslot]
                p.wait(SC, s, v)
                p.wait(SC, f"WD{wslot}", vld)
                vq = p.ins(SC, lambda E, w=wslot: E.activation(
                    i8b[w][:, :], wd_st[w][:, :], AF.Identity, scale=swd[:, 0:1], bias=mqcol[:, 0:1]), "AC")
                wdst_free[wslot] = ("AC", vq)
                s, v = wdq_free[wslot]
                p.wait(VE, s, v)
                p.wait(VE, "AC", vq)
                p.ins(VE, lambda E, w=wslot: E.tensor_scalar(clipb[:, :], i8b[w][:, :], mqcol[:, 0:1], None, A.subtract))
                vqd = p.ins(VE, lambda E, w=wslot: E.tensor_scalar(wd_q[w][:, :], clipb[:, :], -1.0, 1.0, A.max, A.min), "DV")
                i8b_free[wslot] = ("DV", vqd)
                p.wait(TE, "DV", vqd)
                if st_ == 0:
                    hslot = kb % 4
                    heng = GP if hslot % 2 == 0 else SC
                    s, v = h2f_free[hslot]
                    p.wait(heng, s, v)
                    for j in range(2):
                        vh = p.dma(heng, h2f[hslot][:, j * 512:(j + 1) * 512],
                                   d_h2[kb * 4 + th * 2 + j, :, :], f"GH2{hslot}")
                    s, v = preb_free[hslot]
                    p.wait(VE, s, v)
                    p.wait(VE, f"GH2{hslot}", vh)
                    vpre = p.ins(VE, lambda E, h=hslot, k=kb, t2=th: E.scalar_tensor_tensor(
                        preb[h][:, :], h2f[h][:, :], gdcol[:, k:k + 1], bc3g[:, t2 * 1024:(t2 + 1) * 1024],
                        A.mult, A.mult), "DV")
                    h2f_free[hslot] = ("DV", vpre)
                    s, v = i32b_free[hslot]
                    p.wait(SC, s, v)
                    p.wait(SC, "DV", vpre)
                    vcv = p.ins(SC, lambda E, h=hslot: E.activation(
                        i32b[h][:, :], preb[h][:, :], AF.Identity, bias=mqcol[:, 0:1]), "AC")
                    preb_free[hslot] = ("AC", vcv)
                    s, v = h2qs_free[hslot]
                    p.wait(VE, s, v)
                    p.wait(VE, "AC", vcv)
                    vqq = p.ins(VE, lambda E, h=hslot: E.tensor_scalar(
                        h2qs[h][:, :], i32b[h][:, :], mqcol[:, 0:1], None, A.subtract), "DV")
                    i32b_free[hslot] = ("DV", vqq)
                    p.wait(SC, "DV", vqq)
                    for j in range(2):
                        vsp = p.dma(SC, d_h2q[kb * 4 + th * 2 + j, :, :],
                                    h2qs[hslot][:, j * 512:(j + 1) * 512], f"HQ{hslot}")
                    h2qs_free[hslot] = (f"HQ{hslot}", vsp)
                    p.wait(TE, "DV", vqq)
                    for w2 in range(2):
                        bank = (kb % 1) * 0
                        for hb in range(4):
                            b = hb * 2 + w2
                            vlast_mm = p.ins(TE, lambda E, b2=b, w=wslot, h=hslot, h3=hb, w3=w2, k=kb: E.matmul(
                                ps[b2][:, :], wd_q[w][:, h3 * 128:(h3 + 1) * 128],
                                h2qs[h][:, w3 * 512:(w3 + 1) * 512],
                                start=(k == 0), stop=(k == NDKB - 1)), "PE" if hb == 3 else None)
                else:
                    for w2 in range(2):
                        rslot = (kb * 2 + w2) % 4
                        reng = (GP, GP, SC, GP)[rslot]
                        s, v = h2qr_free[rslot]
                        p.wait(reng, s, v)
                        vh = p.dma(reng, h2qr[rslot][:, :],
                                   d_h2q[kb * 4 + th * 2 + w2, :, :],
                                   f"GHQ{rslot}")
                        p.wait(TE, f"GHQ{rslot}", vh)
                        for hb in range(4):
                            b = hb * 2 + w2
                            vlast_mm = p.ins(TE, lambda E, b2=b, w=wslot, r=rslot, h3=hb, k=kb: E.matmul(
                                ps[b2][:, :], wd_q[w][:, h3 * 128:(h3 + 1) * 128], h2qr[r][:, :],
                                start=(k == 0), stop=(k == NDKB - 1)), "PE" if hb == 3 else None)
                        h2qr_free[rslot] = ("PE", vlast_mm)
                wdq_free[wslot] = ("PE", vlast_mm)
            p.wait(VE, "PE", vlast_mm)
            for w2 in range(2):
                for hb in range(4):
                    b = hb * 2 + w2
                    ho = st_ * 4 + hb
                    s_evac_done = p.ins(VE, lambda E, b2=b, h3=ho, t2=th, w3=w2: E.tensor_mul(
                        big1[:, h3 * OWN + t2 * 1024 + w3 * 512:h3 * OWN + t2 * 1024 + w3 * 512 + 512],
                        ps[b2][:, :], bc4g[:, t2 * 1024 + w3 * 512:t2 * 1024 + w3 * 512 + 512]), "DV")
            p.wait(TE, "DV", s_evac_done)
    v_a2 = p.cur("DV")

    p.sync_phase(sems=("SW", "SG", "SA", "DV", "AC", "PE"))
    a2.close()

    p.wait(SY, "SG", v_th)
    p.wait(SY, "PL", v_bis)
    p.ins(VE, lambda E: E.memset(th_row[:1, :], 0.0))
    p.wait(SY, "DV", p.cur("DV"))
    vthl = p.dma(SY, th_row[:1, 0:8], d_small[3:4, 0:8], "SW")
    p.wait(TE, "SW", vthl)
    p.use_bank(0)
    vthb = p.ins(TE, lambda E: E.matmul(ps[0][:, 0:16], ones_row[:1, :], th_row[:1, 0:16], start=True, stop=True), "PE")
    p.wait(VE, "PE", vthb)
    vthe = p.ins(VE, lambda E: E.tensor_copy(out=th_cols[:, :], in_=ps[0][:, 0:8]), "DV")
    p.free_bank(0, "DV", vthe)
    for e in range(NE):
        p.ins(VE, lambda E, e2=e: E.scalar_tensor_tensor(
            tmpA[:, 0:16], sm8[:, e2 * 128:e2 * 128 + 16], th_cols[:, e2:e2 + 1],
            sm8[:, e2 * 128:e2 * 128 + 16], A.is_ge, A.mult))
        if e == 0:
            p.ins(VE, lambda E: E.tensor_copy(out=mxcols[:, 0:16], in_=tmpA[:, 0:16]))
        else:
            p.ins(VE, lambda E: E.tensor_add(mxcols[:, 0:16], mxcols[:, 0:16], tmpA[:, 0:16]))
    vco = p.ins(VE, lambda E: E.engine_nop(), "DV")
    v_row_co = row_put(vco, mxcols[:, 0:16], Q_CO)

    p.sync_phase()
    es2.close()

    f1 = ExitStack()
    fb_ = lambda name, shape, dtyp: f1.enter_context(nc.sbuf_tensor(name, shape, dtyp))
    xnt_res = fb_("xnt_res", [128, NKB * 2048], BF16)
    fw_st = [fb_(f"fw_st{i}", [128, 128], F32) for i in range(4)]
    fq_g = [fb_(f"fq_g{i}", [128, NKB * 128], BF16) for i in range(2)]
    fq_u = [fb_(f"fq_u{i}", [128, NKB * 128], BF16) for i in range(2)]
    silf = [fb_(f"silf{i}", [128, 512], F32) for i in range(2)]
    ubuf = [fb_(f"ubuf{i}", [128, 512], F32) for i in range(2)]
    midb = [fb_(f"midb{i}", [128, 512], BF16) for i in range(4)]

    p.wait(GP, "XB0", p.cur("XB0"))
    p.wait(GP, "XB1", p.cur("XB1"))
    vxa_kb = []
    for kb in range(NKB):
        for sl in range(4):
            vxa = p.dma(GP, xnt_res[:, kb * 2048 + sl * 512:kb * 2048 + (sl + 1) * 512],
                        d_xnt[kb * 4 + sl, :, :], "GXA")
        vxa_kb.append(vxa)

    fwst_free = [("DV", 0)] * 4
    fbank_free = [("DV", 0)] * 8
    silf_free = [("DV", 0)] * 2
    ubuf_free = [("DV", 0)] * 2
    midb_free = [(f"MB{i}", 0) for i in range(4)]
    f1_spill = {}
    fcvt_done = {}
    fpanel_pe = {}

    def emit_fcvt_unit(pc2, u2):
        buf2 = pc2 % 2
        if u2 == 0:
            p.wait(VE, "PE", fpanel_pe.get(pc2 - 2, 0))
        gu = u2 // NKB
        kb2 = u2 % NKB
        src = fwg if gu == 0 else fwu
        dstp = fq_g[buf2] if gu == 0 else fq_u[buf2]
        slot = (pc2 * 32 + u2) % 4
        s, v = fwst_free[slot]
        p.wait(SY, s, v)
        vld = p.dma(SY, fw_st[slot][:, :], src[kb2 * NFKB + pc2, :, :], f"FW{slot}")
        p.wait(VE, f"FW{slot}", vld)
        vcvt = p.ins(VE, lambda E, sl2=slot, d=dstp, k=kb2: E.tensor_copy(
            out=d[:, k * 128:(k + 1) * 128], in_=fw_st[sl2][:, :]), "DV")
        fwst_free[slot] = ("DV", vcvt)

    for u in range(32):
        emit_fcvt_unit(0, u)
    fcvt_done[0] = p.cur("DV")
    for pc in range(NFKB):
        buf = pc % 2
        p.wait(TE, "DV", fcvt_done[pc])
        if pc + 1 < NFKB:
            for q in range(16):
                emit_fcvt_unit(pc + 1, q)
        for ts in range(4):
            vlast_mm = 0
            for gu in range(2):
                bank = ts * 2 + gu
                s, v = fbank_free[bank]
                p.wait(TE, s, v)
                wqp = fq_g[buf] if gu == 0 else fq_u[buf]
                for kb in range(NKB):
                    if pc == 0 and gu == 0:
                        p.wait(TE, "GXA", vxa_kb[kb])
                    vlast_mm = p.ins(TE, lambda E, b=bank, w=wqp, k=kb, t2=ts: E.matmul(
                        ps[b][:, :], w[:, k * 128:(k + 1) * 128],
                        xnt_res[:, k * 2048 + t2 * 512:k * 2048 + t2 * 512 + 512],
                        start=(k == 0), stop=(k == NKB - 1)),
                        "PE" if kb == NKB - 1 else None)
            jp = ts % 2
            p.wait(SC, "PE", vlast_mm)
            s, v = silf_free[jp]
            p.wait(SC, s, v)
            vslf = p.ins(SC, lambda E, j2=jp, b=ts * 2, f2=pc: E.activation(
                silf[j2][:, :], ps[b][:, :], AF.Silu, bias=bgcol[:, f2:f2 + 1]), "AC")
            s, v = ubuf_free[jp]
            p.wait(SC, s, v)
            vub = p.ins(SC, lambda E, j2=jp, b=ts * 2 + 1, f2=pc: E.activation(
                ubuf[j2][:, :], ps[b][:, :], AF.Identity, bias=bucol[:, f2:f2 + 1]), "AC")
            mslot = ts
            s, v = midb_free[mslot]
            p.wait(VE, s, v)
            p.wait(VE, "AC", vub)
            vmd = p.ins(VE, lambda E, j2=jp, m=mslot: E.tensor_mul(
                midb[m][:, :], silf[j2][:, :], ubuf[j2][:, :]), "DV")
            silf_free[jp] = ("DV", vmd)
            ubuf_free[jp] = ("DV", vmd)
            fbank_free[ts * 2] = ("DV", vmd)
            fbank_free[ts * 2 + 1] = ("DV", vmd)
            p.wait(SC, "DV", vmd)
            vmsp = p.dma(SC, d_mid[pc * 4 + ts, :, :], midb[mslot][:, :], f"MB{mslot}")
            midb_free[mslot] = (f"MB{mslot}", vmsp)
            f1_spill[(pc, ts)] = vmsp
            if pc + 1 < NFKB and ts < 2:
                for q in range(8):
                    emit_fcvt_unit(pc + 1, 16 + ts * 8 + q)
        fpanel_pe[pc] = vlast_mm
        fcvt_done[pc + 1] = p.cur("DV")

    p.sync_phase(sems=("SW", "SG", "SA", "DV", "AC", "PE"))
    f1.close()

    f2 = ExitStack()
    fb2_ = lambda name, shape, dtyp: f2.enter_context(nc.sbuf_tensor(name, shape, dtyp))
    mid_res = fb2_("mid_res", [128, NFKB * 1024], BF16)
    fd_st = [fb2_(f"fd_st{i}", [128, 512], F32) for i in range(2)]
    fd_q = [fb2_(f"fd_q{i}", [128, 512], BF16) for i in range(2)]
    xres = [fb2_(f"xres{i}", [128, 512], F32) for i in range(2)]
    outt = [fb2_(f"outt{i}", [128, 512], F32) for i in range(2)]
    gbuf = [fb2_(f"gbuf{i}", [128, 512], F32) for i in range(2)]
    cobc = fb2_("cobc", [128, 2048], F32)

    for t4 in range(4):
        bc_row(Q_CO, t4 * 512, 512, cobc, t4 * 512, v_row_co, 0)
    v_f2bc = p.cur("DV")
    p.wait(TE, "DV", v_f2bc)

    fdst_free = [("DV", 0)] * 2
    fdq_free = [("PE", 0)] * 2
    xres_free = [("DV", 0)] * 2
    outt_free = [("OUT", 0)] * 2
    gbuf_free = [("DV", 0)] * 2
    for th in range(2):
        if th == 1:
            p.wait(SC, "PE", p.cur("PE"))
        vml_kb = []
        for kb in range(NFKB):
            for j in range(2):
                p.wait(SC, f"MB{th * 2 + j}", f1_spill[(kb, th * 2 + j)])
                vml = p.dma(SC, mid_res[:, kb * 1024 + j * 512:kb * 1024 + (j + 1) * 512],
                            d_mid[kb * 4 + th * 2 + j, :, :], "GMI")
            vml_kb.append(vml)
        for st_ in range(4):
            vlast_mm = 0
            for kb in range(NFKB):
                wslot = kb % 2
                weng = SY if wslot == 0 else GP
                s, v = fdst_free[wslot]
                p.wait(weng, s, v)
                vld = p.dma(weng, fd_st[wslot][:, :], fwd[kb * 4 + st_, :, :], f"FD{wslot}")
                s, v = fdq_free[wslot]
                p.wait(VE, s, v)
                p.wait(VE, f"FD{wslot}", vld)
                vcv = p.ins(VE, lambda E, w=wslot: E.tensor_copy(out=fd_q[w][:, :], in_=fd_st[w][:, :]), "DV")
                fdst_free[wslot] = ("DV", vcv)
                p.wait(TE, "DV", vcv)
                if st_ == 0:
                    p.wait(TE, "GMI", vml_kb[kb])
                for w2 in range(2):
                    for hb in range(4):
                        b = hb * 2 + w2
                        vlast_mm = p.ins(TE, lambda E, b2=b, w=wslot, h3=hb, k=kb, w3=w2: E.matmul(
                            ps[b2][:, :], fd_q[w][:, h3 * 128:(h3 + 1) * 128],
                            mid_res[:, k * 1024 + w3 * 512:k * 1024 + w3 * 512 + 512],
                            start=(k == 0), stop=(k == NFKB - 1)), "PE" if (w2 == 1 and hb == 3) else None)
                fdq_free[wslot] = ("PE", vlast_mm)
            p.wait(SC, "PE", vlast_mm)
            for w2 in range(2):
                for hb in range(4):
                    b = hb * 2 + w2
                    ho = st_ * 4 + hb
                    toff = th * 1024 + w2 * 512
                    jg = (w2 * 4 + hb) % 2
                    xslot = jg
                    s, v = xres_free[xslot]
                    p.wait(GP, s, v)
                    vxl = p.dma(GP, xres[xslot][:, :], xt[ho * 32 + th * 2 + w2, :, :], f"XE{xslot}")
                    s, v = gbuf_free[jg]
                    p.wait(SC, s, v)
                    vg1 = p.ins(SC, lambda E, j2=jg, b2=b, h3=ho: E.activation(
                        gbuf[j2][:, :], ps[b2][:, :], AF.Identity, bias=bdcol[:, h3:h3 + 1]), "AC")
                    p.wait(VE, "AC", vg1)
                    vg2 = p.ins(VE, lambda E, j2=jg, t2=toff: E.tensor_mul(
                        gbuf[j2][:, :], gbuf[j2][:, :], cobc[:, t2:t2 + 512]), "DV")
                    p.wait(SC, "DV", vg2)
                    vg3 = p.ins(SC, lambda E, j2=jg: E.activation(gbuf[j2][:, :], gbuf[j2][:, :], AF.Silu), "AC")
                    oslot = jg
                    s, v = outt_free[oslot]
                    p.wait(VE, s, v)
                    p.wait(VE, "AC", vg3)
                    p.ins(VE, lambda E, j2=jg, o=oslot, h3=ho, t2=toff: E.tensor_mul(
                        outt[o][:, :], gbuf[j2][:, :], big1[:, h3 * OWN + t2:h3 * OWN + t2 + 512]))
                    p.wait(VE, f"XE{xslot}", vxl)
                    vo = p.ins(VE, lambda E, o=oslot, x=xslot: E.tensor_add(
                        outt[o][:, :], outt[o][:, :], xres[x][:, :]), "DV")
                    xres_free[xslot] = ("DV", vo)
                    gbuf_free[jg] = ("DV", vo)
                    p.wait(SC, "DV", vo)
                    vout = p.dma(SC, yt[ho * 128:(ho + 1) * 128, toff:toff + 512], outt[oslot][:, :], f"OT{oslot}")
                    outt_free[oslot] = (f"OT{oslot}", vout)
            p.wait(TE, "OT0", p.cur("OT0"))
            p.wait(TE, "OT1", p.cur("OT1"))

    p.wait(SY, "OT0", p.cur("OT0"))
    p.wait(SY, "OT1", p.cur("OT1"))
    f2.close()

    semstack = ExitStack()
    for name in sorted(set(list(Plan.SEMS) + list(p.cnt.keys()))):
        p.sem[name] = semstack.enter_context(nc.semaphore(name))
    with nc.Block() as block:
        @block.sync
        def _(E):
            for f in p.ops["sync"]:
                f(E)

        @block.vector
        def _(E):
            for f in p.ops["vector"]:
                f(E)

        @block.scalar
        def _(E):
            for f in p.ops["scalar"]:
                f(E)

        @block.tensor
        def _(E):
            for f in p.ops["tensor"]:
                f(E)

        @block.gpsimd
        def _(E):
            for f in p.ops["gpsimd"]:
                f(E)

    semstack.close()
    pse.close()
    es.close()
    return nc


def kernel(L0_attn_x, L0_attn_w_gate, L0_attn_g_gate, L0_attn_w_down, L0_attn_g_down,
           L0_ffn_router_w, L0_ffn_router_b, L0_ffn_w_gate, L0_ffn_b_gate,
           L0_ffn_w_up, L0_ffn_b_up, L0_ffn_w_down, L0_ffn_b_down):
    f32 = np.float32

    def tiles(a, pr, pc_):
        nr = a.shape[0] // 128
        ncol = a.shape[1] // pc_
        return np.ascontiguousarray(
            a.reshape(nr, 128, ncol, pc_).transpose(0, 2, 1, 3).reshape(nr * ncol, 128, pc_))

    x = np.ascontiguousarray(np.asarray(L0_attn_x, f32).reshape(NT, H))
    xT = np.ascontiguousarray(x.T)
    wgT = np.asarray(L0_attn_w_gate, f32).T
    wdT = np.asarray(L0_attn_w_down, f32).T
    wgt_t = tiles(np.ascontiguousarray(wgT), 128, 512)
    wdt_t = tiles(np.ascontiguousarray(wdT), 128, 512)
    fwg = np.zeros((H, DFFP), f32)
    fwg[:, :DFF] = np.asarray(L0_ffn_w_gate, f32)
    fwu = np.zeros((H, DFFP), f32)
    fwu[:, :DFF] = np.asarray(L0_ffn_w_up, f32)
    fwd = np.zeros((DFFP, H), f32)
    fwd[:DFF, :] = np.asarray(L0_ffn_w_down, f32)
    fwg_t = tiles(fwg, 128, 128)
    fwu_t = tiles(fwu, 128, 128)
    fwd_t = tiles(fwd, 128, 512)
    bgp = np.zeros((DFFP,), f32)
    bgp[:DFF] = np.asarray(L0_ffn_b_gate, f32)
    bup = np.zeros((DFFP,), f32)
    bup[:DFF] = np.asarray(L0_ffn_b_up, f32)

    ggate2 = np.ascontiguousarray(np.asarray(L0_attn_g_gate, f32).reshape(NKB, 128).T)
    gdown2 = np.ascontiguousarray(np.asarray(L0_attn_g_down, f32).reshape(NDKB, 128).T)
    bg2 = np.ascontiguousarray(bgp.reshape(NFKB, 128).T)
    bu2 = np.ascontiguousarray(bup.reshape(NFKB, 128).T)
    bd2 = np.ascontiguousarray(np.asarray(L0_ffn_b_down, f32).reshape(NKB, 128).T)
    rb = np.asarray(L0_ffn_router_b, f32).reshape(1, NE)
    rw = np.asarray(L0_ffn_router_w, f32)
    rwa1 = np.zeros((128, NKB * 16), f32)
    rwa2 = np.zeros((128, NKB * 16), f32)
    for kb in range(NKB):
        rwa1[:, kb * 16:kb * 16 + 8] = rw[kb * 128:(kb + 1) * 128, :]
        rwa1[:, kb * 16 + 8] = 1.0
        rwa2[:, kb * 16 + 9] = 1.0
    ident = np.eye(128, dtype=f32)

    gh, gl = _gumbel_tables()

    nc = build_program_synced()
    in_maps = []
    for c in range(NCORE):
        sh = c * OWN
        xTc = np.concatenate([xT[:, sh:], xT[:, :sh]], axis=1)
        xtc_t = tiles(np.ascontiguousarray(xTc), 128, 512)
        ghc = np.concatenate([gh[sh:], gh[:sh]], axis=0)
        glc = np.concatenate([gl[sh:], gl[:sh]], axis=0)
        gmhc = np.ascontiguousarray(ghc.reshape(128, 128, NE).transpose(2, 1, 0))
        gmlc = np.ascontiguousarray(glc.reshape(128, 128, NE).transpose(2, 1, 0))
        in_maps.append({
            "xt": xtc_t, "gmh": gmhc, "gml": gmlc, "wgt": wgt_t, "wdt": wdt_t,
            "fwg": fwg_t, "fwu": fwu_t, "fwd": fwd_t, "ggate2": ggate2, "gdown2": gdown2,
            "rwa1": rwa1, "rwa2": rwa2, "bg2": bg2, "bu2": bu2, "bd2": bd2,
            "rbrow": rb, "identd": ident,
        })
    res = run_bass_kernel_spmd(nc, in_maps, core_ids=list(range(NCORE)))
    out = np.empty((NT, H), f32)
    for c in range(NCORE):
        out[c * OWN:(c + 1) * OWN, :] = res.results[c]["yt"].T
    return out.reshape(4, 4096, 2048)


def build_program_synced():
    rec = {}
    build_program(rec=rec)
    dep = compute_edges(rec)
    return build_program(dep=dep)


if __name__ == "__main__":
    prog = build_program_synced()
    n = sum(1 for _ in prog.all_instructions())
    print("built ok, instructions:", n)


# version 09
# speedup vs baseline: 1.1997x; 1.0132x over previous
import os
import numpy as np
import ml_dtypes
import concourse.bass as bass
import concourse.mybir as mybir
from concourse.bass_utils import run_bass_kernel_spmd
from contextlib import ExitStack

F32 = mybir.dt.float32
F32R = mybir.dt.float32r
BF16 = mybir.dt.bfloat16
I32 = mybir.dt.int32
I8 = mybir.dt.int8
A = mybir.AluOpType
AF = mybir.ActivationFunctionType
AX = mybir.AxisListType.X

NCORE = 8
NT = 16384
OWN = 2048
H = 2048
NKB = 16
DG = 16384
DHALF = 8192
NDKB = 64
DFF = 5461
DFFP = 5632
NFKB = 44
NE = 8
KCAP = 2048
M32 = 12582912.0
M64 = 6755399441055744.0
N_BISECT = 36
EPS_LN = 1e-5
EPS_G = 1e-10
INV_H = 1.0 / 2048.0
NWG = float(DG * H)
NWD = float(DHALF * H)

Q_R, Q_MR, Q_RS, Q_SQ, Q_SC1, Q_F2, Q_P2, Q_CO = range(8)


def _gumbel_tables():
    import jax
    import jax.numpy as jnp
    cpu = jax.devices("cpu")[0]
    fp64 = os.environ.get("BASS_SIM_FP64") == "1"
    with jax.default_device(cpu):
        key = jax.random.key(42, impl="threefry2x32")
        if fp64:
            from jax.experimental import enable_x64
            with enable_x64():
                u = jax.random.uniform(key, (NT, NE), dtype=jnp.float64)
                g = np.asarray(-jnp.log(-jnp.log(u + EPS_G) + EPS_G), dtype=np.float64)
            hi = g.astype(np.float32)
            lo = (g - hi.astype(np.float64)).astype(np.float32)
        else:
            u = jax.random.uniform(key, (NT, NE), dtype=jnp.float32)
            g = -jnp.log(-jnp.log(u + np.float32(EPS_G)) + np.float32(EPS_G))
            hi = np.asarray(g, dtype=np.float32)
            lo = np.zeros_like(hi)
    return hi, lo


class Plan:
    ENGS = ("sync", "vector", "scalar", "tensor", "gpsimd")
    SEMS = ("SW", "SG", "SA", "DV", "AC", "PE", "PL", "OUT")
    ESEM = {"vector": "DV", "scalar": "AC", "gpsimd": "PL"}

    def __init__(self, dep=None, rec=None):
        self.ops = {e: [] for e in self.ENGS}
        self.sem = {}
        self.cnt = {}
        self.hw = {e: {} for e in self.ENGS}
        self.bankf = {}
        self.dep = dep
        self.rec = rec
        self.opidx = {e: 0 for e in self.ENGS}

    def _dep_waits(self, eng, idx):
        if self.dep is not None:
            for s, v in self.dep.get((eng, idx), ()):
                self.wait(eng, s, v)

    def _mk_recorder(self, fn, eng, idx, sem, val):
        rec = self.rec
        def run(E, f=fn, key=(eng, idx), sv=(sem, val)):
            inst = f(E)
            if rec is not None:
                try:
                    mi = inst.ins
                    rd = tuple(getattr(x, "memref", None) for x in mi.ins)
                    wr = tuple(getattr(x, "memref", None) for x in mi.outs)
                    rec[key] = (tuple(n for n in rd if n), tuple(n for n in wr if n), sv)
                except Exception:
                    pass
            return inst
        return run

    def use_bank(self, b, eng="tensor"):
        s_v = self.bankf.get(b)
        if s_v:
            self.wait(eng, s_v[0], s_v[1])

    def free_bank(self, b, sem, val):
        self.bankf[b] = (sem, val)

    def emit(self, eng, fn):
        if os.environ.get("KBUILD_DEBUG"):
            import traceback
            site = traceback.extract_stack()[-3]
            def wrapped(E, f=fn, s=site):
                try:
                    return f(E)
                except Exception:
                    print(f"PLAN-SITE: {s.filename}:{s.lineno} {s.line}")
                    raise
            self.ops[eng].append(wrapped)
        else:
            self.ops[eng].append(fn)

    def bump(self, name, k=1):
        self.cnt[name] = self.cnt.get(name, 0) + k
        return self.cnt[name]

    def cur(self, name):
        return self.cnt.get(name, 0)

    def wait(self, eng, sem, val):
        if not val or val <= 0:
            return
        if self.hw[eng].get(sem, 0) >= val:
            return
        self.hw[eng][sem] = val
        self.emit(eng, lambda E, s=sem, v=val: E.wait_ge(self.sem[s], v))

    def dma(self, eng, out, in_, sem, slow=False, ordered=False):
        idx = self.opidx[eng]
        self.opidx[eng] += 1
        self._dep_waits(eng, idx)
        if ordered:
            self.wait(eng, sem, self.cur(sem))
        v = self.bump(sem, 16)
        fn = lambda E, o=out, i=in_, s=sem, sl=slow: E.dma_start(
            out=o, in_=i, allow_slow_non_contiguous=sl).then_inc(self.sem[s], 16)
        self.emit(eng, self._mk_recorder(fn, eng, idx, sem, v))
        return v

    def ins(self, eng, fn, sem=None):
        idx = self.opidx[eng]
        self.opidx[eng] += 1
        self._dep_waits(eng, idx)
        auto = self.ESEM.get(eng)
        if auto is not None:
            sem = auto
        if sem is None:
            self.emit(eng, self._mk_recorder(fn, eng, idx, None, None))
            return None
        v = self.bump(sem)
        fn2 = lambda E, f=fn, s=sem: f(E).then_inc(self.sem[s], 1)
        self.emit(eng, self._mk_recorder(fn2, eng, idx, sem, v))
        return v

    def sync_phase(self, sems=None, engs=None):
        sems = sems or list(self.cnt.keys())
        snap = [(s, self.cur(s)) for s in sems]
        for eng in (engs or self.ENGS):
            for s, v in snap:
                self.wait(eng, s, v)


def v3(ap):
    return ap.rearrange("p (c o) -> p c o", o=1)


def compute_edges(rec):
    dep = {}
    streams = {}
    for (eng, idx) in rec:
        streams.setdefault(eng, []).append(idx)
    for eng, idxs in streams.items():
        if eng == "tensor":
            continue
        track_w = {}
        track_r = {}
        for idx in sorted(idxs):
            rd, wr, (sem, val) = rec[(eng, idx)]
            need = {}
            for n in rd:
                for s, v in track_w.get(n, {}).items():
                    if need.get(s, 0) < v:
                        need[s] = v
            for n in wr:
                for d in (track_w.get(n, {}), track_r.get(n, {})):
                    for s, v in d.items():
                        if need.get(s, 0) < v:
                            need[s] = v
            if sem is not None:
                need.pop(None, None)
                need = {s: v for s, v in need.items() if not (s == sem and v >= val)}
            if need:
                dep[(eng, idx)] = tuple(need.items())
            if sem is not None:
                for n in rd:
                    track_r.setdefault(n, {})[sem] = val
                for n in wr:
                    track_w[n] = {sem: val}
                    track_r[n] = {}
    return dep


def build_program(dep=None, rec=None):
    nc = bass.Bass("TRN2", target_bir_lowering=False, detect_race_conditions=False)
    dt = nc.dram_tensor

    xt = dt("xt", [NKB * 32, 128, 512], F32, kind="ExternalInput").ap()
    gmh = dt("gmh", [NE, 128, 128], F32, kind="ExternalInput").ap()
    gml = dt("gml", [NE, 128, 128], F32, kind="ExternalInput").ap()
    wgt = dt("wgt", [NKB * 32, 128, 512], F32, kind="ExternalInput").ap()
    wdt = dt("wdt", [NDKB * 4, 128, 512], F32, kind="ExternalInput").ap()
    fwg = dt("fwg", [NKB * NFKB, 128, 128], F32, kind="ExternalInput").ap()
    fwu = dt("fwu", [NKB * NFKB, 128, 128], F32, kind="ExternalInput").ap()
    fwd = dt("fwd", [NFKB * 4, 128, 512], F32, kind="ExternalInput").ap()
    ggate2 = dt("ggate2", [128, NKB], F32, kind="ExternalInput").ap()
    gdown2 = dt("gdown2", [128, NDKB], F32, kind="ExternalInput").ap()
    rwa1 = dt("rwa1", [128, NKB * 16], F32, kind="ExternalInput").ap()
    rwa2 = dt("rwa2", [128, NKB * 16], F32, kind="ExternalInput").ap()
    bg2 = dt("bg2", [128, NFKB], F32, kind="ExternalInput").ap()
    bu2 = dt("bu2", [128, NFKB], F32, kind="ExternalInput").ap()
    bd2 = dt("bd2", [128, NKB], F32, kind="ExternalInput").ap()
    rbrow = dt("rbrow", [1, NE], F32, kind="ExternalInput").ap()
    identd = dt("identd", [128, 128], F32, kind="ExternalInput").ap()
    yt = dt("yt", [H, OWN], F32, kind="ExternalOutput").ap()

    dbg = "ExternalOutput" if os.environ.get("KDEBUG_HW") else "Internal"
    d_h2 = dt("d_h2", [NDKB * 4, 128, 512], F32, kind=dbg).ap()
    d_mid = dt("d_mid", [NFKB * 4, 128, 512], BF16, kind=dbg).ap()
    d_xnt = dt("d_xnt", [NKB * 4, 128, 512], BF16, kind=dbg).ap()
    d_h2q = dt("d_h2q", [NDKB * 4, 128, 512], BF16, kind=dbg).ap()
    d_smt = dt("d_smt", [NE, NT], F32, kind=dbg).ap()
    d_rows = dt("d_rows", [8, OWN], F32, kind=dbg).ap()
    d_small = dt("d_small", [4, 128], F32, kind=dbg).ap()
    d_trep = dt("d_trep", [NE, 16], F32).ap()
    d_dbg1 = dt("d_dbg1", [128, NKB * OWN], BF16, kind=dbg).ap()
    d_dbg2 = dt("d_dbg2", [128, NKB * 512], BF16, kind=dbg).ap()

    p = Plan(dep=dep, rec=rec)
    es = ExitStack()
    sb = lambda name, shape, dtyp: es.enter_context(nc.sbuf_tensor(name, shape, dtyp))

    big1 = sb("big1", [128, NKB * OWN], BF16)
    c_id = sb("c_id", [128, 128], F32)
    ones_col = sb("ones_col", [128, 1], F32)
    ones_colr = sb("ones_colr", [128, 16], F32R)
    ones_row = sb("ones_row", [1, 128], F32)
    mqcol = sb("mqcol", [128, 1], F32)
    hlfcol = sb("hlfcol", [128, 1], F32)
    epscol = sb("epscol", [128, 1], F32)
    ggcol = sb("ggcol", [128, NKB], F32)
    gdcol = sb("gdcol", [128, NDKB], F32)
    gd2col = sb("gd2col", [128, NDKB], F32)
    bgcol = sb("bgcol", [128, NFKB], F32)
    bucol = sb("bucol", [128, NFKB], F32)
    bdcol = sb("bdcol", [128, NKB], F32)
    rbr = sb("rbr", [1, NE], F32)
    rbcols = sb("rbcols", [128, NE], F32)
    negcc = sb("negcc", [128, 16], F32)
    rown = sb("rown", [1, 128], F32)
    rowstage = [sb(f"rowstage{i}", [1, 512], F32) for i in range(2)]
    tk_r = sb("tk_r", [128, 128], F32)
    tk_mr = sb("tk_mr", [128, 128], F32)
    tk_rs = sb("tk_rs", [128, 128], F32)
    tmpA = sb("tmpA", [128, 128], F32)
    tmpB = sb("tmpB", [128, 128], F32)
    tmpC = sb("tmpC", [128, 128], F32)
    tmpD = sb("tmpD", [128, 128], F32)
    wacc_g = sb("wacc_g", [128, 1], F32)
    wacc_d = sb("wacc_d", [128, 1], F32)
    wrow = sb("wrow", [1, 16], F32)
    swg = sb("swg", [128, 2], F32)
    swd = sb("swd", [128, 2], F32)
    bis_lo = sb("bis_lo", [128, 1], F32)
    bis_hi = sb("bis_hi", [128, 1], F32)
    bis_mid = sb("bis_mid", [128, 1], F32)
    bis_m8 = sb("bis_m8", [128, 1], F32)
    bis_cnt = sb("bis_cnt", [128, 1], F32)
    bis_c16 = sb("bis_c16", [8, 16], F32)
    bis_sel = sb("bis_sel", [128, 1], F32)
    bis_t1 = sb("bis_t1", [128, 1], F32)
    bis_t2 = sb("bis_t2", [128, 1], F32)
    th_row = sb("th_row", [1, 16], F32)
    th_cols = sb("th_cols", [128, NE], F32)
    mxcols = sb("mxcols", [128, 16], F32)
    stat_cols = sb("stat_cols", [128, 48], F32)
    colasm = sb("colasm", [128, 128], F32)
    es2 = ExitStack()
    sb2 = lambda name, shape, dtyp: es2.enter_context(nc.sbuf_tensor(name, shape, dtyp))
    sm8 = sb2("sm8", [128, NE * 128], F32)
    smb = sb2("smb", [128, 1024], F32)
    bis_junk = sb2("bis_junk", [128, 1024], F32)
    rxs = ExitStack()
    xring = [rxs.enter_context(nc.sbuf_tensor(f"xring{i}", [128, 512], F32)) for i in range(4)]

    pse = ExitStack()
    ps = [pse.enter_context(nc.psum_tensor(f"ps{i}", [128, 512], F32)) for i in range(8)]

    SY, VE, SC, TE, GP = "sync", "vector", "scalar", "tensor", "gpsimd"

    def row_put(cols_dv, src16, q_idx, col0=0, ncols=16):
        p.wait(TE, "DV", cols_dv)
        p.use_bank(2)
        vt = p.ins(TE, lambda E, s2=src16: E.transpose(ps[2][0:ncols, 0:128], s2, c_id[:, :]), "PE")
        p.wait(VE, "PE", vt)
        vev = p.ins(VE, lambda E, n=ncols: E.tensor_copy(out=colasm[0:n, :], in_=ps[2][0:n, 0:128]), "DV")
        p.free_bank(2, "DV", vev)
        p.wait(SY, "DV", vev)
        vd = p.dma(SY, d_rows[q_idx:q_idx + 1, col0 * 128:(col0 + ncols) * 128].rearrange(
            "o (c q) -> (o c) q", q=128), colasm[0:ncols, :], "RW", ordered=True)
        p.wait(VE, "RW", vd)
        return vd

    rowstage_free = [("PE", 0), ("PE", 0)]
    rowstage_idx = [0]

    def bc_row(q_idx, col0, width, dst, dst_off, row_ready_sw, bank):
        j = rowstage_idx[0] % 2
        rowstage_idx[0] += 1
        s, v = rowstage_free[j]
        p.wait(SY, s, v)
        p.wait(SY, "RW", row_ready_sw)
        vr = p.dma(SY, rowstage[j][:1, 0:width], d_rows[q_idx:q_idx + 1, col0:col0 + width], f"RS{j}")
        p.wait(TE, f"RS{j}", vr)
        p.use_bank(bank)
        vb = p.ins(TE, lambda E, j2=j, w=width, b=bank: E.matmul(
            ps[b][:, 0:w], ones_row[:1, :], rowstage[j2][:1, 0:w], start=True, stop=True), "PE")
        rowstage_free[j] = ("PE", vb)
        p.wait(VE, "PE", vb)
        vev = p.ins(VE, lambda E, d=dst, o=dst_off, w=width, b=bank: E.tensor_copy(
            out=d[:, o:o + w], in_=ps[b][:, 0:w]), "DV")
        p.free_bank(bank, "DV", vev)
        return vev

    p.dma(SY, c_id[:, :], identd, "PR")
    p.dma(SY, ggcol[:, :], ggate2, "PR")
    p.dma(SY, gdcol[:, :], gdown2, "PR")
    p.dma(SY, bgcol[:, :], bg2, "PR")
    p.dma(SY, bucol[:, :], bu2, "PR")
    p.dma(SY, bdcol[:, :], bd2, "PR")
    p.dma(SY, rbr[:1, :], rbrow, "PR")
    v_pro = p.cur("PR")

    p.ins(VE, lambda E: E.memset(ones_col[:, :], 1.0))
    p.ins(VE, lambda E: E.memset(stat_cols[:, 0:16], 1.0))
    p.ins(VE, lambda E: E.tensor_copy(out=ones_colr[:, :], in_=stat_cols[:, 0:16]))
    p.ins(VE, lambda E: E.memset(ones_row[:1, :], 1.0))
    p.ins(VE, lambda E: E.memset(epscol[:, :], EPS_LN))
    p.ins(VE, lambda E: E.memset(hlfcol[:, :], 0.5))
    p.ins(VE, lambda E: E.tensor_scalar(hlfcol[:, :], hlfcol[:, :], M32, M32, A.add, A.subtract))
    p.ins(VE, lambda E: E.tensor_scalar(hlfcol[:, :], hlfcol[:, :], 2.0, None, A.mult))
    p.ins(VE, lambda E: E.tensor_scalar(mqcol[:, :], hlfcol[:, :], M64 - M32, M32, A.mult, A.add))
    p.ins(VE, lambda E: E.memset(wacc_g[:, :], 0.0))
    p.ins(VE, lambda E: E.memset(colasm[:, :], 0.0))
    p.ins(VE, lambda E: E.memset(wrow[:1, :], 0.0))
    p.ins(VE, lambda E: E.memset(wacc_d[:, :], 0.0))
    p.wait(VE, "PR", v_pro)
    v_gd2pre = p.ins(VE, lambda E: E.engine_nop(), "DV")
    p.wait(SC, "DV", v_gd2pre)
    p.ins(SC, lambda E: E.activation(gd2col[:, :], gdcol[:, :], AF.Square), "AC")

    rsc = ExitStack()
    rb_ = lambda name, shape, dtyp: rsc.enter_context(nc.sbuf_tensor(name, shape, dtyp))
    statsT = rb_("statsT", [128, 2048], F32)
    gmh_sb = rb_("gmh_sb", [128, NE * 128], F32)
    gml_sb = rb_("gml_sb", [128, NE * 128], F32)
    xfr = [rb_(f"xfr{i}", [128, 512], F32R) for i in range(2)]
    xsq = [rb_(f"xsq{i}", [128, 512], F32R) for i in range(2)]
    stats_sl = [rb_(f"statsl{i}", [128, 512], F32) for i in range(2)]
    rwa1s = rb_("rwa1s", [128, NKB * 16], F32R)
    rwa2s = rb_("rwa2s", [128, NKB * 16], F32R)
    wgpre = [rb_(f"wgpre{i}", [128, 512], F32) for i in range(4)]
    wdpre = [rb_(f"wdpre{i}", [128, 2048], F32) for i in range(2)]
    wtmp = rb_("wtmp", [128, 512], F32)
    wtmp2 = rb_("wtmp2", [128, 128], F32)
    wjunk = rb_("wjunk", [128, 512], F32)
    wjunk2 = rb_("wjunk2", [128, 1024], F32)

    for e in range(NE):
        p.dma(SY, gmh_sb[:, e * 128:(e + 1) * 128], gmh[e, :, :], "PR")
        p.dma(SY, gml_sb[:, e * 128:(e + 1) * 128], gml[e, :, :], "PR")
    p.dma(SY, wgpre[0][:, 0:256], rwa1, "PR")
    v_rwl = p.dma(SY, wgpre[1][:, 0:256], rwa2, "PR")
    p.wait(VE, "PR", v_rwl)
    p.ins(VE, lambda E: E.tensor_copy(out=rwa1s[:, :], in_=wgpre[0][:, 0:256]))
    v_rwa = p.ins(VE, lambda E: E.tensor_copy(out=rwa2s[:, :], in_=wgpre[1][:, 0:256]), "DV")

    NSL = NT // 512
    xr_free = [("DV", 0)] * 4
    xfr_free = [("PE", 0)] * 2
    xsq_free = [("PE", 0)] * 2
    stats_free = [("DV", 0)] * 2
    p.ins(VE, lambda E: E.memset(stats_sl[0][:, :], 0.0))
    p.ins(VE, lambda E: E.memset(stats_sl[1][:, :], 0.0))
    p.wait(TE, "DV", v_rwa)

    wg_free = [("AC", 0)] * 4
    wg_free[0] = ("DV", v_rwa)
    wg_free[1] = ("DV", v_rwa)
    wd_free = [("AC", 0)] * 2
    WGQ = (GP, GP, SY, GP, GP, GP, SC, GP, GP, GP, SY, GP, GP, GP, SC, GP)

    def emit_wg_unit(t):
        kb2 = t % NKB
        cs = t // NKB
        slot = t % 4
        eng = WGQ[t % 16]
        s, v = wg_free[slot]
        p.wait(eng, s, v)
        vld = p.dma(eng, wgpre[slot][:, :], wgt[kb2 * 32 + cs, :, :], f"GWG{slot}")
        p.wait(SC, f"GWG{slot}", vld)
        vr = p.ins(SC, lambda E, sl2=slot, t2=t: E.activation(
            wjunk[:, :], wgpre[sl2][:, :], AF.Abs, accum_out=wtmp[:, t2:t2 + 1]), "AC")
        wg_free[slot] = ("AC", vr)

    def emit_wd_unit(td):
        dslot = td % 2
        s, v = wd_free[dslot]
        p.wait(SC, s, v)
        vldd = 0
        for j in range(4):
            vldd = p.dma(SC, wdpre[dslot][:, j * 512:(j + 1) * 512], wdt[td * 4 + j, :, :], f"GWD{dslot}")
        p.wait(SC, f"GWD{dslot}", vldd)
        p.ins(SC, lambda E, sl2=dslot, t2=td: E.activation(
            wjunk2[:, 0:1024], wdpre[sl2][:, 0:1024], AF.Abs, accum_out=wtmp2[:, t2:t2 + 1]), "AC")
        vrr = p.ins(SC, lambda E, sl2=dslot, t2=td: E.activation(
            wjunk2[:, 0:1024], wdpre[sl2][:, 1024:2048], AF.Abs, accum_out=wtmp2[:, 64 + t2:64 + t2 + 1]), "AC")
        wd_free[dslot] = ("AC", vrr)

    for sl in range(NSL):
        bank = sl % 2
        vmm = None
        for kb in range(NKB):
            u = sl * NKB + kb
            slot = u % 4
            s, v = xr_free[slot]
            p.wait(SY, s, v)
            vld = p.dma(SY, xring[slot][:, :], xt[kb * 32 + sl, :, :], f"XR{slot}")
            j = u % 2
            s, v = xfr_free[j]
            p.wait(VE, s, v)
            p.wait(VE, f"XR{slot}", vld)
            vcv = p.ins(VE, lambda E, a=xfr[j], b=xring[slot]: E.tensor_copy(out=a[:, :], in_=b[:, :]), "DV")
            xr_free[slot] = ("DV", vcv)
            s, v = xsq_free[j]
            p.wait(VE, s, v)
            vsq = p.ins(VE, lambda E, a=xsq[j], b=xfr[j]: E.tensor_mul(a[:, :], b[:, :], b[:, :]), "DV")
            if kb == 0:
                s, v = stats_free[bank]
                p.wait(TE, s, v)
            p.wait(TE, "DV", vcv)
            p.ins(TE, lambda E, b=bank, k=kb, j2=j: E.matmul(
                ps[b][0:16, :], rwa1s[:, k * 16:(k + 1) * 16], xfr[j2][:, :], start=(k == 0), stop=False))
            p.wait(TE, "DV", vsq)
            vmm = p.ins(TE, lambda E, b=bank, k=kb, j2=j: E.matmul(
                ps[b][0:16, :], rwa2s[:, k * 16:(k + 1) * 16], xsq[j2][:, :], start=False, stop=(k == NKB - 1)),
                "PE")
            xfr_free[j] = ("PE", vmm)
            xsq_free[j] = ("PE", vmm)
            emit_wg_unit(u)
            if u % 8 == 0:
                emit_wd_unit(u // 8)
        p.wait(VE, "PE", vmm)
        vev = p.ins(VE, lambda E, d=stats_sl[sl % 2], b=bank: E.tensor_copy(out=d[0:16, :], in_=ps[b][0:16, :]), "DV")
        stats_free[bank] = ("DV", vev)
        for c in range(4):
            p.wait(TE, "DV", vev)
            p.use_bank(2)
            vtr = p.ins(TE, lambda E, d=stats_sl[sl % 2], c2=c: E.transpose(
                ps[2][:, 0:128], d[:, c2 * 128:(c2 + 1) * 128], c_id[:, :]), "PE")
            p.wait(VE, "PE", vtr)
            vst2 = p.ins(VE, lambda E, sl2=sl, c2=c: E.tensor_copy(
                out=statsT[:, (sl2 * 4 + c2) * 16:(sl2 * 4 + c2) * 16 + 16], in_=ps[2][:, 0:16]), "DV")
            p.free_bank(2, "DV", vst2)
    v_statsT = p.ins(VE, lambda E: E.engine_nop(), "DV")

    v_wabs = p.cur("AC")
    p.wait(VE, "AC", v_wabs)
    v_waccg = p.ins(VE, lambda E: E.tensor_reduce(wacc_g[:, :], wtmp[:, :], AX, A.add), "DV")
    v_waccd = p.ins(VE, lambda E: E.tensor_reduce(wacc_d[:, :], wtmp2[:, :], AX, A.add), "DV")

    p.wait(TE, "DV", v_waccg)
    p.use_bank(3)
    vmw = p.ins(TE, lambda E: E.transpose(ps[3][0:1, 0:128], wacc_g[:, 0:1], c_id[:, :]), "PE")
    p.wait(TE, "DV", v_waccd)
    p.use_bank(2)
    vmw2 = p.ins(TE, lambda E: E.transpose(ps[2][0:1, 0:128], wacc_d[:, 0:1], c_id[:, :]), "PE")
    p.wait(VE, "PE", vmw2)
    p.ins(VE, lambda E: E.reduce_sum(wrow[:1, 0:1], ps[3][0:1, 0:128], AX))
    vwd0 = p.ins(VE, lambda E: E.reduce_sum(wrow[:1, 1:2], ps[2][0:1, 0:128], AX), "DV")
    p.free_bank(2, "DV", vwd0)
    p.ins(VE, lambda E: E.tensor_scalar(wrow[:1, 0:1], wrow[:1, 0:1], 1.0 / NWG, 1e-5, A.mult, A.max))
    p.ins(VE, lambda E: E.tensor_scalar(wrow[:1, 1:2], wrow[:1, 1:2], 1.0 / NWD, 1e-5, A.mult, A.max))
    vwr = p.ins(VE, lambda E: E.reciprocal(wrow[:1, 2:4], wrow[:1, 0:2]), "DV")
    p.wait(TE, "DV", vwr)
    p.use_bank(3)
    vbw = p.ins(TE, lambda E: E.matmul(ps[3][:, 0:16], ones_row[:1, :], wrow[:1, 0:16], start=True, stop=True), "PE")
    p.wait(VE, "PE", vbw)
    p.ins(VE, lambda E: E.tensor_copy(out=swg[:, 1:2], in_=ps[3][:, 0:1]))
    p.ins(VE, lambda E: E.tensor_copy(out=swd[:, 1:2], in_=ps[3][:, 1:2]))
    p.ins(VE, lambda E: E.tensor_copy(out=swg[:, 0:1], in_=ps[3][:, 2:3]))
    v_sw = p.ins(VE, lambda E: E.tensor_copy(out=swd[:, 0:1], in_=ps[3][:, 3:4]), "DV")
    p.free_bank(3, "DV", v_sw)

    def qv(q):
        return statsT[:, :].rearrange("p (c q) -> p c q", q=16)[:, :, q:q + 1]

    p.wait(VE, "DV", v_statsT)
    p.use_bank(3)
    for kb in range(NKB):
        vcs = p.ins(TE, lambda E, k=kb: E.matmul(
            ps[3][0:16, 32:48], rwa1s[:, k * 16:(k + 1) * 16], ones_colr[:, :],
            start=(k == 0), stop=(k == NKB - 1)), "PE" if kb == NKB - 1 else None)
    p.wait(VE, "PE", vcs)
    vcse = p.ins(VE, lambda E: E.tensor_copy(out=colasm[0:16, 0:1], in_=ps[3][0:16, 32:33]), "DV")
    p.free_bank(3, "DV", vcse)
    p.wait(TE, "DV", vcse)
    p.use_bank(2)
    vcst = p.ins(TE, lambda E: E.transpose(ps[2][0:1, 0:128], colasm[:, 0:1], c_id[:, :]), "PE")
    p.wait(VE, "PE", vcst)
    vrn = p.ins(VE, lambda E: E.tensor_copy(out=rown[:1, :], in_=ps[2][0:1, 0:128]), "DV")
    p.free_bank(2, "DV", vrn)
    vneg = p.ins(VE, lambda E: E.tensor_scalar(rown[:1, 0:16], rown[:1, 0:16], -INV_H, None, A.mult), "DV")
    p.wait(TE, "DV", vneg)
    p.use_bank(3)
    p.ins(TE, lambda E: E.matmul(ps[3][:, 8:24], ones_row[:1, :], rown[:1, 0:16], start=True, stop=True))
    vb2 = p.ins(TE, lambda E: E.matmul(ps[3][:, 24:32], ones_row[:1, :], rbr[:1, :], start=True, stop=True), "PE")
    p.wait(VE, "PE", vb2)
    p.ins(VE, lambda E: E.tensor_copy(out=negcc[:, :], in_=ps[3][:, 8:24]))
    vrbc = p.ins(VE, lambda E: E.tensor_copy(out=rbcols[:, :], in_=ps[3][:, 24:32]), "DV")
    p.free_bank(3, "DV", vrbc)
    p.ins(VE, lambda E: E.tensor_scalar(v3(tmpA[:, :]), qv(8), INV_H, None, A.mult))
    p.ins(VE, lambda E: E.tensor_scalar(v3(tmpB[:, :]), qv(9), INV_H, None, A.mult))
    p.ins(VE, lambda E: E.tensor_mul(tmpC[:, :], tmpA[:, :], tmpA[:, :]))
    p.ins(VE, lambda E: E.tensor_tensor(tmpB[:, :], tmpB[:, :], tmpC[:, :], A.subtract))
    vv = p.ins(VE, lambda E: E.engine_nop(), "DV")
    p.wait(SC, "DV", vv)
    va = p.ins(SC, lambda E: E.activation(tmpC[:, :], tmpB[:, :], AF.Sqrt, bias=epscol[:, 0:1]), "AC")
    p.wait(VE, "AC", va)
    p.ins(VE, lambda E: E.reciprocal(tk_r[:, :], tmpC[:, :]))
    p.ins(VE, lambda E: E.tensor_mul(tk_mr[:, :], tmpA[:, :], tk_r[:, :]))
    p.ins(VE, lambda E: E.tensor_mul(tmpD[:, :], tk_r[:, :], tk_r[:, :]))
    p.ins(VE, lambda E: E.tensor_mul(tmpD[:, :], tmpB[:, :], tmpD[:, :]))
    vq = p.ins(VE, lambda E: E.engine_nop(), "DV")
    p.wait(SC, "DV", vq)
    va2 = p.ins(SC, lambda E: E.activation(tmpD[:, :], tmpD[:, :], AF.Sqrt, bias=epscol[:, 0:1]), "AC")
    p.wait(VE, "AC", va2)
    p.ins(VE, lambda E: E.reciprocal(tk_rs[:, :], tmpD[:, :]))
    for e in range(NE):
        sl8 = lambda e2: sm8[:, e2 * 128:(e2 + 1) * 128]
        p.ins(VE, lambda E, e2=e: E.scalar_tensor_tensor(
            v3(sm8[:, e2 * 128:(e2 + 1) * 128]), qv(8), negcc[:, e2:e2 + 1], qv(e2), A.mult, A.add))
        p.ins(VE, lambda E, e2=e: E.tensor_mul(sl8(e2), sl8(e2), tk_r[:, :]))
        p.ins(VE, lambda E, e2=e: E.tensor_scalar(sl8(e2), sl8(e2), rbcols[:, e2:e2 + 1], None, A.add))
        p.ins(VE, lambda E, e2=e: E.tensor_add(sl8(e2), sl8(e2), gmh_sb[:, e2 * 128:(e2 + 1) * 128]))
        p.ins(VE, lambda E, e2=e: E.tensor_add(sl8(e2), sl8(e2), gml_sb[:, e2 * 128:(e2 + 1) * 128]))
    p.ins(VE, lambda E: E.tensor_copy(out=tmpA[:, :], in_=sm8[:, 0:128]))
    for e in range(1, NE):
        p.ins(VE, lambda E, e2=e: E.tensor_max(tmpA[:, :], tmpA[:, :], sm8[:, e2 * 128:(e2 + 1) * 128]))
    for e in range(NE):
        p.ins(VE, lambda E, e2=e: E.tensor_tensor(
            sm8[:, e2 * 128:(e2 + 1) * 128], sm8[:, e2 * 128:(e2 + 1) * 128], tmpA[:, :], A.subtract))
    vsm = p.ins(VE, lambda E: E.engine_nop(), "DV")
    p.wait(SC, "DV", vsm)
    vex = None
    for e in range(NE):
        vex = p.ins(SC, lambda E, e2=e: E.activation(
            sm8[:, e2 * 128:(e2 + 1) * 128], sm8[:, e2 * 128:(e2 + 1) * 128], AF.Exp), "AC")
    p.wait(VE, "AC", vex)
    p.ins(VE, lambda E: E.tensor_copy(out=tmpB[:, :], in_=sm8[:, 0:128]))
    for e in range(1, NE):
        p.ins(VE, lambda E, e2=e: E.tensor_add(tmpB[:, :], tmpB[:, :], sm8[:, e2 * 128:(e2 + 1) * 128]))
    p.ins(VE, lambda E: E.reciprocal(tmpC[:, :], tmpB[:, :]))
    for e in range(NE):
        p.ins(VE, lambda E, e2=e: E.tensor_mul(
            sm8[:, e2 * 128:(e2 + 1) * 128], sm8[:, e2 * 128:(e2 + 1) * 128], tmpC[:, :]))
    v_sm8 = p.ins(VE, lambda E: E.engine_nop(), "DV")

    vsmt_dma = 0
    for e in range(NE):
        p.wait(TE, "DV", v_sm8)
        p.use_bank(2)
        vt = p.ins(TE, lambda E, e2=e: E.transpose(ps[2][:, 0:128], sm8[:, e2 * 128:(e2 + 1) * 128], c_id[:, :]), "PE")
        p.wait(VE, "PE", vt)
        p.wait(VE, "SM", vsmt_dma)
        vte = p.ins(VE, lambda E: E.tensor_copy(out=tmpD[:, :], in_=ps[2][:, 0:128]), "DV")
        p.free_bank(2, "DV", vte)
        p.wait(SY, "DV", vte)
        vsmt_dma = p.dma(SY, d_smt[e:e + 1, :].rearrange("o (c q) -> (o c) q", q=128), tmpD[:, :], "SM", ordered=True)
    p.wait(GP, "SM", vsmt_dma)
    v_smb = p.dma(GP, smb[:, :], d_smt.rearrange("e t -> (e t)").rearrange("(q f) -> q f", q=128), "SG")

    vmk = p.ins(VE, lambda E: E.tensor_copy(out=mxcols[:, 0:16], in_=tk_r[:, 0:16]), "DV")
    v_row_r = row_put(vmk, mxcols[:, 0:16], Q_R)
    vmk = p.ins(VE, lambda E: E.tensor_copy(out=mxcols[:, 0:16], in_=tk_mr[:, 0:16]), "DV")
    v_row_mr = row_put(vmk, mxcols[:, 0:16], Q_MR)
    vmk = p.ins(VE, lambda E: E.tensor_copy(out=mxcols[:, 0:16], in_=tk_rs[:, 0:16]), "DV")
    v_row_rs = row_put(vmk, mxcols[:, 0:16], Q_RS)

    p.ins(GP, lambda E: E.memset(bis_lo[:, :], 0.0))
    p.ins(GP, lambda E: E.memset(bis_hi[:, :], 1.5))
    p.ins(GP, lambda E: E.memset(bis_mid[:, :], 0.75))
    p.ins(GP, lambda E: E.memset(bis_m8[:, :], 0.75))
    p.wait(GP, "SG", v_smb)
    for it in range(N_BISECT):
        p.ins(GP, lambda E: E.tensor_single_scalar(bis_junk[:, :], smb[:, :], bis_mid[:, 0:1], A.is_ge))
        for w in (512, 256, 128, 64, 32, 16, 8, 4, 2, 1):
            p.ins(GP, lambda E, w2=w: E.tensor_add(bis_junk[:, 0:w2], bis_junk[:, 0:w2], bis_junk[:, w2:2 * w2]))
        p.ins(GP, lambda E: E.tensor_copy(out=bis_cnt[:, 0:1], in_=bis_junk[:, 0:1]))
        v1 = p.dma(GP, d_small[0:1, 0:128], bis_cnt[:, 0:1], "SG")
        p.wait(GP, "SG", v1)
        v2 = p.dma(GP, bis_c16[0:8, :], d_small[0:1, 0:128].rearrange("o (e j) -> (o e) j", j=16), "SG")
        p.wait(GP, "SG", v2)
        for w in (8, 4, 2, 1):
            p.ins(GP, lambda E, w2=w: E.tensor_add(bis_c16[0:8, 0:w2], bis_c16[0:8, 0:w2], bis_c16[0:8, w2:2 * w2]))
        p.ins(GP, lambda E: E.tensor_copy(out=bis_t1[0:8, 0:1], in_=bis_c16[0:8, 0:1]))
        p.ins(GP, lambda E: E.tensor_scalar(bis_sel[0:8, 0:1], bis_t1[0:8, 0:1], float(KCAP), None, A.is_ge))
        p.ins(GP, lambda E: E.tensor_tensor(bis_t1[0:8, 0:1], bis_m8[0:8, 0:1], bis_lo[0:8, 0:1], A.subtract))
        p.ins(GP, lambda E: E.tensor_mul(bis_t1[0:8, 0:1], bis_t1[0:8, 0:1], bis_sel[0:8, 0:1]))
        p.ins(GP, lambda E: E.tensor_add(bis_lo[0:8, 0:1], bis_lo[0:8, 0:1], bis_t1[0:8, 0:1]))
        p.ins(GP, lambda E: E.tensor_scalar(bis_sel[0:8, 0:1], bis_sel[0:8, 0:1], -1.0, 1.0, A.mult, A.add))
        p.ins(GP, lambda E: E.tensor_tensor(bis_t2[0:8, 0:1], bis_m8[0:8, 0:1], bis_hi[0:8, 0:1], A.subtract))
        p.ins(GP, lambda E: E.tensor_mul(bis_t2[0:8, 0:1], bis_t2[0:8, 0:1], bis_sel[0:8, 0:1]))
        p.ins(GP, lambda E: E.tensor_add(bis_hi[0:8, 0:1], bis_hi[0:8, 0:1], bis_t2[0:8, 0:1]))
        p.ins(GP, lambda E: E.tensor_tensor(bis_t1[0:8, 0:1], bis_lo[0:8, 0:1], bis_hi[0:8, 0:1], A.add))
        p.ins(GP, lambda E: E.tensor_scalar(bis_t1[0:8, 0:1], bis_t1[0:8, 0:1], 0.5, None, A.mult))
        p.ins(GP, lambda E: E.tensor_copy(out=bis_m8[0:8, 0:1], in_=bis_t1[0:8, 0:1]))
        v3_ = p.dma(GP, d_small[1:2, 0:8], bis_t1[0:8, 0:1], "SG")
        p.wait(GP, "SG", v3_)
        vrp = None
        for jj in range(16):
            vrp = p.dma(GP, d_trep[:, jj:jj + 1],
                        d_small[1:2, 0:8].rearrange("o e -> (o e)").rearrange("(e o) -> e o", o=1), "SG", slow=True)
        p.wait(GP, "SG", vrp)
        vm = p.dma(GP, bis_mid[:, 0:1], d_trep.rearrange("e j -> (e j)").rearrange("(q o) -> q o", o=1), "SG")
        p.wait(GP, "SG", vm)
    v_th = p.dma(GP, d_small[3:4, 0:8], bis_lo[0:8, 0:1], "SG")
    v_bis = p.ins(GP, lambda E: E.engine_nop(), "PL")

    p.sync_phase(sems=("SW", "DV", "AC", "PE"))
    rsc.close()

    xsc = ExitStack()
    xb_ = lambda name, shape, dtyp: xsc.enter_context(nc.sbuf_tensor(name, shape, dtyp))
    xr_sl = xb_("xr_sl", [128, NKB * 512], F32)
    xn_f = xb_("xn_f", [128, 512], F32)
    xnb = [xb_(f"xnb{i}", [128, 512], BF16) for i in range(2)]
    sqh = xb_("sqh", [128, 512], F32)
    silx = [xb_(f"silx{i}", [128, 512], F32) for i in range(2)]
    bcx = [xb_(f"bcx{i}", [128, 512], F32) for i in range(3)]
    i32t = [xb_(f"i32t{i}", [128, 512], F32) for i in range(2)]
    prex = [xb_(f"prex{i}", [128, 512], F32) for i in range(2)]

    xnb_free = [("XB0", 0), ("XB1", 0)]
    i32_free = [("DV", 0)] * 2
    prex_free = [("AC", 0)] * 2
    silx_free = [("DV", 0)] * 2
    v_row_sq = 0
    for sl in range(4):
        vr1 = bc_row(Q_R, sl * 512, 512, bcx[0], 0, v_row_r, 0)
        vr2 = bc_row(Q_MR, sl * 512, 512, bcx[1], 0, v_row_mr, 1)
        vr3 = bc_row(Q_RS, sl * 512, 512, bcx[2], 0, v_row_rs, 3)
        p.wait(VE, "DV", max(vr1, vr2, vr3))
        p.ins(VE, lambda E: E.memset(sqh[:, :], 0.0))
        for kb in range(NKB):
            u = sl * NKB + kb
            slot = u % 4
            s, v = xr_free[slot]
            p.wait(SY, s, v)
            vld = p.dma(SY, xring[slot][:, :], xt[kb * 32 + sl, :, :], f"XR{slot}")
            p.wait(VE, f"XR{slot}", vld)
            p.ins(VE, lambda E, sl2=slot: E.tensor_mul(xn_f[:, :], xring[sl2][:, :], bcx[0][:, :]))
            vxn = p.ins(VE, lambda E: E.tensor_tensor(xn_f[:, :], xn_f[:, :], bcx[1][:, :], A.subtract), "DV")
            xr_free[slot] = ("DV", vxn)
            jb = u % 2
            s, v = xnb_free[jb]
            p.wait(VE, s, v)
            p.ins(VE, lambda E, j2=jb: E.tensor_copy(out=xnb[j2][:, :], in_=xn_f[:, :]))
            vxr = p.ins(VE, lambda E, k=kb: E.scalar_tensor_tensor(
                xr_sl[:, k * 512:(k + 1) * 512], xn_f[:, :], ggcol[:, k:k + 1], bcx[2][:, :], A.mult, A.mult), "DV")
            p.wait(SC, "DV", vxr)
            vsp = p.dma(SC, d_xnt[kb * 4 + sl, :, :], xnb[jb][:, :], f"XB{jb}")
            xnb_free[jb] = (f"XB{jb}", vsp)
            jx = u % 2
            s, v = silx_free[jx]
            p.wait(SC, s, v)
            vsq2 = p.ins(SC, lambda E, k=kb, j2=jx: E.activation(
                silx[j2][:, :], xr_sl[:, k * 512:(k + 1) * 512], AF.Square), "AC")
            p.wait(VE, "AC", vsq2)
            vfo = p.ins(VE, lambda E, j2=jx: E.tensor_max(sqh[:, :], sqh[:, :], silx[j2][:, :]), "DV")
            silx_free[jx] = ("DV", vfo)
        vfold = p.cur("DV")
        for c in range(4):
            p.wait(TE, "DV", vfold)
            p.use_bank(2)
            vt = p.ins(TE, lambda E, c2=c: E.transpose(ps[2][:, 0:128], sqh[:, c2 * 128:(c2 + 1) * 128], c_id[:, :]), "PE")
            p.wait(VE, "PE", vt)
            vrm = p.ins(VE, lambda E, s2=sl, c2=c: E.reduce_max(mxcols[:, s2 * 4 + c2:s2 * 4 + c2 + 1], ps[2][:, 0:128], AX), "DV")
            p.free_bank(2, "DV", vrm)
        p.wait(SC, "DV", p.cur("DV"))
        vsc = p.ins(SC, lambda E, s2=sl: E.activation(
            mxcols[:, s2 * 4:s2 * 4 + 4], mxcols[:, s2 * 4:s2 * 4 + 4], AF.Sqrt), "AC")
        p.wait(VE, "AC", vsc)
        p.ins(VE, lambda E, s2=sl: E.tensor_scalar(
            stat_cols[:, s2 * 4:s2 * 4 + 4], mxcols[:, s2 * 4:s2 * 4 + 4], 1e-5, None, A.max))
        p.ins(VE, lambda E, s2=sl: E.reciprocal(mxcols[:, s2 * 4:s2 * 4 + 4], stat_cols[:, s2 * 4:s2 * 4 + 4]))
        vsq_c = p.ins(VE, lambda E, s2=sl: E.tensor_scalar(
            mxcols[:, s2 * 4:s2 * 4 + 4], mxcols[:, s2 * 4:s2 * 4 + 4], 127.0, None, A.mult), "DV")
        v_row_sq = row_put(vsq_c, mxcols[:, sl * 4:sl * 4 + 4], Q_SQ, col0=sl * 4, ncols=4)
        vbq = bc_row(Q_SQ, sl * 512, 512, bcx[0], 0, v_row_sq, 0)
        p.wait(VE, "DV", vbq)
        for kb in range(NKB):
            ji = kb % 2
            s, v = prex_free[ji]
            p.wait(VE, s, v)
            vpre = p.ins(VE, lambda E, k=kb, j2=ji: E.tensor_mul(
                prex[j2][:, :], xr_sl[:, k * 512:(k + 1) * 512], bcx[0][:, :]), "DV")
            s, v = i32_free[ji]
            p.wait(SC, s, v)
            p.wait(SC, "DV", vpre)
            vcv = p.ins(SC, lambda E, j2=ji: E.activation(i32t[j2][:, :], prex[j2][:, :], AF.Identity, bias=mqcol[:, 0:1]), "AC")
            prex_free[ji] = ("AC", vcv)
            p.wait(VE, "AC", vcv)
            vun = p.ins(VE, lambda E, s2=sl, k=kb, j2=ji: E.tensor_scalar(
                big1[:, k * OWN + s2 * 512:k * OWN + s2 * 512 + 512], i32t[j2][:, :], mqcol[:, 0:1], None, A.subtract), "DV")
            i32_free[ji] = ("DV", vun)
    p.wait(VE, "DV", v_sw)
    vsc1c = p.ins(VE, lambda E: E.tensor_scalar(
        mxcols[:, 0:16], stat_cols[:, 0:16], swg[:, 1:2], 1.0 / 127.0, A.mult, A.mult), "DV")
    v_row_sc1 = row_put(vsc1c, mxcols[:, 0:16], Q_SC1)
    v_xq = p.cur("DV")
    if os.environ.get("KDEBUG_A1"):
        p.wait(SY, "DV", v_xq)
        p.dma(SY, d_dbg1, big1[:, :], "SW")

    p.sync_phase(sems=("SW", "SA", "DV", "AC", "PE"))
    xsc.close()
    rxs.close()

    a1 = ExitStack()
    ab_ = lambda name, shape, dtyp: a1.enter_context(nc.sbuf_tensor(name, shape, dtyp))
    wstage = [ab_(f"wstage{i}", [128, 512], F32) for i in range(3)]
    i8a = [ab_(f"i8a{i}", [128, 512], F32) for i in range(2)]
    clipa = ab_("clipa", [128, 512], F32)
    wq_g = [ab_(f"wq_g{i}", [128, NKB * 512], BF16) for i in range(2)]
    wq_v = [ab_(f"wq_v{i}", [128, NKB * 512], BF16) for i in range(2)]
    h2ring = [ab_(f"h2ring{i}", [128, 512], F32) for i in range(4)]
    accR2 = ab_("accR2", [128, OWN], F32)
    accM2 = ab_("accM2", [128, OWN], F32)
    sc1bc = ab_("sc1bc", [128, OWN], F32)
    gsc_t = [ab_(f"gsc_t{i}", [128, 512], F32) for i in range(2)]
    vsc_t = [ab_(f"vsc_t{i}", [128, 512], F32) for i in range(2)]
    sq_t = [ab_(f"sq_t{i}", [128, 512], F32) for i in range(2)]

    p.ins(VE, lambda E: E.memset(accR2[:, :], 0.0))
    p.ins(VE, lambda E: E.memset(accM2[:, :], 0.0))
    for ts in range(4):
        bc_row(Q_SC1, ts * 512, 512, sc1bc, ts * 512, v_row_sc1, ts % 2)
    v_a1pre = p.ins(VE, lambda E: E.engine_nop(), "DV")
    p.wait(TE, "DV", v_a1pre)

    wstage_free = [("AC", 0)] * 3
    i8a_free = [("DV", 0)] * 2
    panel_free = [("PE", 0)] * 2
    set_free = [("DV", 0)] * 2
    h2_free = [[(f"HS{i}", 0)] for i in range(4)]
    gsc_free = [("DV", 0)] * 2
    vsc_free = [("DV", 0)] * 2
    sq_free = [("DV", 0)] * 2

    def quant_w_subtile(dst, dst_off, src_ap, sw_tile, u):
        slot = u % 3
        s, v = wstage_free[slot]
        p.wait(SY, s, v)
        vld = p.dma(SY, wstage[slot][:, :], src_ap, f"WS{slot}")
        ji = u % 2
        s, v = i8a_free[ji]
        p.wait(SC, s, v)
        p.wait(SC, f"WS{slot}", vld)
        vq = p.ins(SC, lambda E, sl2=slot, j2=ji, sw2=sw_tile: E.activation(
            i8a[j2][:, :], wstage[sl2][:, :], AF.Identity, scale=sw2[:, 0:1], bias=mqcol[:, 0:1]), "AC")
        wstage_free[slot] = ("AC", vq)
        p.wait(VE, "AC", vq)
        p.ins(VE, lambda E, j2=ji: E.tensor_scalar(clipa[:, :], i8a[j2][:, :], mqcol[:, 0:1], None, A.subtract))
        vw = p.ins(VE, lambda E, d=dst, o=dst_off: E.tensor_scalar(
            d[:, o:o + 512], clipa[:, :], -1.0, 1.0, A.max, A.min), "DV")
        i8a_free[ji] = ("DV", vw)
        return vw

    NPAIR = 16
    panel_pe = {}
    vwq_done = {}

    def emit_quant_unit(pb2, u2):
        buf2 = pb2 % 2
        if u2 == 0:
            p.wait(VE, "PE", panel_pe.get(pb2 - 2, 0))
        if u2 < NKB:
            vw = quant_w_subtile(wq_g[buf2], u2 * 512, wgt[u2 * 32 + pb2, :, :],
                                 swg, pb2 * 32 + u2)
        else:
            kb2 = u2 - NKB
            vw = quant_w_subtile(wq_v[buf2], kb2 * 512, wgt[kb2 * 32 + 16 + pb2, :, :],
                                 swg, pb2 * 32 + u2)
        vwq_done[pb2] = vw

    for u in range(32):
        emit_quant_unit(0, u)
    for pb in range(NPAIR):
        buf = pb % 2
        p.wait(TE, "DV", vwq_done[pb])
        if pb + 1 < NPAIR:
            for q in range(8):
                emit_quant_unit(pb + 1, q)
        vlast_mm = 0
        for ts in range(4):
            for half in range(2):
                hs = half * 2
                sf, vf = set_free[half]
                p.wait(TE, sf, vf)
                for gv in range(2):
                    wq = wq_g[buf] if gv == 0 else wq_v[buf]
                    for obh in range(2):
                        ob = hs + obh
                        bank = half * 4 + gv * 2 + obh
                        for kb in range(NKB):
                            vlast_mm = p.ins(TE, lambda E, b=bank, w=wq, k=kb, o=ob, t2=ts: E.matmul(
                                ps[b][:, :], w[:, k * 512 + o * 128:k * 512 + o * 128 + 128],
                                big1[:, k * OWN + t2 * 512:k * OWN + t2 * 512 + 512],
                                start=(k == 0), stop=(k == NKB - 1)),
                                "PE" if (kb == NKB - 1) else None)
                vlastdv = 0
                for obh in range(2):
                    ob = hs + obh
                    o_abs = pb * 4 + ob
                    bg_ = half * 4 + 0 * 2 + obh
                    bv_ = half * 4 + 1 * 2 + obh
                    jp = (ts * 2 + obh) % 2
                    p.wait(VE, "PE", vlast_mm)
                    s, v = gsc_free[jp]
                    p.wait(VE, s, v)
                    p.ins(VE, lambda E, j2=jp, b=bg_, t2=ts: E.tensor_mul(
                        gsc_t[j2][:, :], ps[b][:, :], sc1bc[:, t2 * 512:(t2 + 1) * 512]))
                    s, v = vsc_free[jp]
                    p.wait(VE, s, v)
                    vvs = p.ins(VE, lambda E, j2=jp, b=bv_, t2=ts: E.tensor_mul(
                        vsc_t[j2][:, :], ps[b][:, :], sc1bc[:, t2 * 512:(t2 + 1) * 512]), "DV")
                    p.wait(SC, "DV", vvs)
                    vsl = p.ins(SC, lambda E, j2=jp: E.activation(
                        gsc_t[j2][:, :], gsc_t[j2][:, :], AF.Silu), "AC")
                    hslot = (pb * 8 + ts * 2 + obh) % 4
                    for s, v in h2_free[hslot]:
                        p.wait(VE, s, v)
                    p.wait(VE, "AC", vsl)
                    vh2 = p.ins(VE, lambda E, j2=jp, h=hslot: E.tensor_mul(
                        h2ring[h][:, :], gsc_t[j2][:, :], vsc_t[j2][:, :]), "DV")
                    gsc_free[jp] = ("DV", vh2)
                    vsc_free[jp] = ("DV", vh2)
                    p.wait(SC, "DV", vh2)
                    vspl = p.dma(SC, d_h2[o_abs * 4 + ts, :, :],
                                 h2ring[hslot][:, :], f"HS{hslot}")
                    jq = jp
                    s, v = sq_free[jq]
                    p.wait(SC, s, v)
                    vsq3 = p.ins(SC, lambda E, j2=jq, h=hslot: E.activation(
                        sq_t[j2][:, :], h2ring[h][:, :], AF.Square), "AC")
                    p.wait(VE, "AC", vsq3)
                    p.ins(VE, lambda E, j2=jq, t2=ts: E.tensor_add(
                        accR2[:, t2 * 512:(t2 + 1) * 512], accR2[:, t2 * 512:(t2 + 1) * 512], sq_t[j2][:, :]))
                    vlastdv = p.ins(VE, lambda E, j2=jq, t2=ts, oa=o_abs: E.scalar_tensor_tensor(
                        accM2[:, t2 * 512:(t2 + 1) * 512], sq_t[j2][:, :], gd2col[:, oa:oa + 1],
                        accM2[:, t2 * 512:(t2 + 1) * 512], A.mult, A.max), "DV")
                    sq_free[jq] = ("DV", vlastdv)
                    h2_free[hslot] = [(f"HS{hslot}", vspl), ("AC", vsq3)]
                set_free[half] = ("DV", vlastdv)
                if pb + 1 < NPAIR:
                    g = ts * 2 + half
                    if g < 6:
                        for q in range(4):
                            emit_quant_unit(pb + 1, 8 + g * 4 + q)
        panel_pe[pb] = vlast_mm
    v_a1 = p.cur("DV")

    for c in range(16):
        p.wait(TE, "DV", v_a1)
        p.use_bank(2)
        vt1 = p.ins(TE, lambda E, c2=c: E.transpose(ps[2][:, 0:128], accR2[:, c2 * 128:(c2 + 1) * 128], c_id[:, :]), "PE")
        p.use_bank(3)
        vt2 = p.ins(TE, lambda E, c2=c: E.transpose(ps[3][:, 0:128], accM2[:, c2 * 128:(c2 + 1) * 128], c_id[:, :]), "PE")
        p.wait(VE, "PE", vt1)
        vrd1 = p.ins(VE, lambda E, c2=c: E.reduce_sum(stat_cols[:, c2:c2 + 1], ps[2][:, 0:128], AX), "DV")
        p.free_bank(2, "DV", vrd1)
        p.wait(VE, "PE", vt2)
        vrd2 = p.ins(VE, lambda E, c2=c: E.reduce_max(stat_cols[:, 16 + c2:17 + c2], ps[3][:, 0:128], AX), "DV")
        p.free_bank(3, "DV", vrd2)
    vst = p.ins(VE, lambda E: E.tensor_scalar(stat_cols[:, 0:16], stat_cols[:, 0:16], 1.0 / DHALF, None, A.mult), "DV")
    p.wait(SC, "DV", vst)
    vsr2 = p.ins(SC, lambda E: E.activation(stat_cols[:, 0:16], stat_cols[:, 0:16], AF.Sqrt, bias=epscol[:, 0:1]), "AC")
    p.wait(VE, "AC", vsr2)
    p.ins(VE, lambda E: E.reciprocal(stat_cols[:, 0:16], stat_cols[:, 0:16]))
    vms = p.ins(VE, lambda E: E.engine_nop(), "DV")
    p.wait(SC, "DV", vms)
    vsm2 = p.ins(SC, lambda E: E.activation(stat_cols[:, 16:32], stat_cols[:, 16:32], AF.Sqrt), "AC")
    p.wait(VE, "AC", vsm2)
    p.ins(VE, lambda E: E.tensor_mul(stat_cols[:, 16:32], stat_cols[:, 16:32], stat_cols[:, 0:16]))
    p.ins(VE, lambda E: E.tensor_scalar(stat_cols[:, 16:32], stat_cols[:, 16:32], 1e-5, None, A.max))
    p.ins(VE, lambda E: E.reciprocal(stat_cols[:, 32:48], stat_cols[:, 16:32]))
    vf2c = p.ins(VE, lambda E: E.tensor_scalar(
        mxcols[:, 0:16], stat_cols[:, 32:48], 127.0, None, A.mult), "DV")
    p.ins(VE, lambda E: E.tensor_mul(mxcols[:, 0:16], mxcols[:, 0:16], stat_cols[:, 0:16]))
    vf2c = p.ins(VE, lambda E: E.engine_nop(), "DV")
    v_row_f2 = row_put(vf2c, mxcols[:, 0:16], Q_F2)
    vp2c = p.ins(VE, lambda E: E.tensor_scalar(
        mxcols[:, 0:16], stat_cols[:, 16:32], swd[:, 1:2], 1.0 / 127.0, A.mult, A.mult), "DV")
    v_row_p2 = row_put(vp2c, mxcols[:, 0:16], Q_P2)

    p.sync_phase(sems=("SW", "SA", "DV", "AC", "PE"))
    a1.close()

    a2 = ExitStack()
    ab2 = lambda name, shape, dtyp: a2.enter_context(nc.sbuf_tensor(name, shape, dtyp))
    wd_st = [ab2(f"wd_st{i}", [128, 512], F32) for i in range(2)]
    wd_q = [ab2(f"wd_q{i}", [128, 512], BF16) for i in range(2)]
    i8b = [ab2(f"i8b{i}", [128, 512], F32) for i in range(2)]
    clipb = ab2("clipb", [128, 512], F32)
    h2f = [ab2(f"h2f{i}", [128, 1024], F32) for i in range(4)]
    h2qs = [ab2(f"h2qs{i}", [128, 1024], BF16) for i in range(4)]
    h2qr = [ab2(f"h2qr{i}", [128, 512], BF16) for i in range(4)]
    i32b = [ab2(f"i32b{i}", [128, 1024], F32) for i in range(4)]
    preb = [ab2(f"preb{i}", [128, 1024], F32) for i in range(4)]
    bc3g = ab2("bc3g", [128, 2048], F32)
    bc4g = ab2("bc4g", [128, 2048], F32)

    for t4 in range(4):
        bc_row(Q_F2, t4 * 512, 512, bc3g, t4 * 512, v_row_f2, 0)
        bc_row(Q_P2, t4 * 512, 512, bc4g, t4 * 512, v_row_p2, 1)
    v_a2bc = p.cur("DV")
    p.wait(TE, "DV", v_a2bc)

    wdst_free = [("AC", 0)] * 2
    wdq_free = [("PE", 0)] * 2
    i8b_free = [("DV", 0)] * 2
    h2f_free = [("DV", 0)] * 4
    h2qs_free = [(f"HQ{i}", 0) for i in range(4)]
    h2qr_free = [("PE", 0)] * 4
    i32b_free = [("DV", 0)] * 4
    preb_free = [("AC", 0)] * 4
    s_evac_done = 0
    for st_ in range(4):
        if st_ == 1:
            for eng in (GP, SC):
                for hq in range(4):
                    p.wait(eng, f"HQ{hq}", p.cur(f"HQ{hq}"))
        for th in range(2):
            vlast_mm = 0
            for kb in range(NDKB):
                wslot = kb % 2
                s, v = wdst_free[wslot]
                p.wait(SY, s, v)
                vld = p.dma(SY, wd_st[wslot][:, :], wdt[kb * 4 + st_, :, :], f"WD{wslot}")
                s, v = i8b_free[wslot]
                p.wait(SC, s, v)
                p.wait(SC, f"WD{wslot}", vld)
                vq = p.ins(SC, lambda E, w=wslot: E.activation(
                    i8b[w][:, :], wd_st[w][:, :], AF.Identity, scale=swd[:, 0:1], bias=mqcol[:, 0:1]), "AC")
                wdst_free[wslot] = ("AC", vq)
                s, v = wdq_free[wslot]
                p.wait(VE, s, v)
                p.wait(VE, "AC", vq)
                p.ins(VE, lambda E, w=wslot: E.tensor_scalar(clipb[:, :], i8b[w][:, :], mqcol[:, 0:1], None, A.subtract))
                vqd = p.ins(VE, lambda E, w=wslot: E.tensor_scalar(wd_q[w][:, :], clipb[:, :], -1.0, 1.0, A.max, A.min), "DV")
                i8b_free[wslot] = ("DV", vqd)
                p.wait(TE, "DV", vqd)
                if st_ == 0:
                    hslot = kb % 4
                    heng = GP if hslot % 2 == 0 else SC
                    s, v = h2f_free[hslot]
                    p.wait(heng, s, v)
                    for j in range(2):
                        vh = p.dma(heng, h2f[hslot][:, j * 512:(j + 1) * 512],
                                   d_h2[kb * 4 + th * 2 + j, :, :], f"GH2{hslot}")
                    s, v = preb_free[hslot]
                    p.wait(VE, s, v)
                    p.wait(VE, f"GH2{hslot}", vh)
                    vpre = p.ins(VE, lambda E, h=hslot, k=kb, t2=th: E.scalar_tensor_tensor(
                        preb[h][:, :], h2f[h][:, :], gdcol[:, k:k + 1], bc3g[:, t2 * 1024:(t2 + 1) * 1024],
                        A.mult, A.mult), "DV")
                    h2f_free[hslot] = ("DV", vpre)
                    s, v = i32b_free[hslot]
                    p.wait(SC, s, v)
                    p.wait(SC, "DV", vpre)
                    vcv = p.ins(SC, lambda E, h=hslot: E.activation(
                        i32b[h][:, :], preb[h][:, :], AF.Identity, bias=mqcol[:, 0:1]), "AC")
                    preb_free[hslot] = ("AC", vcv)
                    s, v = h2qs_free[hslot]
                    p.wait(VE, s, v)
                    p.wait(VE, "AC", vcv)
                    vqq = p.ins(VE, lambda E, h=hslot: E.tensor_scalar(
                        h2qs[h][:, :], i32b[h][:, :], mqcol[:, 0:1], None, A.subtract), "DV")
                    i32b_free[hslot] = ("DV", vqq)
                    p.wait(SC, "DV", vqq)
                    for j in range(2):
                        vsp = p.dma(SC, d_h2q[kb * 4 + th * 2 + j, :, :],
                                    h2qs[hslot][:, j * 512:(j + 1) * 512], f"HQ{hslot}")
                    h2qs_free[hslot] = (f"HQ{hslot}", vsp)
                    p.wait(TE, "DV", vqq)
                    for w2 in range(2):
                        bank = (kb % 1) * 0
                        for hb in range(4):
                            b = hb * 2 + w2
                            vlast_mm = p.ins(TE, lambda E, b2=b, w=wslot, h=hslot, h3=hb, w3=w2, k=kb: E.matmul(
                                ps[b2][:, :], wd_q[w][:, h3 * 128:(h3 + 1) * 128],
                                h2qs[h][:, w3 * 512:(w3 + 1) * 512],
                                start=(k == 0), stop=(k == NDKB - 1)), "PE" if hb == 3 else None)
                else:
                    for w2 in range(2):
                        rslot = (kb * 2 + w2) % 4
                        reng = (GP, GP, SC, GP)[rslot]
                        s, v = h2qr_free[rslot]
                        p.wait(reng, s, v)
                        vh = p.dma(reng, h2qr[rslot][:, :],
                                   d_h2q[kb * 4 + th * 2 + w2, :, :],
                                   f"GHQ{rslot}")
                        p.wait(TE, f"GHQ{rslot}", vh)
                        for hb in range(4):
                            b = hb * 2 + w2
                            vlast_mm = p.ins(TE, lambda E, b2=b, w=wslot, r=rslot, h3=hb, k=kb: E.matmul(
                                ps[b2][:, :], wd_q[w][:, h3 * 128:(h3 + 1) * 128], h2qr[r][:, :],
                                start=(k == 0), stop=(k == NDKB - 1)), "PE" if hb == 3 else None)
                        h2qr_free[rslot] = ("PE", vlast_mm)
                wdq_free[wslot] = ("PE", vlast_mm)
            p.wait(VE, "PE", vlast_mm)
            for w2 in range(2):
                for hb in range(4):
                    b = hb * 2 + w2
                    ho = st_ * 4 + hb
                    s_evac_done = p.ins(VE, lambda E, b2=b, h3=ho, t2=th, w3=w2: E.tensor_mul(
                        big1[:, h3 * OWN + t2 * 1024 + w3 * 512:h3 * OWN + t2 * 1024 + w3 * 512 + 512],
                        ps[b2][:, :], bc4g[:, t2 * 1024 + w3 * 512:t2 * 1024 + w3 * 512 + 512]), "DV")
            p.wait(TE, "DV", s_evac_done)
    v_a2 = p.cur("DV")

    p.sync_phase(sems=("SW", "SG", "SA", "DV", "AC", "PE"))
    a2.close()

    p.wait(SY, "SG", v_th)
    p.wait(SY, "PL", v_bis)
    p.ins(VE, lambda E: E.memset(th_row[:1, :], 0.0))
    p.wait(SY, "DV", p.cur("DV"))
    vthl = p.dma(SY, th_row[:1, 0:8], d_small[3:4, 0:8], "SW")
    p.wait(TE, "SW", vthl)
    p.use_bank(0)
    vthb = p.ins(TE, lambda E: E.matmul(ps[0][:, 0:16], ones_row[:1, :], th_row[:1, 0:16], start=True, stop=True), "PE")
    p.wait(VE, "PE", vthb)
    vthe = p.ins(VE, lambda E: E.tensor_copy(out=th_cols[:, :], in_=ps[0][:, 0:8]), "DV")
    p.free_bank(0, "DV", vthe)
    for e in range(NE):
        p.ins(VE, lambda E, e2=e: E.scalar_tensor_tensor(
            tmpA[:, 0:16], sm8[:, e2 * 128:e2 * 128 + 16], th_cols[:, e2:e2 + 1],
            sm8[:, e2 * 128:e2 * 128 + 16], A.is_ge, A.mult))
        if e == 0:
            p.ins(VE, lambda E: E.tensor_copy(out=mxcols[:, 0:16], in_=tmpA[:, 0:16]))
        else:
            p.ins(VE, lambda E: E.tensor_add(mxcols[:, 0:16], mxcols[:, 0:16], tmpA[:, 0:16]))
    vco = p.ins(VE, lambda E: E.engine_nop(), "DV")
    v_row_co = row_put(vco, mxcols[:, 0:16], Q_CO)

    p.sync_phase()
    es2.close()

    f1 = ExitStack()
    fb_ = lambda name, shape, dtyp: f1.enter_context(nc.sbuf_tensor(name, shape, dtyp))
    xnt_res = fb_("xnt_res", [128, NKB * 2048], BF16)
    fw_st = [fb_(f"fw_st{i}", [128, 128], F32) for i in range(4)]
    fq_g = [fb_(f"fq_g{i}", [128, NKB * 128], BF16) for i in range(2)]
    fq_u = [fb_(f"fq_u{i}", [128, NKB * 128], BF16) for i in range(2)]
    silf = [fb_(f"silf{i}", [128, 512], F32) for i in range(2)]
    ubuf = [fb_(f"ubuf{i}", [128, 512], F32) for i in range(2)]
    midb = [fb_(f"midb{i}", [128, 512], BF16) for i in range(4)]

    p.wait(GP, "XB0", p.cur("XB0"))
    p.wait(GP, "XB1", p.cur("XB1"))
    vxa_kb = []
    for kb in range(NKB):
        for sl in range(4):
            vxa = p.dma(GP, xnt_res[:, kb * 2048 + sl * 512:kb * 2048 + (sl + 1) * 512],
                        d_xnt[kb * 4 + sl, :, :], "GXA")
        vxa_kb.append(vxa)

    fwst_free = [("DV", 0)] * 4
    fbank_free = [("DV", 0)] * 8
    silf_free = [("DV", 0)] * 2
    ubuf_free = [("DV", 0)] * 2
    midb_free = [(f"MB{i}", 0) for i in range(4)]
    f1_spill = {}
    fcvt_done = {}
    fpanel_pe = {}

    def emit_fcvt_unit(pc2, u2):
        buf2 = pc2 % 2
        if u2 == 0:
            p.wait(VE, "PE", fpanel_pe.get(pc2 - 2, 0))
        gu = u2 // NKB
        kb2 = u2 % NKB
        src = fwg if gu == 0 else fwu
        dstp = fq_g[buf2] if gu == 0 else fq_u[buf2]
        slot = (pc2 * 32 + u2) % 4
        s, v = fwst_free[slot]
        p.wait(SY, s, v)
        vld = p.dma(SY, fw_st[slot][:, :], src[kb2 * NFKB + pc2, :, :], f"FW{slot}")
        p.wait(VE, f"FW{slot}", vld)
        vcvt = p.ins(VE, lambda E, sl2=slot, d=dstp, k=kb2: E.tensor_copy(
            out=d[:, k * 128:(k + 1) * 128], in_=fw_st[sl2][:, :]), "DV")
        fwst_free[slot] = ("DV", vcvt)
        fcvt_done[pc2] = vcvt

    for u in range(32):
        emit_fcvt_unit(0, u)
    for pc in range(NFKB):
        buf = pc % 2
        p.wait(TE, "DV", fcvt_done[pc])
        if pc + 1 < NFKB:
            for q in range(16):
                emit_fcvt_unit(pc + 1, q)
        for ts in range(4):
            vlast_mm = 0
            for gu in range(2):
                bank = ts * 2 + gu
                s, v = fbank_free[bank]
                p.wait(TE, s, v)
                wqp = fq_g[buf] if gu == 0 else fq_u[buf]
                for kb in range(NKB):
                    if pc == 0 and gu == 0:
                        p.wait(TE, "GXA", vxa_kb[kb])
                    vlast_mm = p.ins(TE, lambda E, b=bank, w=wqp, k=kb, t2=ts: E.matmul(
                        ps[b][:, :], w[:, k * 128:(k + 1) * 128],
                        xnt_res[:, k * 2048 + t2 * 512:k * 2048 + t2 * 512 + 512],
                        start=(k == 0), stop=(k == NKB - 1)),
                        "PE" if kb == NKB - 1 else None)
            jp = ts % 2
            p.wait(SC, "PE", vlast_mm)
            s, v = silf_free[jp]
            p.wait(SC, s, v)
            vslf = p.ins(SC, lambda E, j2=jp, b=ts * 2, f2=pc: E.activation(
                silf[j2][:, :], ps[b][:, :], AF.Silu, bias=bgcol[:, f2:f2 + 1]), "AC")
            s, v = ubuf_free[jp]
            p.wait(SC, s, v)
            vub = p.ins(SC, lambda E, j2=jp, b=ts * 2 + 1, f2=pc: E.activation(
                ubuf[j2][:, :], ps[b][:, :], AF.Identity, bias=bucol[:, f2:f2 + 1]), "AC")
            mslot = ts
            s, v = midb_free[mslot]
            p.wait(VE, s, v)
            p.wait(VE, "AC", vub)
            vmd = p.ins(VE, lambda E, j2=jp, m=mslot: E.tensor_mul(
                midb[m][:, :], silf[j2][:, :], ubuf[j2][:, :]), "DV")
            silf_free[jp] = ("DV", vmd)
            ubuf_free[jp] = ("DV", vmd)
            fbank_free[ts * 2] = ("DV", vmd)
            fbank_free[ts * 2 + 1] = ("DV", vmd)
            p.wait(SC, "DV", vmd)
            vmsp = p.dma(SC, d_mid[pc * 4 + ts, :, :], midb[mslot][:, :], f"MB{mslot}")
            midb_free[mslot] = (f"MB{mslot}", vmsp)
            f1_spill[(pc, ts)] = vmsp
            if pc + 1 < NFKB and ts < 2:
                for q in range(8):
                    emit_fcvt_unit(pc + 1, 16 + ts * 8 + q)
        fpanel_pe[pc] = vlast_mm

    p.sync_phase(sems=("SW", "SG", "SA", "DV", "AC", "PE"))
    f1.close()

    f2 = ExitStack()
    fb2_ = lambda name, shape, dtyp: f2.enter_context(nc.sbuf_tensor(name, shape, dtyp))
    mid_res = fb2_("mid_res", [128, NFKB * 1024], BF16)
    fd_st = [fb2_(f"fd_st{i}", [128, 512], F32) for i in range(2)]
    fd_q = [fb2_(f"fd_q{i}", [128, 512], BF16) for i in range(2)]
    xres = [fb2_(f"xres{i}", [128, 512], F32) for i in range(2)]
    outt = [fb2_(f"outt{i}", [128, 512], F32) for i in range(2)]
    gbuf = [fb2_(f"gbuf{i}", [128, 512], F32) for i in range(2)]
    cobc = fb2_("cobc", [128, 2048], F32)

    for t4 in range(4):
        bc_row(Q_CO, t4 * 512, 512, cobc, t4 * 512, v_row_co, 0)
    v_f2bc = p.cur("DV")
    p.wait(TE, "DV", v_f2bc)

    fdst_free = [("DV", 0)] * 2
    fdq_free = [("PE", 0)] * 2
    xres_free = [("DV", 0)] * 2
    outt_free = [("OUT", 0)] * 2
    gbuf_free = [("DV", 0)] * 2
    bank_ac = {}
    for th in range(2):
        if th == 1:
            p.wait(SC, "PE", p.cur("PE"))
        vml_kb = []
        for kb in range(NFKB):
            for j in range(2):
                p.wait(SC, f"MB{th * 2 + j}", f1_spill[(kb, th * 2 + j)])
                vml = p.dma(SC, mid_res[:, kb * 1024 + j * 512:kb * 1024 + (j + 1) * 512],
                            d_mid[kb * 4 + th * 2 + j, :, :], "GMI")
            vml_kb.append(vml)
        fd_cvt = {}

        def emit_fd_unit(st2, kb2):
            key = (st2, kb2)
            if key in fd_cvt:
                return fd_cvt[key]
            wslot = kb2 % 2
            weng = SY if wslot == 0 else GP
            s, v = fdst_free[wslot]
            p.wait(weng, s, v)
            vld = p.dma(weng, fd_st[wslot][:, :], fwd[kb2 * 4 + st2, :, :], f"FD{wslot}")
            s, v = fdq_free[wslot]
            p.wait(VE, s, v)
            p.wait(VE, f"FD{wslot}", vld)
            vcv = p.ins(VE, lambda E, w=wslot: E.tensor_copy(out=fd_q[w][:, :], in_=fd_st[w][:, :]), "DV")
            fdst_free[wslot] = ("DV", vcv)
            fd_cvt[key] = vcv
            return vcv

        for st_ in range(4):
            vlast_mm = 0
            for kb in range(NFKB):
                wslot = kb % 2
                vcv = emit_fd_unit(st_, kb)
                p.wait(TE, "DV", vcv)
                if st_ == 0:
                    p.wait(TE, "GMI", vml_kb[kb])
                for w2 in range(2):
                    for hb in range(4):
                        b = hb * 2 + w2
                        if kb == 0:
                            p.wait(TE, "AC", bank_ac.get(b, 0))
                        vlast_mm = p.ins(TE, lambda E, b2=b, w=wslot, h3=hb, k=kb, w3=w2: E.matmul(
                            ps[b2][:, :], fd_q[w][:, h3 * 128:(h3 + 1) * 128],
                            mid_res[:, k * 1024 + w3 * 512:k * 1024 + w3 * 512 + 512],
                            start=(k == 0), stop=(k == NFKB - 1)), "PE" if (w2 == 1 and hb == 3) else None)
                fdq_free[wslot] = ("PE", vlast_mm)
            if st_ < 3:
                emit_fd_unit(st_ + 1, 0)
                emit_fd_unit(st_ + 1, 1)
            p.wait(SC, "PE", vlast_mm)
            for w2 in range(2):
                for hb in range(4):
                    b = hb * 2 + w2
                    ho = st_ * 4 + hb
                    toff = th * 1024 + w2 * 512
                    jg = (w2 * 4 + hb) % 2
                    xslot = jg
                    s, v = xres_free[xslot]
                    p.wait(GP, s, v)
                    vxl = p.dma(GP, xres[xslot][:, :], xt[ho * 32 + th * 2 + w2, :, :], f"XE{xslot}")
                    s, v = gbuf_free[jg]
                    p.wait(SC, s, v)
                    vg1 = p.ins(SC, lambda E, j2=jg, b2=b, h3=ho: E.activation(
                        gbuf[j2][:, :], ps[b2][:, :], AF.Identity, bias=bdcol[:, h3:h3 + 1]), "AC")
                    bank_ac[b] = vg1
                    p.wait(VE, "AC", vg1)
                    vg2 = p.ins(VE, lambda E, j2=jg, t2=toff: E.tensor_mul(
                        gbuf[j2][:, :], gbuf[j2][:, :], cobc[:, t2:t2 + 512]), "DV")
                    p.wait(SC, "DV", vg2)
                    vg3 = p.ins(SC, lambda E, j2=jg: E.activation(gbuf[j2][:, :], gbuf[j2][:, :], AF.Silu), "AC")
                    oslot = jg
                    s, v = outt_free[oslot]
                    p.wait(VE, s, v)
                    p.wait(VE, "AC", vg3)
                    p.ins(VE, lambda E, j2=jg, o=oslot, h3=ho, t2=toff: E.tensor_mul(
                        outt[o][:, :], gbuf[j2][:, :], big1[:, h3 * OWN + t2:h3 * OWN + t2 + 512]))
                    p.wait(VE, f"XE{xslot}", vxl)
                    vo = p.ins(VE, lambda E, o=oslot, x=xslot: E.tensor_add(
                        outt[o][:, :], outt[o][:, :], xres[x][:, :]), "DV")
                    xres_free[xslot] = ("DV", vo)
                    gbuf_free[jg] = ("DV", vo)
                    p.wait(SC, "DV", vo)
                    vout = p.dma(SC, yt[ho * 128:(ho + 1) * 128, toff:toff + 512], outt[oslot][:, :], f"OT{oslot}")
                    outt_free[oslot] = (f"OT{oslot}", vout)

    p.wait(SY, "OT0", p.cur("OT0"))
    p.wait(SY, "OT1", p.cur("OT1"))
    f2.close()

    semstack = ExitStack()
    for name in sorted(set(list(Plan.SEMS) + list(p.cnt.keys()))):
        p.sem[name] = semstack.enter_context(nc.semaphore(name))
    with nc.Block() as block:
        @block.sync
        def _(E):
            for f in p.ops["sync"]:
                f(E)

        @block.vector
        def _(E):
            for f in p.ops["vector"]:
                f(E)

        @block.scalar
        def _(E):
            for f in p.ops["scalar"]:
                f(E)

        @block.tensor
        def _(E):
            for f in p.ops["tensor"]:
                f(E)

        @block.gpsimd
        def _(E):
            for f in p.ops["gpsimd"]:
                f(E)

    semstack.close()
    pse.close()
    es.close()
    return nc


def kernel(L0_attn_x, L0_attn_w_gate, L0_attn_g_gate, L0_attn_w_down, L0_attn_g_down,
           L0_ffn_router_w, L0_ffn_router_b, L0_ffn_w_gate, L0_ffn_b_gate,
           L0_ffn_w_up, L0_ffn_b_up, L0_ffn_w_down, L0_ffn_b_down):
    f32 = np.float32

    def tiles(a, pr, pc_):
        nr = a.shape[0] // 128
        ncol = a.shape[1] // pc_
        return np.ascontiguousarray(
            a.reshape(nr, 128, ncol, pc_).transpose(0, 2, 1, 3).reshape(nr * ncol, 128, pc_))

    x = np.ascontiguousarray(np.asarray(L0_attn_x, f32).reshape(NT, H))
    xT = np.ascontiguousarray(x.T)
    wgT = np.asarray(L0_attn_w_gate, f32).T
    wdT = np.asarray(L0_attn_w_down, f32).T
    wgt_t = tiles(np.ascontiguousarray(wgT), 128, 512)
    wdt_t = tiles(np.ascontiguousarray(wdT), 128, 512)
    fwg = np.zeros((H, DFFP), f32)
    fwg[:, :DFF] = np.asarray(L0_ffn_w_gate, f32)
    fwu = np.zeros((H, DFFP), f32)
    fwu[:, :DFF] = np.asarray(L0_ffn_w_up, f32)
    fwd = np.zeros((DFFP, H), f32)
    fwd[:DFF, :] = np.asarray(L0_ffn_w_down, f32)
    fwg_t = tiles(fwg, 128, 128)
    fwu_t = tiles(fwu, 128, 128)
    fwd_t = tiles(fwd, 128, 512)
    bgp = np.zeros((DFFP,), f32)
    bgp[:DFF] = np.asarray(L0_ffn_b_gate, f32)
    bup = np.zeros((DFFP,), f32)
    bup[:DFF] = np.asarray(L0_ffn_b_up, f32)

    ggate2 = np.ascontiguousarray(np.asarray(L0_attn_g_gate, f32).reshape(NKB, 128).T)
    gdown2 = np.ascontiguousarray(np.asarray(L0_attn_g_down, f32).reshape(NDKB, 128).T)
    bg2 = np.ascontiguousarray(bgp.reshape(NFKB, 128).T)
    bu2 = np.ascontiguousarray(bup.reshape(NFKB, 128).T)
    bd2 = np.ascontiguousarray(np.asarray(L0_ffn_b_down, f32).reshape(NKB, 128).T)
    rb = np.asarray(L0_ffn_router_b, f32).reshape(1, NE)
    rw = np.asarray(L0_ffn_router_w, f32)
    rwa1 = np.zeros((128, NKB * 16), f32)
    rwa2 = np.zeros((128, NKB * 16), f32)
    for kb in range(NKB):
        rwa1[:, kb * 16:kb * 16 + 8] = rw[kb * 128:(kb + 1) * 128, :]
        rwa1[:, kb * 16 + 8] = 1.0
        rwa2[:, kb * 16 + 9] = 1.0
    ident = np.eye(128, dtype=f32)

    gh, gl = _gumbel_tables()

    nc = build_program_synced()
    in_maps = []
    for c in range(NCORE):
        sh = c * OWN
        xTc = np.concatenate([xT[:, sh:], xT[:, :sh]], axis=1)
        xtc_t = tiles(np.ascontiguousarray(xTc), 128, 512)
        ghc = np.concatenate([gh[sh:], gh[:sh]], axis=0)
        glc = np.concatenate([gl[sh:], gl[:sh]], axis=0)
        gmhc = np.ascontiguousarray(ghc.reshape(128, 128, NE).transpose(2, 1, 0))
        gmlc = np.ascontiguousarray(glc.reshape(128, 128, NE).transpose(2, 1, 0))
        in_maps.append({
            "xt": xtc_t, "gmh": gmhc, "gml": gmlc, "wgt": wgt_t, "wdt": wdt_t,
            "fwg": fwg_t, "fwu": fwu_t, "fwd": fwd_t, "ggate2": ggate2, "gdown2": gdown2,
            "rwa1": rwa1, "rwa2": rwa2, "bg2": bg2, "bu2": bu2, "bd2": bd2,
            "rbrow": rb, "identd": ident,
        })
    res = run_bass_kernel_spmd(nc, in_maps, core_ids=list(range(NCORE)))
    out = np.empty((NT, H), f32)
    for c in range(NCORE):
        out[c * OWN:(c + 1) * OWN, :] = res.results[c]["yt"].T
    return out.reshape(4, 4096, 2048)


def build_program_synced():
    rec = {}
    build_program(rec=rec)
    dep = compute_edges(rec)
    return build_program(dep=dep)


if __name__ == "__main__":
    prog = build_program_synced()
    n = sum(1 for _ in prog.all_instructions())
    print("built ok, instructions:", n)


# version 10 (final)
# speedup vs baseline: 1.2286x; 1.0241x over previous
import os
import numpy as np
import ml_dtypes
import concourse.bass as bass
import concourse.mybir as mybir
from concourse.bass_utils import run_bass_kernel_spmd
from contextlib import ExitStack

F32 = mybir.dt.float32
F32R = mybir.dt.float32r
BF16 = mybir.dt.bfloat16
I32 = mybir.dt.int32
I8 = mybir.dt.int8
A = mybir.AluOpType
AF = mybir.ActivationFunctionType
AX = mybir.AxisListType.X

NCORE = 8
NT = 16384
OWN = 2048
H = 2048
NKB = 16
DG = 16384
DHALF = 8192
NDKB = 64
DFF = 5461
DFFP = 5632
NFKB = 44
NE = 8
KCAP = 2048
M32 = 12582912.0
M64 = 6755399441055744.0
N_BISECT = 36
EPS_LN = 1e-5
EPS_G = 1e-10
INV_H = 1.0 / 2048.0
NWG = float(DG * H)
NWD = float(DHALF * H)

Q_R, Q_MR, Q_RS, Q_SQ, Q_SC1, Q_F2, Q_P2, Q_CO = range(8)


def _gumbel_tables():
    import jax
    import jax.numpy as jnp
    cpu = jax.devices("cpu")[0]
    fp64 = os.environ.get("BASS_SIM_FP64") == "1"
    with jax.default_device(cpu):
        key = jax.random.key(42, impl="threefry2x32")
        if fp64:
            from jax.experimental import enable_x64
            with enable_x64():
                u = jax.random.uniform(key, (NT, NE), dtype=jnp.float64)
                g = np.asarray(-jnp.log(-jnp.log(u + EPS_G) + EPS_G), dtype=np.float64)
            hi = g.astype(np.float32)
            lo = (g - hi.astype(np.float64)).astype(np.float32)
        else:
            u = jax.random.uniform(key, (NT, NE), dtype=jnp.float32)
            g = -jnp.log(-jnp.log(u + np.float32(EPS_G)) + np.float32(EPS_G))
            hi = np.asarray(g, dtype=np.float32)
            lo = np.zeros_like(hi)
    return hi, lo


class Plan:
    ENGS = ("sync", "vector", "scalar", "tensor", "gpsimd")
    SEMS = ("SW", "SG", "SA", "DV", "AC", "PE", "PL", "OUT")
    ESEM = {"vector": "DV", "scalar": "AC", "gpsimd": "PL"}

    def __init__(self, dep=None, rec=None):
        self.ops = {e: [] for e in self.ENGS}
        self.sem = {}
        self.cnt = {}
        self.hw = {e: {} for e in self.ENGS}
        self.bankf = {}
        self.dep = dep
        self.rec = rec
        self.opidx = {e: 0 for e in self.ENGS}

    def _dep_waits(self, eng, idx):
        if self.dep is not None:
            for s, v in self.dep.get((eng, idx), ()):
                self.wait(eng, s, v)

    def _mk_recorder(self, fn, eng, idx, sem, val):
        rec = self.rec
        def run(E, f=fn, key=(eng, idx), sv=(sem, val)):
            inst = f(E)
            if rec is not None:
                try:
                    mi = inst.ins
                    rd = tuple(getattr(x, "memref", None) for x in mi.ins)
                    wr = tuple(getattr(x, "memref", None) for x in mi.outs)
                    rec[key] = (tuple(n for n in rd if n), tuple(n for n in wr if n), sv)
                except Exception:
                    pass
            return inst
        return run

    def use_bank(self, b, eng="tensor"):
        s_v = self.bankf.get(b)
        if s_v:
            self.wait(eng, s_v[0], s_v[1])

    def free_bank(self, b, sem, val):
        self.bankf[b] = (sem, val)

    def emit(self, eng, fn):
        if os.environ.get("KBUILD_DEBUG"):
            import traceback
            site = traceback.extract_stack()[-3]
            def wrapped(E, f=fn, s=site):
                try:
                    return f(E)
                except Exception:
                    print(f"PLAN-SITE: {s.filename}:{s.lineno} {s.line}")
                    raise
            self.ops[eng].append(wrapped)
        else:
            self.ops[eng].append(fn)

    def bump(self, name, k=1):
        self.cnt[name] = self.cnt.get(name, 0) + k
        return self.cnt[name]

    def cur(self, name):
        return self.cnt.get(name, 0)

    def wait(self, eng, sem, val):
        if not val or val <= 0:
            return
        if self.hw[eng].get(sem, 0) >= val:
            return
        self.hw[eng][sem] = val
        self.emit(eng, lambda E, s=sem, v=val: E.wait_ge(self.sem[s], v))

    def dma(self, eng, out, in_, sem, slow=False, ordered=False):
        idx = self.opidx[eng]
        self.opidx[eng] += 1
        self._dep_waits(eng, idx)
        if ordered:
            self.wait(eng, sem, self.cur(sem))
        v = self.bump(sem, 16)
        fn = lambda E, o=out, i=in_, s=sem, sl=slow: E.dma_start(
            out=o, in_=i, allow_slow_non_contiguous=sl).then_inc(self.sem[s], 16)
        self.emit(eng, self._mk_recorder(fn, eng, idx, sem, v))
        return v

    def ins(self, eng, fn, sem=None):
        idx = self.opidx[eng]
        self.opidx[eng] += 1
        self._dep_waits(eng, idx)
        auto = self.ESEM.get(eng)
        if auto is not None:
            sem = auto
        if sem is None:
            self.emit(eng, self._mk_recorder(fn, eng, idx, None, None))
            return None
        v = self.bump(sem)
        fn2 = lambda E, f=fn, s=sem: f(E).then_inc(self.sem[s], 1)
        self.emit(eng, self._mk_recorder(fn2, eng, idx, sem, v))
        return v

    def sync_phase(self, sems=None, engs=None):
        sems = sems or list(self.cnt.keys())
        snap = [(s, self.cur(s)) for s in sems]
        for eng in (engs or self.ENGS):
            for s, v in snap:
                self.wait(eng, s, v)


def v3(ap):
    return ap.rearrange("p (c o) -> p c o", o=1)


def compute_edges(rec):
    dep = {}
    streams = {}
    for (eng, idx) in rec:
        streams.setdefault(eng, []).append(idx)
    for eng, idxs in streams.items():
        if eng == "tensor":
            continue
        track_w = {}
        track_r = {}
        for idx in sorted(idxs):
            rd, wr, (sem, val) = rec[(eng, idx)]
            need = {}
            for n in rd:
                for s, v in track_w.get(n, {}).items():
                    if need.get(s, 0) < v:
                        need[s] = v
            for n in wr:
                for d in (track_w.get(n, {}), track_r.get(n, {})):
                    for s, v in d.items():
                        if need.get(s, 0) < v:
                            need[s] = v
            if sem is not None:
                need.pop(None, None)
                need = {s: v for s, v in need.items() if not (s == sem and v >= val)}
            if need:
                dep[(eng, idx)] = tuple(need.items())
            if sem is not None:
                for n in rd:
                    track_r.setdefault(n, {})[sem] = val
                for n in wr:
                    track_w[n] = {sem: val}
                    track_r[n] = {}
    return dep


def build_program(dep=None, rec=None):
    nc = bass.Bass("TRN2", target_bir_lowering=False, detect_race_conditions=False)
    dt = nc.dram_tensor

    xt = dt("xt", [NKB * 32, 128, 512], F32, kind="ExternalInput").ap()
    gmh = dt("gmh", [NE, 128, 128], F32, kind="ExternalInput").ap()
    gml = dt("gml", [NE, 128, 128], F32, kind="ExternalInput").ap()
    wgt = dt("wgt", [NKB * 32, 128, 512], F32, kind="ExternalInput").ap()
    wdt = dt("wdt", [NDKB * 4, 128, 512], F32, kind="ExternalInput").ap()
    fwg = dt("fwg", [NKB * NFKB, 128, 128], F32, kind="ExternalInput").ap()
    fwu = dt("fwu", [NKB * NFKB, 128, 128], F32, kind="ExternalInput").ap()
    fwd = dt("fwd", [NFKB * 4, 128, 512], F32, kind="ExternalInput").ap()
    ggate2 = dt("ggate2", [128, NKB], F32, kind="ExternalInput").ap()
    gdown2 = dt("gdown2", [128, NDKB], F32, kind="ExternalInput").ap()
    rwa1 = dt("rwa1", [128, NKB * 16], F32, kind="ExternalInput").ap()
    rwa2 = dt("rwa2", [128, NKB * 16], F32, kind="ExternalInput").ap()
    bg2 = dt("bg2", [128, NFKB], F32, kind="ExternalInput").ap()
    bu2 = dt("bu2", [128, NFKB], F32, kind="ExternalInput").ap()
    bd2 = dt("bd2", [128, NKB], F32, kind="ExternalInput").ap()
    rbrow = dt("rbrow", [1, NE], F32, kind="ExternalInput").ap()
    identd = dt("identd", [128, 128], F32, kind="ExternalInput").ap()
    yt = dt("yt", [H, OWN], F32, kind="ExternalOutput").ap()

    dbg = "ExternalOutput" if os.environ.get("KDEBUG_HW") else "Internal"
    d_h2 = dt("d_h2", [NDKB * 4, 128, 512], F32, kind=dbg).ap()
    d_mid = dt("d_mid", [NFKB * 4, 128, 512], BF16, kind=dbg).ap()
    d_xnt = dt("d_xnt", [NKB * 4, 128, 512], BF16, kind=dbg).ap()
    d_h2q = dt("d_h2q", [NDKB * 4, 128, 512], BF16, kind=dbg).ap()
    d_smt = dt("d_smt", [NE, NT], F32, kind=dbg).ap()
    d_rows = dt("d_rows", [8, OWN], F32, kind=dbg).ap()
    d_small = dt("d_small", [4, 128], F32, kind=dbg).ap()
    d_trep = dt("d_trep", [NE, 16], F32).ap()
    d_dbg1 = dt("d_dbg1", [128, NKB * OWN], BF16, kind=dbg).ap()
    d_dbg2 = dt("d_dbg2", [128, NKB * 512], BF16, kind=dbg).ap()

    p = Plan(dep=dep, rec=rec)
    es = ExitStack()
    sb = lambda name, shape, dtyp: es.enter_context(nc.sbuf_tensor(name, shape, dtyp))

    big1 = sb("big1", [128, NKB * OWN], BF16)
    c_id = sb("c_id", [128, 128], F32)
    ones_col = sb("ones_col", [128, 1], F32)
    ones_colr = sb("ones_colr", [128, 16], F32R)
    ones_row = sb("ones_row", [1, 128], F32)
    mqcol = sb("mqcol", [128, 1], F32)
    hlfcol = sb("hlfcol", [128, 1], F32)
    epscol = sb("epscol", [128, 1], F32)
    ggcol = sb("ggcol", [128, NKB], F32)
    gdcol = sb("gdcol", [128, NDKB], F32)
    gd2col = sb("gd2col", [128, NDKB], F32)
    bgcol = sb("bgcol", [128, NFKB], F32)
    bucol = sb("bucol", [128, NFKB], F32)
    bdcol = sb("bdcol", [128, NKB], F32)
    rbr = sb("rbr", [1, NE], F32)
    rbcols = sb("rbcols", [128, NE], F32)
    negcc = sb("negcc", [128, 16], F32)
    rown = sb("rown", [1, 128], F32)
    rowstage = [sb(f"rowstage{i}", [1, 512], F32) for i in range(2)]
    tk_r = sb("tk_r", [128, 128], F32)
    tk_mr = sb("tk_mr", [128, 128], F32)
    tk_rs = sb("tk_rs", [128, 128], F32)
    tmpA = sb("tmpA", [128, 128], F32)
    tmpB = sb("tmpB", [128, 128], F32)
    tmpC = sb("tmpC", [128, 128], F32)
    tmpD = sb("tmpD", [128, 128], F32)
    wacc_g = sb("wacc_g", [128, 1], F32)
    wacc_d = sb("wacc_d", [128, 1], F32)
    wrow = sb("wrow", [1, 16], F32)
    swg = sb("swg", [128, 2], F32)
    swd = sb("swd", [128, 2], F32)
    bis_lo = sb("bis_lo", [128, 1], F32)
    bis_hi = sb("bis_hi", [128, 1], F32)
    bis_mid = sb("bis_mid", [128, 1], F32)
    bis_m8 = sb("bis_m8", [128, 1], F32)
    bis_cnt = sb("bis_cnt", [128, 1], F32)
    bis_c16 = sb("bis_c16", [8, 16], F32)
    bis_sel = sb("bis_sel", [128, 1], F32)
    bis_t1 = sb("bis_t1", [128, 1], F32)
    bis_t2 = sb("bis_t2", [128, 1], F32)
    th_row = sb("th_row", [1, 16], F32)
    th_cols = sb("th_cols", [128, NE], F32)
    mxcols = sb("mxcols", [128, 16], F32)
    stat_cols = sb("stat_cols", [128, 48], F32)
    colasm = sb("colasm", [128, 128], F32)
    es2 = ExitStack()
    sb2 = lambda name, shape, dtyp: es2.enter_context(nc.sbuf_tensor(name, shape, dtyp))
    sm8 = sb2("sm8", [128, NE * 128], F32)
    smb = sb2("smb", [128, 1024], F32)
    bis_junk = sb2("bis_junk", [128, 1024], F32)
    rxs = ExitStack()
    xring = [rxs.enter_context(nc.sbuf_tensor(f"xring{i}", [128, 512], F32)) for i in range(4)]

    pse = ExitStack()
    ps = [pse.enter_context(nc.psum_tensor(f"ps{i}", [128, 512], F32)) for i in range(8)]

    SY, VE, SC, TE, GP = "sync", "vector", "scalar", "tensor", "gpsimd"

    def row_put(cols_dv, src16, q_idx, col0=0, ncols=16):
        p.wait(TE, "DV", cols_dv)
        p.use_bank(2)
        vt = p.ins(TE, lambda E, s2=src16: E.transpose(ps[2][0:ncols, 0:128], s2, c_id[:, :]), "PE")
        p.wait(VE, "PE", vt)
        vev = p.ins(VE, lambda E, n=ncols: E.tensor_copy(out=colasm[0:n, :], in_=ps[2][0:n, 0:128]), "DV")
        p.free_bank(2, "DV", vev)
        p.wait(SY, "DV", vev)
        vd = p.dma(SY, d_rows[q_idx:q_idx + 1, col0 * 128:(col0 + ncols) * 128].rearrange(
            "o (c q) -> (o c) q", q=128), colasm[0:ncols, :], "RW", ordered=True)
        p.wait(VE, "RW", vd)
        return vd

    rowstage_free = [("PE", 0), ("PE", 0)]
    rowstage_idx = [0]

    def bc_row(q_idx, col0, width, dst, dst_off, row_ready_sw, bank):
        j = rowstage_idx[0] % 2
        rowstage_idx[0] += 1
        s, v = rowstage_free[j]
        p.wait(SY, s, v)
        p.wait(SY, "RW", row_ready_sw)
        vr = p.dma(SY, rowstage[j][:1, 0:width], d_rows[q_idx:q_idx + 1, col0:col0 + width], f"RS{j}")
        p.wait(TE, f"RS{j}", vr)
        p.use_bank(bank)
        vb = p.ins(TE, lambda E, j2=j, w=width, b=bank: E.matmul(
            ps[b][:, 0:w], ones_row[:1, :], rowstage[j2][:1, 0:w], start=True, stop=True), "PE")
        rowstage_free[j] = ("PE", vb)
        p.wait(VE, "PE", vb)
        vev = p.ins(VE, lambda E, d=dst, o=dst_off, w=width, b=bank: E.tensor_copy(
            out=d[:, o:o + w], in_=ps[b][:, 0:w]), "DV")
        p.free_bank(bank, "DV", vev)
        return vev

    p.dma(SY, c_id[:, :], identd, "PR")
    p.dma(SY, ggcol[:, :], ggate2, "PR")
    p.dma(SY, gdcol[:, :], gdown2, "PR")
    p.dma(SY, bgcol[:, :], bg2, "PR")
    p.dma(SY, bucol[:, :], bu2, "PR")
    p.dma(SY, bdcol[:, :], bd2, "PR")
    p.dma(SY, rbr[:1, :], rbrow, "PR")
    v_pro = p.cur("PR")

    p.ins(VE, lambda E: E.memset(ones_col[:, :], 1.0))
    p.ins(VE, lambda E: E.memset(stat_cols[:, 0:16], 1.0))
    p.ins(VE, lambda E: E.tensor_copy(out=ones_colr[:, :], in_=stat_cols[:, 0:16]))
    p.ins(VE, lambda E: E.memset(ones_row[:1, :], 1.0))
    p.ins(VE, lambda E: E.memset(epscol[:, :], EPS_LN))
    p.ins(VE, lambda E: E.memset(hlfcol[:, :], 0.5))
    p.ins(VE, lambda E: E.tensor_scalar(hlfcol[:, :], hlfcol[:, :], M32, M32, A.add, A.subtract))
    p.ins(VE, lambda E: E.tensor_scalar(hlfcol[:, :], hlfcol[:, :], 2.0, None, A.mult))
    p.ins(VE, lambda E: E.tensor_scalar(mqcol[:, :], hlfcol[:, :], M64 - M32, M32, A.mult, A.add))
    p.ins(VE, lambda E: E.memset(wacc_g[:, :], 0.0))
    p.ins(VE, lambda E: E.memset(colasm[:, :], 0.0))
    p.ins(VE, lambda E: E.memset(wrow[:1, :], 0.0))
    p.ins(VE, lambda E: E.memset(wacc_d[:, :], 0.0))
    p.wait(VE, "PR", v_pro)
    v_gd2pre = p.ins(VE, lambda E: E.engine_nop(), "DV")
    p.wait(SC, "DV", v_gd2pre)
    p.ins(SC, lambda E: E.activation(gd2col[:, :], gdcol[:, :], AF.Square), "AC")

    rsc = ExitStack()
    rb_ = lambda name, shape, dtyp: rsc.enter_context(nc.sbuf_tensor(name, shape, dtyp))
    statsT = rb_("statsT", [128, 2048], F32)
    gmh_sb = rb_("gmh_sb", [128, NE * 128], F32)
    gml_sb = rb_("gml_sb", [128, NE * 128], F32)
    xfr = [rb_(f"xfr{i}", [128, 512], F32R) for i in range(2)]
    xsq = [rb_(f"xsq{i}", [128, 512], F32R) for i in range(2)]
    stats_sl = [rb_(f"statsl{i}", [128, 512], F32) for i in range(2)]
    rwa1s = rb_("rwa1s", [128, NKB * 16], F32R)
    rwa2s = rb_("rwa2s", [128, NKB * 16], F32R)
    wgpre = [rb_(f"wgpre{i}", [128, 512], F32) for i in range(4)]
    wdpre = [rb_(f"wdpre{i}", [128, 2048], F32) for i in range(2)]
    wtmp = rb_("wtmp", [128, 512], F32)
    wtmp2 = rb_("wtmp2", [128, 128], F32)
    wjunk = rb_("wjunk", [128, 512], F32)
    wjunk2 = rb_("wjunk2", [128, 1024], F32)

    for e in range(NE):
        p.dma(SY, gmh_sb[:, e * 128:(e + 1) * 128], gmh[e, :, :], "PR")
        p.dma(SY, gml_sb[:, e * 128:(e + 1) * 128], gml[e, :, :], "PR")
    p.dma(SY, wgpre[0][:, 0:256], rwa1, "PR")
    v_rwl = p.dma(SY, wgpre[1][:, 0:256], rwa2, "PR")
    p.wait(VE, "PR", v_rwl)
    p.ins(VE, lambda E: E.tensor_copy(out=rwa1s[:, :], in_=wgpre[0][:, 0:256]))
    v_rwa = p.ins(VE, lambda E: E.tensor_copy(out=rwa2s[:, :], in_=wgpre[1][:, 0:256]), "DV")

    NSL = NT // 512
    xr_free = [("DV", 0)] * 4
    xfr_free = [("PE", 0)] * 2
    xsq_free = [("PE", 0)] * 2
    stats_free = [("DV", 0)] * 2
    p.ins(VE, lambda E: E.memset(stats_sl[0][:, :], 0.0))
    p.ins(VE, lambda E: E.memset(stats_sl[1][:, :], 0.0))
    p.wait(TE, "DV", v_rwa)

    wg_free = [("AC", 0)] * 4
    wg_free[0] = ("DV", v_rwa)
    wg_free[1] = ("DV", v_rwa)
    wd_free = [("AC", 0)] * 2
    WGQ = (GP, GP, SY, GP, GP, GP, SC, GP, GP, GP, SY, GP, GP, GP, SC, GP)

    def emit_wg_unit(t):
        kb2 = t % NKB
        cs = t // NKB
        slot = t % 4
        eng = WGQ[t % 16]
        s, v = wg_free[slot]
        p.wait(eng, s, v)
        vld = p.dma(eng, wgpre[slot][:, :], wgt[kb2 * 32 + cs, :, :], f"GWG{slot}")
        p.wait(SC, f"GWG{slot}", vld)
        vr = p.ins(SC, lambda E, sl2=slot, t2=t: E.activation(
            wjunk[:, :], wgpre[sl2][:, :], AF.Abs, accum_out=wtmp[:, t2:t2 + 1]), "AC")
        wg_free[slot] = ("AC", vr)

    def emit_wd_unit(td):
        dslot = td % 2
        s, v = wd_free[dslot]
        p.wait(SC, s, v)
        vldd = 0
        for j in range(4):
            vldd = p.dma(SC, wdpre[dslot][:, j * 512:(j + 1) * 512], wdt[td * 4 + j, :, :], f"GWD{dslot}")
        p.wait(SC, f"GWD{dslot}", vldd)
        p.ins(SC, lambda E, sl2=dslot, t2=td: E.activation(
            wjunk2[:, 0:1024], wdpre[sl2][:, 0:1024], AF.Abs, accum_out=wtmp2[:, t2:t2 + 1]), "AC")
        vrr = p.ins(SC, lambda E, sl2=dslot, t2=td: E.activation(
            wjunk2[:, 0:1024], wdpre[sl2][:, 1024:2048], AF.Abs, accum_out=wtmp2[:, 64 + t2:64 + t2 + 1]), "AC")
        wd_free[dslot] = ("AC", vrr)

    for sl in range(NSL):
        bank = sl % 2
        vmm = None
        for kb in range(NKB):
            u = sl * NKB + kb
            slot = u % 4
            s, v = xr_free[slot]
            p.wait(SY, s, v)
            vld = p.dma(SY, xring[slot][:, :], xt[kb * 32 + sl, :, :], f"XR{slot}")
            j = u % 2
            s, v = xfr_free[j]
            p.wait(VE, s, v)
            p.wait(VE, f"XR{slot}", vld)
            vcv = p.ins(VE, lambda E, a=xfr[j], b=xring[slot]: E.tensor_copy(out=a[:, :], in_=b[:, :]), "DV")
            xr_free[slot] = ("DV", vcv)
            s, v = xsq_free[j]
            p.wait(VE, s, v)
            vsq = p.ins(VE, lambda E, a=xsq[j], b=xfr[j]: E.tensor_mul(a[:, :], b[:, :], b[:, :]), "DV")
            if kb == 0:
                s, v = stats_free[bank]
                p.wait(TE, s, v)
            p.wait(TE, "DV", vcv)
            p.ins(TE, lambda E, b=bank, k=kb, j2=j: E.matmul(
                ps[b][0:16, :], rwa1s[:, k * 16:(k + 1) * 16], xfr[j2][:, :], start=(k == 0), stop=False))
            p.wait(TE, "DV", vsq)
            vmm = p.ins(TE, lambda E, b=bank, k=kb, j2=j: E.matmul(
                ps[b][0:16, :], rwa2s[:, k * 16:(k + 1) * 16], xsq[j2][:, :], start=False, stop=(k == NKB - 1)),
                "PE")
            xfr_free[j] = ("PE", vmm)
            xsq_free[j] = ("PE", vmm)
            emit_wg_unit(u)
            if u % 8 == 0:
                emit_wd_unit(u // 8)
        p.wait(VE, "PE", vmm)
        vev = p.ins(VE, lambda E, d=stats_sl[sl % 2], b=bank: E.tensor_copy(out=d[0:16, :], in_=ps[b][0:16, :]), "DV")
        stats_free[bank] = ("DV", vev)
        for c in range(4):
            p.wait(TE, "DV", vev)
            p.use_bank(2)
            vtr = p.ins(TE, lambda E, d=stats_sl[sl % 2], c2=c: E.transpose(
                ps[2][:, 0:128], d[:, c2 * 128:(c2 + 1) * 128], c_id[:, :]), "PE")
            p.wait(VE, "PE", vtr)
            vst2 = p.ins(VE, lambda E, sl2=sl, c2=c: E.tensor_copy(
                out=statsT[:, (sl2 * 4 + c2) * 16:(sl2 * 4 + c2) * 16 + 16], in_=ps[2][:, 0:16]), "DV")
            p.free_bank(2, "DV", vst2)
    v_statsT = p.ins(VE, lambda E: E.engine_nop(), "DV")

    v_wabs = p.cur("AC")
    p.wait(VE, "AC", v_wabs)
    v_waccg = p.ins(VE, lambda E: E.tensor_reduce(wacc_g[:, :], wtmp[:, :], AX, A.add), "DV")
    v_waccd = p.ins(VE, lambda E: E.tensor_reduce(wacc_d[:, :], wtmp2[:, :], AX, A.add), "DV")

    p.wait(TE, "DV", v_waccg)
    p.use_bank(3)
    vmw = p.ins(TE, lambda E: E.transpose(ps[3][0:1, 0:128], wacc_g[:, 0:1], c_id[:, :]), "PE")
    p.wait(TE, "DV", v_waccd)
    p.use_bank(2)
    vmw2 = p.ins(TE, lambda E: E.transpose(ps[2][0:1, 0:128], wacc_d[:, 0:1], c_id[:, :]), "PE")
    p.wait(VE, "PE", vmw2)
    p.ins(VE, lambda E: E.reduce_sum(wrow[:1, 0:1], ps[3][0:1, 0:128], AX))
    vwd0 = p.ins(VE, lambda E: E.reduce_sum(wrow[:1, 1:2], ps[2][0:1, 0:128], AX), "DV")
    p.free_bank(2, "DV", vwd0)
    p.ins(VE, lambda E: E.tensor_scalar(wrow[:1, 0:1], wrow[:1, 0:1], 1.0 / NWG, 1e-5, A.mult, A.max))
    p.ins(VE, lambda E: E.tensor_scalar(wrow[:1, 1:2], wrow[:1, 1:2], 1.0 / NWD, 1e-5, A.mult, A.max))
    vwr = p.ins(VE, lambda E: E.reciprocal(wrow[:1, 2:4], wrow[:1, 0:2]), "DV")
    p.wait(TE, "DV", vwr)
    p.use_bank(3)
    vbw = p.ins(TE, lambda E: E.matmul(ps[3][:, 0:16], ones_row[:1, :], wrow[:1, 0:16], start=True, stop=True), "PE")
    p.wait(VE, "PE", vbw)
    p.ins(VE, lambda E: E.tensor_copy(out=swg[:, 1:2], in_=ps[3][:, 0:1]))
    p.ins(VE, lambda E: E.tensor_copy(out=swd[:, 1:2], in_=ps[3][:, 1:2]))
    p.ins(VE, lambda E: E.tensor_copy(out=swg[:, 0:1], in_=ps[3][:, 2:3]))
    v_sw = p.ins(VE, lambda E: E.tensor_copy(out=swd[:, 0:1], in_=ps[3][:, 3:4]), "DV")
    p.free_bank(3, "DV", v_sw)

    def qv(q):
        return statsT[:, :].rearrange("p (c q) -> p c q", q=16)[:, :, q:q + 1]

    p.wait(VE, "DV", v_statsT)
    p.use_bank(3)
    for kb in range(NKB):
        vcs = p.ins(TE, lambda E, k=kb: E.matmul(
            ps[3][0:16, 32:48], rwa1s[:, k * 16:(k + 1) * 16], ones_colr[:, :],
            start=(k == 0), stop=(k == NKB - 1)), "PE" if kb == NKB - 1 else None)
    p.wait(VE, "PE", vcs)
    vcse = p.ins(VE, lambda E: E.tensor_copy(out=colasm[0:16, 0:1], in_=ps[3][0:16, 32:33]), "DV")
    p.free_bank(3, "DV", vcse)
    p.wait(TE, "DV", vcse)
    p.use_bank(2)
    vcst = p.ins(TE, lambda E: E.transpose(ps[2][0:1, 0:128], colasm[:, 0:1], c_id[:, :]), "PE")
    p.wait(VE, "PE", vcst)
    vrn = p.ins(VE, lambda E: E.tensor_copy(out=rown[:1, :], in_=ps[2][0:1, 0:128]), "DV")
    p.free_bank(2, "DV", vrn)
    vneg = p.ins(VE, lambda E: E.tensor_scalar(rown[:1, 0:16], rown[:1, 0:16], -INV_H, None, A.mult), "DV")
    p.wait(TE, "DV", vneg)
    p.use_bank(3)
    p.ins(TE, lambda E: E.matmul(ps[3][:, 8:24], ones_row[:1, :], rown[:1, 0:16], start=True, stop=True))
    vb2 = p.ins(TE, lambda E: E.matmul(ps[3][:, 24:32], ones_row[:1, :], rbr[:1, :], start=True, stop=True), "PE")
    p.wait(VE, "PE", vb2)
    p.ins(VE, lambda E: E.tensor_copy(out=negcc[:, :], in_=ps[3][:, 8:24]))
    vrbc = p.ins(VE, lambda E: E.tensor_copy(out=rbcols[:, :], in_=ps[3][:, 24:32]), "DV")
    p.free_bank(3, "DV", vrbc)
    p.ins(VE, lambda E: E.tensor_scalar(v3(tmpA[:, :]), qv(8), INV_H, None, A.mult))
    p.ins(VE, lambda E: E.tensor_scalar(v3(tmpB[:, :]), qv(9), INV_H, None, A.mult))
    p.ins(VE, lambda E: E.tensor_mul(tmpC[:, :], tmpA[:, :], tmpA[:, :]))
    p.ins(VE, lambda E: E.tensor_tensor(tmpB[:, :], tmpB[:, :], tmpC[:, :], A.subtract))
    vv = p.ins(VE, lambda E: E.engine_nop(), "DV")
    p.wait(SC, "DV", vv)
    va = p.ins(SC, lambda E: E.activation(tmpC[:, :], tmpB[:, :], AF.Sqrt, bias=epscol[:, 0:1]), "AC")
    p.wait(VE, "AC", va)
    p.ins(VE, lambda E: E.reciprocal(tk_r[:, :], tmpC[:, :]))
    p.ins(VE, lambda E: E.tensor_mul(tk_mr[:, :], tmpA[:, :], tk_r[:, :]))
    p.ins(VE, lambda E: E.tensor_mul(tmpD[:, :], tk_r[:, :], tk_r[:, :]))
    p.ins(VE, lambda E: E.tensor_mul(tmpD[:, :], tmpB[:, :], tmpD[:, :]))
    vq = p.ins(VE, lambda E: E.engine_nop(), "DV")
    p.wait(SC, "DV", vq)
    va2 = p.ins(SC, lambda E: E.activation(tmpD[:, :], tmpD[:, :], AF.Sqrt, bias=epscol[:, 0:1]), "AC")
    p.wait(VE, "AC", va2)
    p.ins(VE, lambda E: E.reciprocal(tk_rs[:, :], tmpD[:, :]))
    for e in range(NE):
        sl8 = lambda e2: sm8[:, e2 * 128:(e2 + 1) * 128]
        p.ins(VE, lambda E, e2=e: E.scalar_tensor_tensor(
            v3(sm8[:, e2 * 128:(e2 + 1) * 128]), qv(8), negcc[:, e2:e2 + 1], qv(e2), A.mult, A.add))
        p.ins(VE, lambda E, e2=e: E.tensor_mul(sl8(e2), sl8(e2), tk_r[:, :]))
        p.ins(VE, lambda E, e2=e: E.tensor_scalar(sl8(e2), sl8(e2), rbcols[:, e2:e2 + 1], None, A.add))
        p.ins(VE, lambda E, e2=e: E.tensor_add(sl8(e2), sl8(e2), gmh_sb[:, e2 * 128:(e2 + 1) * 128]))
        p.ins(VE, lambda E, e2=e: E.tensor_add(sl8(e2), sl8(e2), gml_sb[:, e2 * 128:(e2 + 1) * 128]))
    p.ins(VE, lambda E: E.tensor_copy(out=tmpA[:, :], in_=sm8[:, 0:128]))
    for e in range(1, NE):
        p.ins(VE, lambda E, e2=e: E.tensor_max(tmpA[:, :], tmpA[:, :], sm8[:, e2 * 128:(e2 + 1) * 128]))
    for e in range(NE):
        p.ins(VE, lambda E, e2=e: E.tensor_tensor(
            sm8[:, e2 * 128:(e2 + 1) * 128], sm8[:, e2 * 128:(e2 + 1) * 128], tmpA[:, :], A.subtract))
    vsm = p.ins(VE, lambda E: E.engine_nop(), "DV")
    p.wait(SC, "DV", vsm)
    vex = None
    for e in range(NE):
        vex = p.ins(SC, lambda E, e2=e: E.activation(
            sm8[:, e2 * 128:(e2 + 1) * 128], sm8[:, e2 * 128:(e2 + 1) * 128], AF.Exp), "AC")
    p.wait(VE, "AC", vex)
    p.ins(VE, lambda E: E.tensor_copy(out=tmpB[:, :], in_=sm8[:, 0:128]))
    for e in range(1, NE):
        p.ins(VE, lambda E, e2=e: E.tensor_add(tmpB[:, :], tmpB[:, :], sm8[:, e2 * 128:(e2 + 1) * 128]))
    p.ins(VE, lambda E: E.reciprocal(tmpC[:, :], tmpB[:, :]))
    for e in range(NE):
        p.ins(VE, lambda E, e2=e: E.tensor_mul(
            sm8[:, e2 * 128:(e2 + 1) * 128], sm8[:, e2 * 128:(e2 + 1) * 128], tmpC[:, :]))
    v_sm8 = p.ins(VE, lambda E: E.engine_nop(), "DV")

    vsmt_dma = 0
    for e in range(NE):
        p.wait(TE, "DV", v_sm8)
        p.use_bank(2)
        vt = p.ins(TE, lambda E, e2=e: E.transpose(ps[2][:, 0:128], sm8[:, e2 * 128:(e2 + 1) * 128], c_id[:, :]), "PE")
        p.wait(VE, "PE", vt)
        p.wait(VE, "SM", vsmt_dma)
        vte = p.ins(VE, lambda E: E.tensor_copy(out=tmpD[:, :], in_=ps[2][:, 0:128]), "DV")
        p.free_bank(2, "DV", vte)
        p.wait(SY, "DV", vte)
        vsmt_dma = p.dma(SY, d_smt[e:e + 1, :].rearrange("o (c q) -> (o c) q", q=128), tmpD[:, :], "SM", ordered=True)
    p.wait(GP, "SM", vsmt_dma)
    v_smb = p.dma(GP, smb[:, :], d_smt.rearrange("e t -> (e t)").rearrange("(q f) -> q f", q=128), "SG")

    vmk = p.ins(VE, lambda E: E.tensor_copy(out=mxcols[:, 0:16], in_=tk_r[:, 0:16]), "DV")
    v_row_r = row_put(vmk, mxcols[:, 0:16], Q_R)
    vmk = p.ins(VE, lambda E: E.tensor_copy(out=mxcols[:, 0:16], in_=tk_mr[:, 0:16]), "DV")
    v_row_mr = row_put(vmk, mxcols[:, 0:16], Q_MR)
    vmk = p.ins(VE, lambda E: E.tensor_copy(out=mxcols[:, 0:16], in_=tk_rs[:, 0:16]), "DV")
    v_row_rs = row_put(vmk, mxcols[:, 0:16], Q_RS)

    p.ins(GP, lambda E: E.memset(bis_lo[:, :], 0.0))
    p.ins(GP, lambda E: E.memset(bis_hi[:, :], 1.5))
    p.ins(GP, lambda E: E.memset(bis_mid[:, :], 0.75))
    p.ins(GP, lambda E: E.memset(bis_m8[:, :], 0.75))
    p.wait(GP, "SG", v_smb)
    for it in range(N_BISECT):
        p.ins(GP, lambda E: E.tensor_single_scalar(bis_junk[:, :], smb[:, :], bis_mid[:, 0:1], A.is_ge))
        for w in (512, 256, 128, 64, 32, 16, 8, 4, 2, 1):
            p.ins(GP, lambda E, w2=w: E.tensor_add(bis_junk[:, 0:w2], bis_junk[:, 0:w2], bis_junk[:, w2:2 * w2]))
        p.ins(GP, lambda E: E.tensor_copy(out=bis_cnt[:, 0:1], in_=bis_junk[:, 0:1]))
        v1 = p.dma(GP, d_small[0:1, 0:128], bis_cnt[:, 0:1], "SG")
        p.wait(GP, "SG", v1)
        v2 = p.dma(GP, bis_c16[0:8, :], d_small[0:1, 0:128].rearrange("o (e j) -> (o e) j", j=16), "SG")
        p.wait(GP, "SG", v2)
        for w in (8, 4, 2, 1):
            p.ins(GP, lambda E, w2=w: E.tensor_add(bis_c16[0:8, 0:w2], bis_c16[0:8, 0:w2], bis_c16[0:8, w2:2 * w2]))
        p.ins(GP, lambda E: E.tensor_copy(out=bis_t1[0:8, 0:1], in_=bis_c16[0:8, 0:1]))
        p.ins(GP, lambda E: E.tensor_scalar(bis_sel[0:8, 0:1], bis_t1[0:8, 0:1], float(KCAP), None, A.is_ge))
        p.ins(GP, lambda E: E.tensor_tensor(bis_t1[0:8, 0:1], bis_m8[0:8, 0:1], bis_lo[0:8, 0:1], A.subtract))
        p.ins(GP, lambda E: E.tensor_mul(bis_t1[0:8, 0:1], bis_t1[0:8, 0:1], bis_sel[0:8, 0:1]))
        p.ins(GP, lambda E: E.tensor_add(bis_lo[0:8, 0:1], bis_lo[0:8, 0:1], bis_t1[0:8, 0:1]))
        p.ins(GP, lambda E: E.tensor_scalar(bis_sel[0:8, 0:1], bis_sel[0:8, 0:1], -1.0, 1.0, A.mult, A.add))
        p.ins(GP, lambda E: E.tensor_tensor(bis_t2[0:8, 0:1], bis_m8[0:8, 0:1], bis_hi[0:8, 0:1], A.subtract))
        p.ins(GP, lambda E: E.tensor_mul(bis_t2[0:8, 0:1], bis_t2[0:8, 0:1], bis_sel[0:8, 0:1]))
        p.ins(GP, lambda E: E.tensor_add(bis_hi[0:8, 0:1], bis_hi[0:8, 0:1], bis_t2[0:8, 0:1]))
        p.ins(GP, lambda E: E.tensor_tensor(bis_t1[0:8, 0:1], bis_lo[0:8, 0:1], bis_hi[0:8, 0:1], A.add))
        p.ins(GP, lambda E: E.tensor_scalar(bis_t1[0:8, 0:1], bis_t1[0:8, 0:1], 0.5, None, A.mult))
        p.ins(GP, lambda E: E.tensor_copy(out=bis_m8[0:8, 0:1], in_=bis_t1[0:8, 0:1]))
        v3_ = p.dma(GP, d_small[1:2, 0:8], bis_t1[0:8, 0:1], "SG")
        p.wait(GP, "SG", v3_)
        vrp = None
        for jj in range(16):
            vrp = p.dma(GP, d_trep[:, jj:jj + 1],
                        d_small[1:2, 0:8].rearrange("o e -> (o e)").rearrange("(e o) -> e o", o=1), "SG", slow=True)
        p.wait(GP, "SG", vrp)
        vm = p.dma(GP, bis_mid[:, 0:1], d_trep.rearrange("e j -> (e j)").rearrange("(q o) -> q o", o=1), "SG")
        p.wait(GP, "SG", vm)
    v_th = p.dma(GP, d_small[3:4, 0:8], bis_lo[0:8, 0:1], "SG")
    v_bis = p.ins(GP, lambda E: E.engine_nop(), "PL")

    p.sync_phase(sems=("SW", "DV", "AC", "PE"))
    rsc.close()

    xsc = ExitStack()
    xb_ = lambda name, shape, dtyp: xsc.enter_context(nc.sbuf_tensor(name, shape, dtyp))
    xr_sl = xb_("xr_sl", [128, NKB * 512], F32)
    xn_f = xb_("xn_f", [128, 512], F32)
    xnb = [xb_(f"xnb{i}", [128, 512], BF16) for i in range(2)]
    sqh = xb_("sqh", [128, 512], F32)
    silx = [xb_(f"silx{i}", [128, 512], F32) for i in range(2)]
    bcx = [xb_(f"bcx{i}", [128, 512], F32) for i in range(3)]
    i32t = [xb_(f"i32t{i}", [128, 512], F32) for i in range(2)]
    prex = [xb_(f"prex{i}", [128, 512], F32) for i in range(2)]

    xnb_free = [("XB0", 0), ("XB1", 0)]
    i32_free = [("DV", 0)] * 2
    prex_free = [("AC", 0)] * 2
    silx_free = [("DV", 0)] * 2
    v_row_sq = 0
    for sl in range(4):
        vr1 = bc_row(Q_R, sl * 512, 512, bcx[0], 0, v_row_r, 0)
        vr2 = bc_row(Q_MR, sl * 512, 512, bcx[1], 0, v_row_mr, 1)
        vr3 = bc_row(Q_RS, sl * 512, 512, bcx[2], 0, v_row_rs, 3)
        p.wait(VE, "DV", max(vr1, vr2, vr3))
        p.ins(VE, lambda E: E.memset(sqh[:, :], 0.0))
        for kb in range(NKB):
            u = sl * NKB + kb
            slot = u % 4
            s, v = xr_free[slot]
            p.wait(SY, s, v)
            vld = p.dma(SY, xring[slot][:, :], xt[kb * 32 + sl, :, :], f"XR{slot}")
            p.wait(VE, f"XR{slot}", vld)
            p.ins(VE, lambda E, sl2=slot: E.tensor_mul(xn_f[:, :], xring[sl2][:, :], bcx[0][:, :]))
            vxn = p.ins(VE, lambda E: E.tensor_tensor(xn_f[:, :], xn_f[:, :], bcx[1][:, :], A.subtract), "DV")
            xr_free[slot] = ("DV", vxn)
            jb = u % 2
            s, v = xnb_free[jb]
            p.wait(VE, s, v)
            p.ins(VE, lambda E, j2=jb: E.tensor_copy(out=xnb[j2][:, :], in_=xn_f[:, :]))
            vxr = p.ins(VE, lambda E, k=kb: E.scalar_tensor_tensor(
                xr_sl[:, k * 512:(k + 1) * 512], xn_f[:, :], ggcol[:, k:k + 1], bcx[2][:, :], A.mult, A.mult), "DV")
            p.wait(SC, "DV", vxr)
            vsp = p.dma(SC, d_xnt[kb * 4 + sl, :, :], xnb[jb][:, :], f"XB{jb}")
            xnb_free[jb] = (f"XB{jb}", vsp)
            jx = u % 2
            s, v = silx_free[jx]
            p.wait(SC, s, v)
            vsq2 = p.ins(SC, lambda E, k=kb, j2=jx: E.activation(
                silx[j2][:, :], xr_sl[:, k * 512:(k + 1) * 512], AF.Square), "AC")
            p.wait(VE, "AC", vsq2)
            vfo = p.ins(VE, lambda E, j2=jx: E.tensor_max(sqh[:, :], sqh[:, :], silx[j2][:, :]), "DV")
            silx_free[jx] = ("DV", vfo)
        vfold = p.cur("DV")
        for c in range(4):
            p.wait(TE, "DV", vfold)
            p.use_bank(2)
            vt = p.ins(TE, lambda E, c2=c: E.transpose(ps[2][:, 0:128], sqh[:, c2 * 128:(c2 + 1) * 128], c_id[:, :]), "PE")
            p.wait(VE, "PE", vt)
            vrm = p.ins(VE, lambda E, s2=sl, c2=c: E.reduce_max(mxcols[:, s2 * 4 + c2:s2 * 4 + c2 + 1], ps[2][:, 0:128], AX), "DV")
            p.free_bank(2, "DV", vrm)
        p.wait(SC, "DV", p.cur("DV"))
        vsc = p.ins(SC, lambda E, s2=sl: E.activation(
            mxcols[:, s2 * 4:s2 * 4 + 4], mxcols[:, s2 * 4:s2 * 4 + 4], AF.Sqrt), "AC")
        p.wait(VE, "AC", vsc)
        p.ins(VE, lambda E, s2=sl: E.tensor_scalar(
            stat_cols[:, s2 * 4:s2 * 4 + 4], mxcols[:, s2 * 4:s2 * 4 + 4], 1e-5, None, A.max))
        p.ins(VE, lambda E, s2=sl: E.reciprocal(mxcols[:, s2 * 4:s2 * 4 + 4], stat_cols[:, s2 * 4:s2 * 4 + 4]))
        vsq_c = p.ins(VE, lambda E, s2=sl: E.tensor_scalar(
            mxcols[:, s2 * 4:s2 * 4 + 4], mxcols[:, s2 * 4:s2 * 4 + 4], 127.0, None, A.mult), "DV")
        v_row_sq = row_put(vsq_c, mxcols[:, sl * 4:sl * 4 + 4], Q_SQ, col0=sl * 4, ncols=4)
        vbq = bc_row(Q_SQ, sl * 512, 512, bcx[0], 0, v_row_sq, 0)
        p.wait(VE, "DV", vbq)
        for kb in range(NKB):
            ji = kb % 2
            s, v = prex_free[ji]
            p.wait(VE, s, v)
            vpre = p.ins(VE, lambda E, k=kb, j2=ji: E.tensor_mul(
                prex[j2][:, :], xr_sl[:, k * 512:(k + 1) * 512], bcx[0][:, :]), "DV")
            s, v = i32_free[ji]
            p.wait(SC, s, v)
            p.wait(SC, "DV", vpre)
            vcv = p.ins(SC, lambda E, j2=ji: E.activation(i32t[j2][:, :], prex[j2][:, :], AF.Identity, bias=mqcol[:, 0:1]), "AC")
            prex_free[ji] = ("AC", vcv)
            p.wait(VE, "AC", vcv)
            vun = p.ins(VE, lambda E, s2=sl, k=kb, j2=ji: E.tensor_scalar(
                big1[:, k * OWN + s2 * 512:k * OWN + s2 * 512 + 512], i32t[j2][:, :], mqcol[:, 0:1], None, A.subtract), "DV")
            i32_free[ji] = ("DV", vun)
    p.wait(VE, "DV", v_sw)
    vsc1c = p.ins(VE, lambda E: E.tensor_scalar(
        mxcols[:, 0:16], stat_cols[:, 0:16], swg[:, 1:2], 1.0 / 127.0, A.mult, A.mult), "DV")
    v_row_sc1 = row_put(vsc1c, mxcols[:, 0:16], Q_SC1)
    v_xq = p.cur("DV")
    if os.environ.get("KDEBUG_A1"):
        p.wait(SY, "DV", v_xq)
        p.dma(SY, d_dbg1, big1[:, :], "SW")

    p.sync_phase(sems=("SW", "SA", "DV", "AC", "PE"))
    xsc.close()
    rxs.close()

    a1 = ExitStack()
    ab_ = lambda name, shape, dtyp: a1.enter_context(nc.sbuf_tensor(name, shape, dtyp))
    wstage = [ab_(f"wstage{i}", [128, 512], F32) for i in range(3)]
    i8a = [ab_(f"i8a{i}", [128, 512], F32) for i in range(2)]
    clipa = ab_("clipa", [128, 512], F32)
    wq_g = [ab_(f"wq_g{i}", [128, NKB * 512], BF16) for i in range(2)]
    wq_v = [ab_(f"wq_v{i}", [128, NKB * 512], BF16) for i in range(2)]
    h2ring = [ab_(f"h2ring{i}", [128, 512], F32) for i in range(4)]
    accR2 = ab_("accR2", [128, OWN], F32)
    accM2 = ab_("accM2", [128, OWN], F32)
    sc1bc = ab_("sc1bc", [128, OWN], F32)
    gsc_t = [ab_(f"gsc_t{i}", [128, 512], F32) for i in range(2)]
    vsc_t = [ab_(f"vsc_t{i}", [128, 512], F32) for i in range(2)]
    sq_t = [ab_(f"sq_t{i}", [128, 512], F32) for i in range(2)]

    p.ins(VE, lambda E: E.memset(accR2[:, :], 0.0))
    p.ins(VE, lambda E: E.memset(accM2[:, :], 0.0))
    for ts in range(4):
        bc_row(Q_SC1, ts * 512, 512, sc1bc, ts * 512, v_row_sc1, ts % 2)
    v_a1pre = p.ins(VE, lambda E: E.engine_nop(), "DV")
    p.wait(TE, "DV", v_a1pre)

    wstage_free = [("AC", 0)] * 3
    i8a_free = [("DV", 0)] * 2
    panel_free = [("PE", 0)] * 2
    set_free = [("DV", 0)] * 2
    h2_free = [[(f"HS{i}", 0)] for i in range(4)]
    gsc_free = [("DV", 0)] * 2
    vsc_free = [("DV", 0)] * 2
    sq_free = [("DV", 0)] * 2

    def quant_w_subtile(dst, dst_off, src_ap, sw_tile, u):
        slot = u % 3
        s, v = wstage_free[slot]
        p.wait(SY, s, v)
        vld = p.dma(SY, wstage[slot][:, :], src_ap, f"WS{slot}")
        ji = u % 2
        s, v = i8a_free[ji]
        p.wait(SC, s, v)
        p.wait(SC, f"WS{slot}", vld)
        vq = p.ins(SC, lambda E, sl2=slot, j2=ji, sw2=sw_tile: E.activation(
            i8a[j2][:, :], wstage[sl2][:, :], AF.Identity, scale=sw2[:, 0:1], bias=mqcol[:, 0:1]), "AC")
        wstage_free[slot] = ("AC", vq)
        p.wait(VE, "AC", vq)
        p.ins(VE, lambda E, j2=ji: E.tensor_scalar(clipa[:, :], i8a[j2][:, :], mqcol[:, 0:1], None, A.subtract))
        vw = p.ins(VE, lambda E, d=dst, o=dst_off: E.tensor_scalar(
            d[:, o:o + 512], clipa[:, :], -1.0, 1.0, A.max, A.min), "DV")
        i8a_free[ji] = ("DV", vw)
        return vw

    NPAIR = 16
    panel_pe = {}
    vwq_done = {}

    def emit_quant_unit(pb2, u2):
        buf2 = pb2 % 2
        if u2 == 0:
            p.wait(VE, "PE", panel_pe.get(pb2 - 2, 0))
        if u2 < NKB:
            vw = quant_w_subtile(wq_g[buf2], u2 * 512, wgt[u2 * 32 + pb2, :, :],
                                 swg, pb2 * 32 + u2)
        else:
            kb2 = u2 - NKB
            vw = quant_w_subtile(wq_v[buf2], kb2 * 512, wgt[kb2 * 32 + 16 + pb2, :, :],
                                 swg, pb2 * 32 + u2)
        vwq_done[pb2] = vw

    for u in range(32):
        emit_quant_unit(0, u)
    for pb in range(NPAIR):
        buf = pb % 2
        p.wait(TE, "DV", vwq_done[pb])
        if pb + 1 < NPAIR:
            for q in range(8):
                emit_quant_unit(pb + 1, q)
        vlast_mm = 0
        for ts in range(4):
            for half in range(2):
                hs = half * 2
                sf, vf = set_free[half]
                p.wait(TE, sf, vf)
                for gv in range(2):
                    wq = wq_g[buf] if gv == 0 else wq_v[buf]
                    for obh in range(2):
                        ob = hs + obh
                        bank = half * 4 + gv * 2 + obh
                        for kb in range(NKB):
                            vlast_mm = p.ins(TE, lambda E, b=bank, w=wq, k=kb, o=ob, t2=ts: E.matmul(
                                ps[b][:, :], w[:, k * 512 + o * 128:k * 512 + o * 128 + 128],
                                big1[:, k * OWN + t2 * 512:k * OWN + t2 * 512 + 512],
                                start=(k == 0), stop=(k == NKB - 1)),
                                "PE" if (kb == NKB - 1) else None)
                vlastdv = 0
                for obh in range(2):
                    ob = hs + obh
                    o_abs = pb * 4 + ob
                    bg_ = half * 4 + 0 * 2 + obh
                    bv_ = half * 4 + 1 * 2 + obh
                    jp = (ts * 2 + obh) % 2
                    p.wait(VE, "PE", vlast_mm)
                    s, v = gsc_free[jp]
                    p.wait(VE, s, v)
                    p.ins(VE, lambda E, j2=jp, b=bg_, t2=ts: E.tensor_mul(
                        gsc_t[j2][:, :], ps[b][:, :], sc1bc[:, t2 * 512:(t2 + 1) * 512]))
                    s, v = vsc_free[jp]
                    p.wait(VE, s, v)
                    vvs = p.ins(VE, lambda E, j2=jp, b=bv_, t2=ts: E.tensor_mul(
                        vsc_t[j2][:, :], ps[b][:, :], sc1bc[:, t2 * 512:(t2 + 1) * 512]), "DV")
                    p.wait(SC, "DV", vvs)
                    vsl = p.ins(SC, lambda E, j2=jp: E.activation(
                        gsc_t[j2][:, :], gsc_t[j2][:, :], AF.Silu), "AC")
                    hslot = (pb * 8 + ts * 2 + obh) % 4
                    for s, v in h2_free[hslot]:
                        p.wait(VE, s, v)
                    p.wait(VE, "AC", vsl)
                    vh2 = p.ins(VE, lambda E, j2=jp, h=hslot: E.tensor_mul(
                        h2ring[h][:, :], gsc_t[j2][:, :], vsc_t[j2][:, :]), "DV")
                    gsc_free[jp] = ("DV", vh2)
                    vsc_free[jp] = ("DV", vh2)
                    p.wait(SC, "DV", vh2)
                    vspl = p.dma(SC, d_h2[o_abs * 4 + ts, :, :],
                                 h2ring[hslot][:, :], f"HS{hslot}")
                    jq = jp
                    s, v = sq_free[jq]
                    p.wait(SC, s, v)
                    vsq3 = p.ins(SC, lambda E, j2=jq, h=hslot: E.activation(
                        sq_t[j2][:, :], h2ring[h][:, :], AF.Square), "AC")
                    p.wait(VE, "AC", vsq3)
                    p.ins(VE, lambda E, j2=jq, t2=ts: E.tensor_add(
                        accR2[:, t2 * 512:(t2 + 1) * 512], accR2[:, t2 * 512:(t2 + 1) * 512], sq_t[j2][:, :]))
                    vlastdv = p.ins(VE, lambda E, j2=jq, t2=ts, oa=o_abs: E.scalar_tensor_tensor(
                        accM2[:, t2 * 512:(t2 + 1) * 512], sq_t[j2][:, :], gd2col[:, oa:oa + 1],
                        accM2[:, t2 * 512:(t2 + 1) * 512], A.mult, A.max), "DV")
                    sq_free[jq] = ("DV", vlastdv)
                    h2_free[hslot] = [(f"HS{hslot}", vspl), ("AC", vsq3)]
                set_free[half] = ("DV", vlastdv)
                if pb + 1 < NPAIR:
                    g = ts * 2 + half
                    if g < 6:
                        for q in range(4):
                            emit_quant_unit(pb + 1, 8 + g * 4 + q)
        panel_pe[pb] = vlast_mm
    v_a1 = p.cur("DV")

    for c in range(16):
        p.wait(TE, "DV", v_a1)
        p.use_bank(2)
        vt1 = p.ins(TE, lambda E, c2=c: E.transpose(ps[2][:, 0:128], accR2[:, c2 * 128:(c2 + 1) * 128], c_id[:, :]), "PE")
        p.use_bank(3)
        vt2 = p.ins(TE, lambda E, c2=c: E.transpose(ps[3][:, 0:128], accM2[:, c2 * 128:(c2 + 1) * 128], c_id[:, :]), "PE")
        p.wait(VE, "PE", vt1)
        vrd1 = p.ins(VE, lambda E, c2=c: E.reduce_sum(stat_cols[:, c2:c2 + 1], ps[2][:, 0:128], AX), "DV")
        p.free_bank(2, "DV", vrd1)
        p.wait(VE, "PE", vt2)
        vrd2 = p.ins(VE, lambda E, c2=c: E.reduce_max(stat_cols[:, 16 + c2:17 + c2], ps[3][:, 0:128], AX), "DV")
        p.free_bank(3, "DV", vrd2)
    vst = p.ins(VE, lambda E: E.tensor_scalar(stat_cols[:, 0:16], stat_cols[:, 0:16], 1.0 / DHALF, None, A.mult), "DV")
    p.wait(SC, "DV", vst)
    vsr2 = p.ins(SC, lambda E: E.activation(stat_cols[:, 0:16], stat_cols[:, 0:16], AF.Sqrt, bias=epscol[:, 0:1]), "AC")
    p.wait(VE, "AC", vsr2)
    p.ins(VE, lambda E: E.reciprocal(stat_cols[:, 0:16], stat_cols[:, 0:16]))
    vms = p.ins(VE, lambda E: E.engine_nop(), "DV")
    p.wait(SC, "DV", vms)
    vsm2 = p.ins(SC, lambda E: E.activation(stat_cols[:, 16:32], stat_cols[:, 16:32], AF.Sqrt), "AC")
    p.wait(VE, "AC", vsm2)
    p.ins(VE, lambda E: E.tensor_mul(stat_cols[:, 16:32], stat_cols[:, 16:32], stat_cols[:, 0:16]))
    p.ins(VE, lambda E: E.tensor_scalar(stat_cols[:, 16:32], stat_cols[:, 16:32], 1e-5, None, A.max))
    p.ins(VE, lambda E: E.reciprocal(stat_cols[:, 32:48], stat_cols[:, 16:32]))
    vf2c = p.ins(VE, lambda E: E.tensor_scalar(
        mxcols[:, 0:16], stat_cols[:, 32:48], 127.0, None, A.mult), "DV")
    p.ins(VE, lambda E: E.tensor_mul(mxcols[:, 0:16], mxcols[:, 0:16], stat_cols[:, 0:16]))
    vf2c = p.ins(VE, lambda E: E.engine_nop(), "DV")
    v_row_f2 = row_put(vf2c, mxcols[:, 0:16], Q_F2)
    vp2c = p.ins(VE, lambda E: E.tensor_scalar(
        mxcols[:, 0:16], stat_cols[:, 16:32], swd[:, 1:2], 1.0 / 127.0, A.mult, A.mult), "DV")
    v_row_p2 = row_put(vp2c, mxcols[:, 0:16], Q_P2)

    p.sync_phase(sems=("SW", "SA", "DV", "AC", "PE"))
    a1.close()

    a2 = ExitStack()
    ab2 = lambda name, shape, dtyp: a2.enter_context(nc.sbuf_tensor(name, shape, dtyp))
    wd_st = [ab2(f"wd_st{i}", [128, 512], F32) for i in range(2)]
    wd_q = [ab2(f"wd_q{i}", [128, 512], BF16) for i in range(2)]
    i8b = [ab2(f"i8b{i}", [128, 512], F32) for i in range(2)]
    clipb = ab2("clipb", [128, 512], F32)
    h2f = [ab2(f"h2f{i}", [128, 1024], F32) for i in range(4)]
    h2qs = [ab2(f"h2qs{i}", [128, 1024], BF16) for i in range(4)]
    h2qr = [ab2(f"h2qr{i}", [128, 512], BF16) for i in range(4)]
    i32b = [ab2(f"i32b{i}", [128, 1024], F32) for i in range(4)]
    preb = [ab2(f"preb{i}", [128, 1024], F32) for i in range(4)]
    bc3g = ab2("bc3g", [128, 2048], F32)
    bc4g = ab2("bc4g", [128, 2048], F32)

    for t4 in range(4):
        bc_row(Q_F2, t4 * 512, 512, bc3g, t4 * 512, v_row_f2, 0)
        bc_row(Q_P2, t4 * 512, 512, bc4g, t4 * 512, v_row_p2, 1)
    v_a2bc = p.cur("DV")
    p.wait(TE, "DV", v_a2bc)

    wdst_free = [("AC", 0)] * 2
    wdq_free = [("PE", 0)] * 2
    i8b_free = [("DV", 0)] * 2
    h2f_free = [("DV", 0)] * 4
    h2qs_free = [(f"HQ{i}", 0) for i in range(4)]
    h2qr_free = [("PE", 0)] * 4
    i32b_free = [("DV", 0)] * 4
    preb_free = [("AC", 0)] * 4
    s_evac_done = 0
    for st_ in range(4):
        if st_ == 1:
            for eng in (GP, SC):
                for hq in range(4):
                    p.wait(eng, f"HQ{hq}", p.cur(f"HQ{hq}"))
        for th in range(2):
            vlast_mm = 0
            for kb in range(NDKB):
                wslot = kb % 2
                s, v = wdst_free[wslot]
                p.wait(SY, s, v)
                vld = p.dma(SY, wd_st[wslot][:, :], wdt[kb * 4 + st_, :, :], f"WD{wslot}")
                s, v = i8b_free[wslot]
                p.wait(SC, s, v)
                p.wait(SC, f"WD{wslot}", vld)
                vq = p.ins(SC, lambda E, w=wslot: E.activation(
                    i8b[w][:, :], wd_st[w][:, :], AF.Identity, scale=swd[:, 0:1], bias=mqcol[:, 0:1]), "AC")
                wdst_free[wslot] = ("AC", vq)
                s, v = wdq_free[wslot]
                p.wait(VE, s, v)
                p.wait(VE, "AC", vq)
                p.ins(VE, lambda E, w=wslot: E.tensor_scalar(clipb[:, :], i8b[w][:, :], mqcol[:, 0:1], None, A.subtract))
                vqd = p.ins(VE, lambda E, w=wslot: E.tensor_scalar(wd_q[w][:, :], clipb[:, :], -1.0, 1.0, A.max, A.min), "DV")
                i8b_free[wslot] = ("DV", vqd)
                p.wait(TE, "DV", vqd)
                if st_ == 0:
                    hslot = kb % 4
                    heng = GP
                    s, v = h2f_free[hslot]
                    p.wait(heng, s, v)
                    for j in range(2):
                        vh = p.dma(heng, h2f[hslot][:, j * 512:(j + 1) * 512],
                                   d_h2[kb * 4 + th * 2 + j, :, :], f"GH2{hslot}")
                    s, v = preb_free[hslot]
                    p.wait(VE, s, v)
                    p.wait(VE, f"GH2{hslot}", vh)
                    vpre = p.ins(VE, lambda E, h=hslot, k=kb, t2=th: E.scalar_tensor_tensor(
                        preb[h][:, :], h2f[h][:, :], gdcol[:, k:k + 1], bc3g[:, t2 * 1024:(t2 + 1) * 1024],
                        A.mult, A.mult), "DV")
                    h2f_free[hslot] = ("DV", vpre)
                    s, v = i32b_free[hslot]
                    p.wait(SC, s, v)
                    p.wait(SC, "DV", vpre)
                    vcv = p.ins(SC, lambda E, h=hslot: E.activation(
                        i32b[h][:, :], preb[h][:, :], AF.Identity, bias=mqcol[:, 0:1]), "AC")
                    preb_free[hslot] = ("AC", vcv)
                    s, v = h2qs_free[hslot]
                    p.wait(VE, s, v)
                    p.wait(VE, "AC", vcv)
                    vqq = p.ins(VE, lambda E, h=hslot: E.tensor_scalar(
                        h2qs[h][:, :], i32b[h][:, :], mqcol[:, 0:1], None, A.subtract), "DV")
                    i32b_free[hslot] = ("DV", vqq)
                    p.wait(SC, "DV", vqq)
                    for j in range(2):
                        vsp = p.dma(SC, d_h2q[kb * 4 + th * 2 + j, :, :],
                                    h2qs[hslot][:, j * 512:(j + 1) * 512], f"HQ{hslot}")
                    h2qs_free[hslot] = (f"HQ{hslot}", vsp)
                    p.wait(TE, "DV", vqq)
                    for w2 in range(2):
                        bank = (kb % 1) * 0
                        for hb in range(4):
                            b = hb * 2 + w2
                            vlast_mm = p.ins(TE, lambda E, b2=b, w=wslot, h=hslot, h3=hb, w3=w2, k=kb: E.matmul(
                                ps[b2][:, :], wd_q[w][:, h3 * 128:(h3 + 1) * 128],
                                h2qs[h][:, w3 * 512:(w3 + 1) * 512],
                                start=(k == 0), stop=(k == NDKB - 1)), "PE" if hb == 3 else None)
                else:
                    for w2 in range(2):
                        rslot = (kb * 2 + w2) % 4
                        reng = (GP, GP, SC, GP)[rslot]
                        s, v = h2qr_free[rslot]
                        p.wait(reng, s, v)
                        vh = p.dma(reng, h2qr[rslot][:, :],
                                   d_h2q[kb * 4 + th * 2 + w2, :, :],
                                   f"GHQ{rslot}")
                        p.wait(TE, f"GHQ{rslot}", vh)
                        for hb in range(4):
                            b = hb * 2 + w2
                            vlast_mm = p.ins(TE, lambda E, b2=b, w=wslot, r=rslot, h3=hb, k=kb: E.matmul(
                                ps[b2][:, :], wd_q[w][:, h3 * 128:(h3 + 1) * 128], h2qr[r][:, :],
                                start=(k == 0), stop=(k == NDKB - 1)), "PE" if hb == 3 else None)
                        h2qr_free[rslot] = ("PE", vlast_mm)
                wdq_free[wslot] = ("PE", vlast_mm)
            p.wait(VE, "PE", vlast_mm)
            for w2 in range(2):
                for hb in range(4):
                    b = hb * 2 + w2
                    ho = st_ * 4 + hb
                    s_evac_done = p.ins(VE, lambda E, b2=b, h3=ho, t2=th, w3=w2: E.tensor_mul(
                        big1[:, h3 * OWN + t2 * 1024 + w3 * 512:h3 * OWN + t2 * 1024 + w3 * 512 + 512],
                        ps[b2][:, :], bc4g[:, t2 * 1024 + w3 * 512:t2 * 1024 + w3 * 512 + 512]), "DV")
            p.wait(TE, "DV", s_evac_done)
    v_a2 = p.cur("DV")

    p.sync_phase(sems=("SW", "SG", "SA", "DV", "AC", "PE"))
    a2.close()

    p.wait(SY, "SG", v_th)
    p.wait(SY, "PL", v_bis)
    p.ins(VE, lambda E: E.memset(th_row[:1, :], 0.0))
    p.wait(SY, "DV", p.cur("DV"))
    vthl = p.dma(SY, th_row[:1, 0:8], d_small[3:4, 0:8], "SW")
    p.wait(TE, "SW", vthl)
    p.use_bank(0)
    vthb = p.ins(TE, lambda E: E.matmul(ps[0][:, 0:16], ones_row[:1, :], th_row[:1, 0:16], start=True, stop=True), "PE")
    p.wait(VE, "PE", vthb)
    vthe = p.ins(VE, lambda E: E.tensor_copy(out=th_cols[:, :], in_=ps[0][:, 0:8]), "DV")
    p.free_bank(0, "DV", vthe)
    for e in range(NE):
        p.ins(VE, lambda E, e2=e: E.scalar_tensor_tensor(
            tmpA[:, 0:16], sm8[:, e2 * 128:e2 * 128 + 16], th_cols[:, e2:e2 + 1],
            sm8[:, e2 * 128:e2 * 128 + 16], A.is_ge, A.mult))
        if e == 0:
            p.ins(VE, lambda E: E.tensor_copy(out=mxcols[:, 0:16], in_=tmpA[:, 0:16]))
        else:
            p.ins(VE, lambda E: E.tensor_add(mxcols[:, 0:16], mxcols[:, 0:16], tmpA[:, 0:16]))
    vco = p.ins(VE, lambda E: E.engine_nop(), "DV")
    v_row_co = row_put(vco, mxcols[:, 0:16], Q_CO)

    p.sync_phase()
    es2.close()

    f1 = ExitStack()
    fb_ = lambda name, shape, dtyp: f1.enter_context(nc.sbuf_tensor(name, shape, dtyp))
    xnt_res = fb_("xnt_res", [128, NKB * 2048], BF16)
    fw_st = [fb_(f"fw_st{i}", [128, 128], F32) for i in range(4)]
    fq_g = [fb_(f"fq_g{i}", [128, NKB * 128], BF16) for i in range(2)]
    fq_u = [fb_(f"fq_u{i}", [128, NKB * 128], BF16) for i in range(2)]
    silf = [fb_(f"silf{i}", [128, 512], F32) for i in range(2)]
    ubuf = [fb_(f"ubuf{i}", [128, 512], F32) for i in range(2)]
    midb = [fb_(f"midb{i}", [128, 512], BF16) for i in range(4)]

    p.wait(GP, "XB0", p.cur("XB0"))
    p.wait(GP, "XB1", p.cur("XB1"))
    vxa_kb = []
    for kb in range(NKB):
        for sl in range(4):
            vxa = p.dma(GP, xnt_res[:, kb * 2048 + sl * 512:kb * 2048 + (sl + 1) * 512],
                        d_xnt[kb * 4 + sl, :, :], "GXA")
        vxa_kb.append(vxa)

    fwst_free = [("DV", 0)] * 4
    fbank_free = [("DV", 0)] * 8
    silf_free = [("DV", 0)] * 2
    ubuf_free = [("DV", 0)] * 2
    midb_free = [(f"MB{i}", 0) for i in range(4)]
    f1_spill = {}
    fcvt_done = {}
    fpanel_pe = {}

    def emit_fcvt_unit(pc2, u2):
        buf2 = pc2 % 2
        if u2 == 0:
            p.wait(VE, "PE", fpanel_pe.get(pc2 - 2, 0))
        gu = u2 // NKB
        kb2 = u2 % NKB
        src = fwg if gu == 0 else fwu
        dstp = fq_g[buf2] if gu == 0 else fq_u[buf2]
        slot = (pc2 * 32 + u2) % 4
        s, v = fwst_free[slot]
        p.wait(SY, s, v)
        vld = p.dma(SY, fw_st[slot][:, :], src[kb2 * NFKB + pc2, :, :], f"FW{slot}")
        p.wait(VE, f"FW{slot}", vld)
        vcvt = p.ins(VE, lambda E, sl2=slot, d=dstp, k=kb2: E.tensor_copy(
            out=d[:, k * 128:(k + 1) * 128], in_=fw_st[sl2][:, :]), "DV")
        fwst_free[slot] = ("DV", vcvt)
        fcvt_done[pc2] = vcvt

    for u in range(32):
        emit_fcvt_unit(0, u)
    for pc in range(NFKB):
        buf = pc % 2
        p.wait(TE, "DV", fcvt_done[pc])
        if pc + 1 < NFKB:
            for q in range(16):
                emit_fcvt_unit(pc + 1, q)
        for ts in range(4):
            vlast_mm = 0
            for gu in range(2):
                bank = ts * 2 + gu
                s, v = fbank_free[bank]
                p.wait(TE, s, v)
                wqp = fq_g[buf] if gu == 0 else fq_u[buf]
                for kb in range(NKB):
                    if pc == 0 and gu == 0:
                        p.wait(TE, "GXA", vxa_kb[kb])
                    vlast_mm = p.ins(TE, lambda E, b=bank, w=wqp, k=kb, t2=ts: E.matmul(
                        ps[b][:, :], w[:, k * 128:(k + 1) * 128],
                        xnt_res[:, k * 2048 + t2 * 512:k * 2048 + t2 * 512 + 512],
                        start=(k == 0), stop=(k == NKB - 1)),
                        "PE" if kb == NKB - 1 else None)
            jp = ts % 2
            p.wait(SC, "PE", vlast_mm)
            s, v = silf_free[jp]
            p.wait(SC, s, v)
            vslf = p.ins(SC, lambda E, j2=jp, b=ts * 2, f2=pc: E.activation(
                silf[j2][:, :], ps[b][:, :], AF.Silu, bias=bgcol[:, f2:f2 + 1]), "AC")
            s, v = ubuf_free[jp]
            p.wait(SC, s, v)
            vub = p.ins(SC, lambda E, j2=jp, b=ts * 2 + 1, f2=pc: E.activation(
                ubuf[j2][:, :], ps[b][:, :], AF.Identity, bias=bucol[:, f2:f2 + 1]), "AC")
            mslot = ts
            s, v = midb_free[mslot]
            p.wait(VE, s, v)
            p.wait(VE, "AC", vub)
            vmd = p.ins(VE, lambda E, j2=jp, m=mslot: E.tensor_mul(
                midb[m][:, :], silf[j2][:, :], ubuf[j2][:, :]), "DV")
            silf_free[jp] = ("DV", vmd)
            ubuf_free[jp] = ("DV", vmd)
            fbank_free[ts * 2] = ("DV", vmd)
            fbank_free[ts * 2 + 1] = ("DV", vmd)
            p.wait(SC, "DV", vmd)
            vmsp = p.dma(SC, d_mid[pc * 4 + ts, :, :], midb[mslot][:, :], f"MB{mslot}")
            midb_free[mslot] = (f"MB{mslot}", vmsp)
            f1_spill[(pc, ts)] = vmsp
            if pc + 1 < NFKB and ts < 2:
                for q in range(8):
                    emit_fcvt_unit(pc + 1, 16 + ts * 8 + q)
        fpanel_pe[pc] = vlast_mm

    p.sync_phase(sems=("SW", "SG", "SA", "DV", "AC", "PE"))
    f1.close()

    f2 = ExitStack()
    fb2_ = lambda name, shape, dtyp: f2.enter_context(nc.sbuf_tensor(name, shape, dtyp))
    mid_res = fb2_("mid_res", [128, NFKB * 1024], BF16)
    fd_st = [fb2_(f"fd_st{i}", [128, 512], F32) for i in range(2)]
    fd_q = [fb2_(f"fd_q{i}", [128, 512], BF16) for i in range(2)]
    xres = [fb2_(f"xres{i}", [128, 512], F32) for i in range(2)]
    outt = [fb2_(f"outt{i}", [128, 512], F32) for i in range(2)]
    gbuf = [fb2_(f"gbuf{i}", [128, 512], F32) for i in range(2)]
    cobc = fb2_("cobc", [128, 2048], F32)

    for t4 in range(4):
        bc_row(Q_CO, t4 * 512, 512, cobc, t4 * 512, v_row_co, 0)
    v_f2bc = p.cur("DV")
    p.wait(TE, "DV", v_f2bc)

    fdst_free = [("DV", 0)] * 2
    fdq_free = [("PE", 0)] * 2
    xres_free = [("DV", 0)] * 2
    outt_free = [("OUT", 0)] * 2
    gbuf_free = [("DV", 0)] * 2
    bank_ac = {}
    for th in range(2):
        if th == 1:
            p.wait(SC, "PE", p.cur("PE"))
        vml_kb = []
        for kb in range(NFKB):
            for j in range(2):
                p.wait(SC, f"MB{th * 2 + j}", f1_spill[(kb, th * 2 + j)])
                vml = p.dma(SC, mid_res[:, kb * 1024 + j * 512:kb * 1024 + (j + 1) * 512],
                            d_mid[kb * 4 + th * 2 + j, :, :], "GMI")
            vml_kb.append(vml)
        fd_cvt = {}

        def emit_fd_unit(st2, kb2):
            key = (st2, kb2)
            if key in fd_cvt:
                return fd_cvt[key]
            wslot = kb2 % 2
            weng = SY if wslot == 0 else GP
            s, v = fdst_free[wslot]
            p.wait(weng, s, v)
            vld = p.dma(weng, fd_st[wslot][:, :], fwd[kb2 * 4 + st2, :, :], f"FD{wslot}")
            s, v = fdq_free[wslot]
            p.wait(VE, s, v)
            p.wait(VE, f"FD{wslot}", vld)
            vcv = p.ins(VE, lambda E, w=wslot: E.tensor_copy(out=fd_q[w][:, :], in_=fd_st[w][:, :]), "DV")
            fdst_free[wslot] = ("DV", vcv)
            fd_cvt[key] = vcv
            return vcv

        for st_ in range(4):
            vlast_mm = 0
            for kb in range(NFKB):
                wslot = kb % 2
                vcv = emit_fd_unit(st_, kb)
                p.wait(TE, "DV", vcv)
                if st_ == 0:
                    p.wait(TE, "GMI", vml_kb[kb])
                for w2 in range(2):
                    for hb in range(4):
                        b = hb * 2 + w2
                        if kb == 0:
                            p.wait(TE, "AC", bank_ac.get(b, 0))
                        vlast_mm = p.ins(TE, lambda E, b2=b, w=wslot, h3=hb, k=kb, w3=w2: E.matmul(
                            ps[b2][:, :], fd_q[w][:, h3 * 128:(h3 + 1) * 128],
                            mid_res[:, k * 1024 + w3 * 512:k * 1024 + w3 * 512 + 512],
                            start=(k == 0), stop=(k == NFKB - 1)), "PE" if (w2 == 1 and hb == 3) else None)
                fdq_free[wslot] = ("PE", vlast_mm)
            if st_ < 3:
                emit_fd_unit(st_ + 1, 0)
                emit_fd_unit(st_ + 1, 1)
            p.wait(SC, "PE", vlast_mm)
            for w2 in range(2):
                for hb in range(4):
                    b = hb * 2 + w2
                    ho = st_ * 4 + hb
                    toff = th * 1024 + w2 * 512
                    jg = (w2 * 4 + hb) % 2
                    xslot = jg
                    s, v = xres_free[xslot]
                    p.wait(GP, s, v)
                    vxl = p.dma(GP, xres[xslot][:, :], xt[ho * 32 + th * 2 + w2, :, :], f"XE{xslot}")
                    s, v = gbuf_free[jg]
                    p.wait(SC, s, v)
                    vg1 = p.ins(SC, lambda E, j2=jg, b2=b, h3=ho: E.activation(
                        gbuf[j2][:, :], ps[b2][:, :], AF.Identity, bias=bdcol[:, h3:h3 + 1]), "AC")
                    bank_ac[b] = vg1
                    p.wait(VE, "AC", vg1)
                    vg2 = p.ins(VE, lambda E, j2=jg, t2=toff: E.tensor_mul(
                        gbuf[j2][:, :], gbuf[j2][:, :], cobc[:, t2:t2 + 512]), "DV")
                    p.wait(SC, "DV", vg2)
                    vg3 = p.ins(SC, lambda E, j2=jg: E.activation(gbuf[j2][:, :], gbuf[j2][:, :], AF.Silu), "AC")
                    oslot = jg
                    s, v = outt_free[oslot]
                    p.wait(VE, s, v)
                    p.wait(VE, "AC", vg3)
                    p.ins(VE, lambda E, j2=jg, o=oslot, h3=ho, t2=toff: E.tensor_mul(
                        outt[o][:, :], gbuf[j2][:, :], big1[:, h3 * OWN + t2:h3 * OWN + t2 + 512]))
                    p.wait(VE, f"XE{xslot}", vxl)
                    vo = p.ins(VE, lambda E, o=oslot, x=xslot: E.tensor_add(
                        outt[o][:, :], outt[o][:, :], xres[x][:, :]), "DV")
                    xres_free[xslot] = ("DV", vo)
                    gbuf_free[jg] = ("DV", vo)
                    p.wait(SC, "DV", vo)
                    vout = p.dma(SC, yt[ho * 128:(ho + 1) * 128, toff:toff + 512], outt[oslot][:, :], f"OT{oslot}")
                    outt_free[oslot] = (f"OT{oslot}", vout)

    p.wait(SY, "OT0", p.cur("OT0"))
    p.wait(SY, "OT1", p.cur("OT1"))
    f2.close()

    semstack = ExitStack()
    for name in sorted(set(list(Plan.SEMS) + list(p.cnt.keys()))):
        p.sem[name] = semstack.enter_context(nc.semaphore(name))
    with nc.Block() as block:
        @block.sync
        def _(E):
            for f in p.ops["sync"]:
                f(E)

        @block.vector
        def _(E):
            for f in p.ops["vector"]:
                f(E)

        @block.scalar
        def _(E):
            for f in p.ops["scalar"]:
                f(E)

        @block.tensor
        def _(E):
            for f in p.ops["tensor"]:
                f(E)

        @block.gpsimd
        def _(E):
            for f in p.ops["gpsimd"]:
                f(E)

    semstack.close()
    pse.close()
    es.close()
    return nc


def kernel(L0_attn_x, L0_attn_w_gate, L0_attn_g_gate, L0_attn_w_down, L0_attn_g_down,
           L0_ffn_router_w, L0_ffn_router_b, L0_ffn_w_gate, L0_ffn_b_gate,
           L0_ffn_w_up, L0_ffn_b_up, L0_ffn_w_down, L0_ffn_b_down):
    f32 = np.float32

    def tiles(a, pr, pc_):
        nr = a.shape[0] // 128
        ncol = a.shape[1] // pc_
        return np.ascontiguousarray(
            a.reshape(nr, 128, ncol, pc_).transpose(0, 2, 1, 3).reshape(nr * ncol, 128, pc_))

    x = np.ascontiguousarray(np.asarray(L0_attn_x, f32).reshape(NT, H))
    xT = np.ascontiguousarray(x.T)
    wgT = np.asarray(L0_attn_w_gate, f32).T
    wdT = np.asarray(L0_attn_w_down, f32).T
    wgt_t = tiles(np.ascontiguousarray(wgT), 128, 512)
    wdt_t = tiles(np.ascontiguousarray(wdT), 128, 512)
    fwg = np.zeros((H, DFFP), f32)
    fwg[:, :DFF] = np.asarray(L0_ffn_w_gate, f32)
    fwu = np.zeros((H, DFFP), f32)
    fwu[:, :DFF] = np.asarray(L0_ffn_w_up, f32)
    fwd = np.zeros((DFFP, H), f32)
    fwd[:DFF, :] = np.asarray(L0_ffn_w_down, f32)
    fwg_t = tiles(fwg, 128, 128)
    fwu_t = tiles(fwu, 128, 128)
    fwd_t = tiles(fwd, 128, 512)
    bgp = np.zeros((DFFP,), f32)
    bgp[:DFF] = np.asarray(L0_ffn_b_gate, f32)
    bup = np.zeros((DFFP,), f32)
    bup[:DFF] = np.asarray(L0_ffn_b_up, f32)

    ggate2 = np.ascontiguousarray(np.asarray(L0_attn_g_gate, f32).reshape(NKB, 128).T)
    gdown2 = np.ascontiguousarray(np.asarray(L0_attn_g_down, f32).reshape(NDKB, 128).T)
    bg2 = np.ascontiguousarray(bgp.reshape(NFKB, 128).T)
    bu2 = np.ascontiguousarray(bup.reshape(NFKB, 128).T)
    bd2 = np.ascontiguousarray(np.asarray(L0_ffn_b_down, f32).reshape(NKB, 128).T)
    rb = np.asarray(L0_ffn_router_b, f32).reshape(1, NE)
    rw = np.asarray(L0_ffn_router_w, f32)
    rwa1 = np.zeros((128, NKB * 16), f32)
    rwa2 = np.zeros((128, NKB * 16), f32)
    for kb in range(NKB):
        rwa1[:, kb * 16:kb * 16 + 8] = rw[kb * 128:(kb + 1) * 128, :]
        rwa1[:, kb * 16 + 8] = 1.0
        rwa2[:, kb * 16 + 9] = 1.0
    ident = np.eye(128, dtype=f32)

    gh, gl = _gumbel_tables()

    nc = build_program_synced()
    in_maps = []
    for c in range(NCORE):
        sh = c * OWN
        xTc = np.concatenate([xT[:, sh:], xT[:, :sh]], axis=1)
        xtc_t = tiles(np.ascontiguousarray(xTc), 128, 512)
        ghc = np.concatenate([gh[sh:], gh[:sh]], axis=0)
        glc = np.concatenate([gl[sh:], gl[:sh]], axis=0)
        gmhc = np.ascontiguousarray(ghc.reshape(128, 128, NE).transpose(2, 1, 0))
        gmlc = np.ascontiguousarray(glc.reshape(128, 128, NE).transpose(2, 1, 0))
        in_maps.append({
            "xt": xtc_t, "gmh": gmhc, "gml": gmlc, "wgt": wgt_t, "wdt": wdt_t,
            "fwg": fwg_t, "fwu": fwu_t, "fwd": fwd_t, "ggate2": ggate2, "gdown2": gdown2,
            "rwa1": rwa1, "rwa2": rwa2, "bg2": bg2, "bu2": bu2, "bd2": bd2,
            "rbrow": rb, "identd": ident,
        })
    res = run_bass_kernel_spmd(nc, in_maps, core_ids=list(range(NCORE)))
    out = np.empty((NT, H), f32)
    for c in range(NCORE):
        out[c * OWN:(c + 1) * OWN, :] = res.results[c]["yt"].T
    return out.reshape(4, 4096, 2048)


def build_program_synced():
    rec = {}
    build_program(rec=rec)
    dep = compute_edges(rec)
    return build_program(dep=dep)


if __name__ == "__main__":
    prog = build_program_synced()
    n = sum(1 for _ in prog.all_instructions())
    print("built ok, instructions:", n)
